# version 21 (final)
# baseline (speedup 1.0000x reference)
import math
import os
from contextlib import ExitStack
import numpy as np

os.environ.setdefault("GAUGE_NTFF_IGNORE_DMA", "1")
import concourse.bass as bass
import concourse.mybir as mybir
import bass_rust
from concourse.bass_utils import run_bass_kernel_spmd

F32 = mybir.dt.float32
I32 = mybir.dt.int32
U32 = mybir.dt.uint32
U8 = mybir.dt.uint8
AF = mybir.ActivationFunctionType
ALU = mybir.AluOpType
AX = mybir.AxisListType

N_CORES = 8
B, S, D = 8, 2048, 1024
TOK = S
NT = TOK // 128


def dview(t, offset, pairs):
    v = t.copy() if isinstance(t, bass_rust.AP) else t.ap().copy()
    v.offset = offset
    v.ap = bass_rust.VecI64Pair([list(p) for p in pairs])
    return v


class Prog:

    ENGINES = ("sync", "tensor", "vector", "scalar", "gpsimd")

    def __init__(self):
        self.nc = bass.Bass("TRN2", target_bir_lowering=False)
        self.steps = []
        self._uid = 0
        self.pf_counts = {}
        self.pf_units = {}
        self.pf_pending = None
        self._prefix = None
        self._stack = None

    _tag = ""

    def settag(self, t):
        self._tag = t

    def scope_open(self, prefix):
        self._prefix = prefix
        self._tag = ""
        self._stacks = [ExitStack()]

    def scope_close(self):
        while self._stacks:
            self._stacks.pop().close()
        self._prefix = None
        self._stacks = None

    def sub_open(self, tag):
        self._sub = tag
        self._stacks.append(ExitStack())

    def sub_close(self):
        self._stacks.pop().close()
        self._sub = None

    _sub = None
    _stacks = None

    def _nm(self, name, kind):
        nm = name or self.uid(kind)
        if self._sub:
            nm = self._sub + nm
        return (self._prefix + nm) if self._prefix else nm

    def uid(self, p):
        self._uid += 1
        return f"{p}{self._uid}"

    PREFETCH = os.environ.get("KPREFETCH", "all")

    def _pf(self, site):
        return self.PREFETCH == "all" or site in self.PREFETCH.split(",")

    def mark(self):
        return sum(s[1] for s in self.steps)

    def dma(self, out, in_, engine="sync", allow_slow=False, after=None, site=None, pf=None):
        if (after is not None or pf is not None) and site is not None and not self._pf(site):
            after = None
            pf = None
        tag = (self._prefix or "top") + self._tag
        if pf is not None:
            key = (self._prefix, site, pf)
            self.pf_units[key] = 16
            n = self.pf_counts.get(key, 0) + 1
            self.pf_counts[key] = n
            self.steps.append((engine, 0, lambda e: e.dma_start(out=out, in_=in_), "dma", tag, after, ("inc", key, 16)))
            return
        if allow_slow:
            self.steps.append((engine, 16, lambda e: e.dma_start(out=out, in_=in_, allow_slow_non_contiguous=True), "dma", tag, after, None))
        else:
            self.steps.append((engine, 16, lambda e: e.dma_start(out=out, in_=in_), "dma", tag, after, None))

    pf_units = {}

    def pfuse(self, site, pf, unit=16):
        if not self._pf(site):
            return
        key = (self._prefix, site, pf)
        self.pf_pending = ("wait", key, unit * self.pf_counts.get(key, 0))

    pf_counts = None
    pf_pending = None

    def op(self, engine, fn, weight=1, kind=None, after=None, site=None, pf=None):
        if (after is not None or pf is not None) and site is not None and not self._pf(site):
            after = None
            pf = None
        if pf is not None:
            key = (self._prefix, site, pf)
            self.pf_units[key] = 1
            n = self.pf_counts.get(key, 0) + 1
            self.pf_counts[key] = n
            self.steps.append((engine, 0, fn, "cmp", (self._prefix or "top") + self._tag, after, ("inc", key, 1)))
            return
        extra = self.pf_pending
        self.pf_pending = None
        self.steps.append((engine, weight, fn, kind or ("dma" if weight >= 16 else "cmp"), (self._prefix or "top") + self._tag, after, extra))

    def sb(self, shape, dtype=F32, name=None):
        nm = self._nm(name, "sb")
        if self._stacks:
            return self._stacks[-1].enter_context(self.nc.sbuf_tensor(nm, shape, dtype))
        return self.nc.alloc_sbuf_tensor(nm, shape, dtype)

    def ps(self, shape, dtype=F32, name=None):
        nm = self._nm(name, "ps")
        if self._stacks:
            return self._stacks[-1].enter_context(self.nc.psum_tensor(nm, shape, dtype))
        return self.nc.alloc_psum_tensor(nm, shape, dtype)

    def din(self, name, shape, dtype=F32):
        return self.nc.dram_tensor(name, shape, dtype, kind="ExternalInput")

    def dout(self, name, shape, dtype=F32):
        return self.nc.dram_tensor(name, shape, dtype, kind="ExternalOutput")

    def dtmp(self, shape, dtype=F32, name=None):
        return self.nc.dram_tensor(self._nm(name, "dt"), shape, dtype, kind="Internal")

    EPOCH = 30000

    def finalize(self):
        nc = self.nc
        acc = 0
        epochs = []
        bases = [0]
        skips = []
        pf_sems = {}
        for key in self.pf_counts:
            pf_sems[key] = nc.alloc_semaphore(f"pf{len(pf_sems)}")
        prev_eng, prev_kind = None, None
        for (eng, w, fn, kind, tag, after, extra) in self.steps:
            ep = acc // self.EPOCH
            while len(bases) <= ep:
                bases.append(acc)
            skips.append(False)
            epochs.append((ep, acc, w))
            acc += w
            prev_eng, prev_kind = eng, kind
        n_ep = len(bases)
        sems = [nc.alloc_semaphore(f"oseq{i}") for i in range(n_ep)]
        ep_tot = [0] * n_ep
        for (ep, cum, w) in epochs:
            ep_tot[ep] += w
        seqs = {e: [] for e in self.ENGINES}
        for i, (eng, w, fn, kind, tag, after, extra) in enumerate(self.steps):
            ep, cum, _ = epochs[i]
            wait_cum = cum if after is None else min(after, cum)
            wep = 0
            for k in range(len(bases) - 1, -1, -1):
                if wait_cum >= bases[k] and (k == 0 or bases[k] < wait_cum or wait_cum == 0):
                    pass
            wep = 0
            for k in range(len(bases)):
                if bases[k] <= wait_cum:
                    wep = k
            seqs[eng].append((ep, wep, wait_cum - bases[wep], w, fn, skips[i], tag, extra))
        with nc.Block() as blk:
            for eng in self.ENGINES:
                items = seqs[eng]

                def make(items, eng=eng):
                    def body(e):
                        last_ep = 0
                        for (ep, wep, wait, w, fn, skip, tag, extra) in items:
                            with nc.named_scope(tag):
                                if ep > last_ep:
                                    e.wait_ge(sems[ep - 1], ep_tot[ep - 1])
                                    last_ep = ep
                                if wait > 0 and not skip:
                                    e.wait_ge(sems[wep], wait)
                                if extra is not None and extra[0] == "wait" and extra[2] > 0:
                                    e.wait_ge(pf_sems[extra[1]], extra[2])
                                inst = fn(e)
                                if extra is not None and extra[0] == "inc":
                                    inst.then_inc(pf_sems[extra[1]], extra[2])
                                elif w > 0:
                                    inst.then_inc(sems[ep], w)
                        if eng == "sync":
                            for i2 in range(n_ep):
                                e.wait_ge(sems[i2], ep_tot[i2])
                            for key, s2 in pf_sems.items():
                                e.wait_ge(s2, self.pf_units.get(key, 16) * self.pf_counts[key])
                    return body

                getattr(blk, eng)(make(items))
        return nc



def emit_layernorm(P, x_ap, out_ap, sq, stats, eps=1e-5, D_=1024, gw=None, gb=None):
    sq = sq[:, 0:D_]
    s1 = stats[:, 0:1]
    s2 = stats[:, 1:2]
    mean = stats[:, 2:3]
    var = stats[:, 3:4]
    rstd = stats[:, 4:5]
    nb = stats[:, 5:6]
    P.op("vector", lambda e: e.tensor_reduce(out=s1, in_=x_ap, axis=AX.X, op=ALU.add))
    P.op("scalar", lambda e: e.activation(out=sq, in_=x_ap, func=AF.Square, accum_out=s2))
    P.op("vector", lambda e: e.tensor_scalar(out=mean, in0=s1, scalar1=1.0 / D_, scalar2=None, op0=ALU.mult))
    P.op("vector", lambda e: e.tensor_tensor(out=var, in0=mean, in1=mean, op=ALU.mult))
    P.op("vector", lambda e: e.scalar_tensor_tensor(out=var, in0=s2, scalar=1.0 / D_, in1=var, op0=ALU.mult, op1=ALU.subtract))
    P.op("vector", lambda e: e.tensor_scalar(out=var, in0=var, scalar1=float(eps), scalar2=None, op0=ALU.add))
    P.op("scalar", lambda e: e.activation(out=rstd, in_=var, func=AF.Sqrt))
    P.op("vector", lambda e: e.reciprocal(out=rstd, in_=rstd))
    P.op("vector", lambda e: e.tensor_tensor(out=nb, in0=mean, in1=rstd, op=ALU.mult))
    P.op("vector", lambda e: e.tensor_scalar(out=nb, in0=nb, scalar1=-1.0, scalar2=None, op0=ALU.mult))
    P.op("vector", lambda e: e.tensor_scalar(out=out_ap, in0=x_ap, scalar1=rstd, scalar2=nb, op0=ALU.mult, op1=ALU.add))


def emit_rmsnorm(P, x_ap, out_ap, sq, stats, eps, D_):
    sq = sq[:, 0:D_]
    s2 = stats[:, 1:2]
    var = stats[:, 3:4]
    rstd = stats[:, 4:5]
    P.op("scalar", lambda e: e.activation(out=sq, in_=x_ap, func=AF.Square, accum_out=s2))
    P.op("vector", lambda e: e.tensor_scalar(out=var, in0=s2, scalar1=1.0 / D_, scalar2=float(eps), op0=ALU.mult, op1=ALU.add))
    P.op("scalar", lambda e: e.activation(out=rstd, in_=var, func=AF.Sqrt))
    P.op("vector", lambda e: e.reciprocal(out=rstd, in_=rstd))
    P.op("vector", lambda e: e.tensor_scalar(out=out_ap, in0=x_ap, scalar1=rstd, scalar2=None, op0=ALU.mult))


def emit_transpose(P, ident, src_ap, pst, dst_ap):
    P.op("tensor", lambda e: e.transpose(pst[:, 0:128], src_ap, ident[:, :]))
    P.op("vector", lambda e: e.tensor_copy(dst_ap, pst[:, 0:128]))


def emit_mm_group(P, psum_ap, lhsT_aps, rhs_aps, start=True):
    n = len(lhsT_aps)

    def fn(e):
        inst = None
        for k in range(n):
            inst = e.matmul(psum_ap, lhsT_aps[k], rhs_aps[k],
                            start=(start and k == 0), stop=(k == n - 1))
        return inst
    P.op("tensor", fn)



def emit_l3ffn(P, IN):
    nc = P.nc
    x = IN["x"]
    wgT = IN["wgT"]
    wuT = IN["wuT"]
    wdT = IN["wdT"]
    ident = IN["ident"]
    xo = IN["xo"]

    HID = 3072
    NH_ = HID // 128
    TC = 512
    NTC = TOK // TC
    KD = D // 128

    xsb = P.sb([128, 4, D], name="xsb")
    idsb = P.sb([128, 128], name="idsb")
    xn = P.sb([128, 4, D], name="xn")
    xnT = P.sb([128, KD, TC], name="xnT")
    sq = P.sb([128, D], name="sq")
    stats = P.sb([128, 16], name="stats")
    wgi2 = [P.sb([128, NH_, 128], I32, name="wgi0"), P.sb([128, NH_, 128], I32, name="wgi1")]
    wgf = P.sb([128, KD, 128], name="wgf")
    wuf = P.sb([128, KD, 128], name="wuf")
    wdf = P.sb([128, NH_, 128], name="wdf")
    hsb = P.sb([128, NH_, TC], name="hsb")
    hs = P.sb([128, TC], name="hs")
    dsb = P.sb([128, TC], name="dsb")
    xout = P.sb([128, 4, D], name="xout")
    psG = P.ps([128, 512], name="psG")
    psU = P.ps([128, 512], name="psU")
    psD = P.ps([128, 512], name="psD")
    psT = P.ps([128, 128], name="psT")


    P.dma(idsb[:, :], ident)

    DQ = 1.0 / 127.0
    xr = x.rearrange("(tc n p) d -> tc p n d", p=128, n=4)
    for tc in range(NTC):
        P.dma(xsb[:, :, :], xr[tc])
        for t in range(4):
            emit_layernorm(P, xsb[:, t, :], xn[:, t, :], sq, stats)
            for c in range(KD):
                emit_transpose(P, idsb, xn[:, t, 0 + 128 * c:128 * (c + 1)], psT,
                               xnT[:, c, 128 * t:128 * (t + 1)])
        _m0 = P.mark(); mkg = [_m0, _m0]
        for oc in range(NH_):
            P.dma(wgi2[0][:, 0:KD, :], wgT.ap()[:, 128 * oc:128 * (oc + 1)].rearrange("(c p) m -> p c m", p=128), after=mkg[0], site="l3f", pf=0)
            P.pfuse("l3f", 0)
            P.op("vector", lambda e: e.tensor_scalar(out=wgf[:, :, :], in0=wgi2[0][:, 0:KD, :], scalar1=DQ, scalar2=None, op0=ALU.mult))
            mkg[0] = P.mark()
            P.dma(wgi2[1][:, 0:KD, :], wuT.ap()[:, 128 * oc:128 * (oc + 1)].rearrange("(c p) m -> p c m", p=128), after=mkg[1], site="l3f", pf=1)
            emit_mm_group(P, psG[:, :], [wgf[:, k, :] for k in range(KD)], [xnT[:, k, :] for k in range(KD)])
            P.pfuse("l3f", 1)
            P.op("vector", lambda e: e.tensor_scalar(out=wuf[:, :, :], in0=wgi2[1][:, 0:KD, :], scalar1=DQ, scalar2=None, op0=ALU.mult))
            mkg[1] = P.mark()
            emit_mm_group(P, psU[:, :], [wuf[:, k, :] for k in range(KD)], [xnT[:, k, :] for k in range(KD)])
            P.op("scalar", lambda e: e.activation(out=hs[:, :], in_=psG[:, :], func=AF.Silu))
            P.op("vector", lambda e, _oc=oc: e.tensor_tensor(out=hsb[:, _oc, :], in0=hs[:, :], in1=psU[:, :], op=ALU.mult))
        _m0 = P.mark(); mkd = [_m0, _m0]
        for oc2 in range(KD):
            b = oc2 % 2
            P.dma(wgi2[b][:, :, :], wdT.ap()[:, 128 * oc2:128 * (oc2 + 1)].rearrange("(c p) m -> p c m", p=128), after=mkd[b], site="l3f", pf=b)
            P.pfuse("l3f", b)
            P.op("vector", lambda e, _b=b: e.tensor_scalar(out=wdf[:, :, :], in0=wgi2[_b][:, :, :], scalar1=DQ, scalar2=None, op0=ALU.mult))
            mkd[b] = P.mark()
            emit_mm_group(P, psD[:, :], [wdf[:, k, :] for k in range(NH_)], [hsb[:, k, :] for k in range(NH_)])
            P.op("vector", lambda e: e.tensor_scalar(out=dsb[:, :], in0=psD[:, :], scalar1=0.5, scalar2=None, op0=ALU.mult))
            for t in range(4):
                P.op("tensor", lambda e, _t=t: e.transpose(psT[:, :], dsb[:, 128 * _t:128 * (_t + 1)], idsb[:, :]))
                P.op("vector", lambda e, _t=t, _oc2=oc2: e.tensor_tensor(
                    out=xout[:, _t, 128 * _oc2:128 * (_oc2 + 1)],
                    in0=xsb[:, _t, 128 * _oc2:128 * (_oc2 + 1)], in1=psT[:, :], op=ALU.add))
        P.dma(xo.rearrange("(tc n p) d -> tc p n d", p=128, n=4)[tc], xout[:, :, :], engine="gpsimd")




_IDENT = np.eye(128, dtype=np.float32)


def build_l3ffn():
    P = Prog()
    IN = {}
    IN["x"] = P.din("x", [TOK, D]).ap()
    IN["wgT"] = P.din("wgT", [D, 3072], I32)
    IN["wuT"] = P.din("wuT", [D, 3072], I32)
    IN["wdT"] = P.din("wdT", [3072, D], I32)
    IN["ident"] = P.din("ident", [128, 128]).ap()
    IN["xo"] = P.dout("xo", [TOK, D]).ap()
    emit_l3ffn(P, IN)
    return P.finalize()

def _shard_tok(x):
    return [np.ascontiguousarray(x[c].reshape(TOK, D)) for c in range(N_CORES)]


def run_l3ffn(x, wg, wu, wd):
    nc = build_l3ffn()
    wgT = np.ascontiguousarray(wg.T)
    wuT = np.ascontiguousarray(wu.T)
    wdT = np.ascontiguousarray(wd.T)
    shards = _shard_tok(x)
    in_maps = [{"x": s, "wgT": wgT, "wuT": wuT, "wdT": wdT, "ident": _IDENT} for s in shards]
    res = run_bass_kernel_spmd(nc, in_maps, core_ids=list(range(N_CORES)))
    return np.stack([r["xo"].reshape(S, D) for r in res.results])





def emit_l1ffn(P, IN):
    x = IN["x"]
    k0 = IN["k0"]
    k1 = IN["k1"]
    k2 = IN["k2"]
    s0 = IN["s0"]
    s1 = IN["s1"]
    s2 = IN["s2"]
    ident = IN["ident"]
    xo = IN["xo"]

    H1 = 4096
    NH1 = H1 // 128
    KD = D // 128
    TC = 512
    NTC = TOK // TC

    xsb = P.sb([128, 4, D], name="xsb")
    idsb = P.sb([128, 128], name="idsb")
    xn = P.sb([128, 4, D], name="xn")
    xnT = P.sb([128, KD, TC], name="xnT")
    sq = P.sb([128, D], name="sq")
    stats = P.sb([128, 16], name="stats")
    svec = P.sb([128, 8], name="svec")
    w8a = [P.sb([128, NH1, 128], U8, name="w8a0"), P.sb([128, NH1, 128], U8, name="w8a1")]
    w8b = [P.sb([128, KD, 128], U8, name="w8b0"), P.sb([128, KD, 128], U8, name="w8b1")]
    wf = P.sb([128, NH1, 128], name="wf")
    r1 = P.sb([128, NH1, TC], name="r1")
    r2 = P.sb([128, NH1, TC], name="r2")
    dsb = sq[:, 0:TC]
    xout = xn
    psH = P.ps([128, 512], name="psH")
    psT = P.ps([128, 128], name="psT")

    P.dma(idsb[:, :], ident)
    for i, st in enumerate((s0, s1, s2)):
        P.dma(svec[:, i:i + 1], dview(st, 0, [[0, 128], [1, 1]]))
    P.op("vector", lambda e: e.tensor_scalar(out=svec[:, 3:5], in0=svec[:, 0:2], scalar1=2.0, scalar2=None, op0=ALU.mult))

    xr = x.rearrange("(tc n p) d -> tc p n d", p=128, n=4)
    xor_ = xo.rearrange("(tc n p) d -> tc p n d", p=128, n=4)
    for tc in range(NTC):
        P.dma(xsb[:, :, :], xr[tc])
        for t in range(4):
            emit_layernorm(P, xsb[:, t, :], xn[:, t, :], sq, stats)
            for c in range(KD):
                emit_transpose(P, idsb, xn[:, t, 128 * c:128 * (c + 1)], psT,
                               xnT[:, c, 128 * t:128 * (t + 1)])
        _m0 = P.mark(); mka = [_m0, _m0]
        for oc in range(NH1):
            b = oc % 2
            P.dma(w8a[b][:, 0:KD, :], k0.ap()[0:D, 128 * oc:128 * (oc + 1)].rearrange("(c p) m -> p c m", p=128), after=mka[b], site="l1f", pf=b)
            P.dma(w8b[b][:, 0:KD, :], k0.ap()[D:2 * D, 128 * oc:128 * (oc + 1)].rearrange("(c p) m -> p c m", p=128), after=mka[b], site="l1f", pf=b)
            P.pfuse("l1f", b)
            P.op("vector", lambda e, _b=b: e.tensor_tensor(out=wf[:, 0:KD, :], in0=w8a[_b][:, 0:KD, :], in1=w8b[_b][:, 0:KD, :], op=ALU.subtract))
            mka[b] = P.mark()
            emit_mm_group(P, psH[:, 0:TC], [wf[:, k, :] for k in range(KD)], [xnT[:, k, :] for k in range(KD)])
            P.op("vector", lambda e, _oc=oc: e.tensor_scalar(out=r1[:, _oc, :], in0=psH[:, 0:TC], scalar1=svec[:, 3:4], scalar2=0.0, op0=ALU.mult, op1=ALU.max))
        _m0 = P.mark(); mka = [_m0, _m0]
        for oc in range(NH1):
            b = oc % 2
            P.dma(w8a[b][:, :, :], k1.ap()[:, 128 * oc:128 * (oc + 1)].rearrange("(c p) m -> p c m", p=128), after=mka[b], site="l1f", pf=b)
            P.pfuse("l1f", b)
            P.op("vector", lambda e, _b=b: e.tensor_scalar(out=wf[:, :, :], in0=w8a[_b][:, :, :], scalar1=2.0, scalar2=-1.0, op0=ALU.mult, op1=ALU.add))
            mka[b] = P.mark()
            emit_mm_group(P, psH[:, 0:TC], [wf[:, k, :] for k in range(NH1)], [r1[:, k, :] for k in range(NH1)])
            P.op("vector", lambda e, _oc=oc: e.tensor_scalar(out=r2[:, _oc, :], in0=psH[:, 0:TC], scalar1=svec[:, 4:5], scalar2=0.0, op0=ALU.mult, op1=ALU.max))
        _m0 = P.mark(); mka = [_m0, _m0]
        for oc2 in range(KD):
            b = oc2 % 2
            P.dma(w8a[b][:, :, :], k2.ap()[:, 128 * oc2:128 * (oc2 + 1)].rearrange("(c p) m -> p c m", p=128), after=mka[b], site="l1f", pf=b)
            P.pfuse("l1f", b)
            P.op("vector", lambda e, _b=b: e.tensor_scalar(out=wf[:, :, :], in0=w8a[_b][:, :, :], scalar1=2.0, scalar2=-1.0, op0=ALU.mult, op1=ALU.add))
            mka[b] = P.mark()
            emit_mm_group(P, psH[:, 0:TC], [wf[:, k, :] for k in range(NH1)], [r2[:, k, :] for k in range(NH1)])
            P.op("vector", lambda e: e.tensor_scalar(out=dsb[:, :], in0=psH[:, 0:TC], scalar1=svec[:, 2:3], scalar2=None, op0=ALU.mult))
            for t in range(4):
                P.op("tensor", lambda e, _t=t: e.transpose(psT[:, :], dsb[:, 128 * _t:128 * (_t + 1)], idsb[:, :]))
                P.op("vector", lambda e, _t=t, _oc2=oc2: e.tensor_tensor(
                    out=xout[:, _t, 128 * _oc2:128 * (_oc2 + 1)],
                    in0=xsb[:, _t, 128 * _oc2:128 * (_oc2 + 1)], in1=psT[:, :], op=ALU.add))
        P.dma(xor_[tc], xout[:, :, :], engine="gpsimd")


def build_l1ffn():
    P = Prog()
    IN = {}
    IN["x"] = P.din("x", [TOK, D]).ap()
    IN["k0"] = P.din("k0", [2 * D, 4096], U8)
    IN["k1"] = P.din("k1", [4096, 4096], U8)
    IN["k2"] = P.din("k2", [4096, D], U8)
    IN["s0"] = P.din("s0", [1, 1])
    IN["s1"] = P.din("s1", [1, 1])
    IN["s2"] = P.din("s2", [1, 1])
    IN["ident"] = P.din("ident", [128, 128]).ap()
    IN["xo"] = P.dout("xo", [TOK, D]).ap()
    emit_l1ffn(P, IN)
    return P.finalize()

def run_l1ffn(x, k0, k1, k2, s0, s1, s2):
    nc = build_l1ffn()
    shards = _shard_tok(x)
    k0u = np.ascontiguousarray(k0).view(np.uint8)
    k1u = np.ascontiguousarray(k1).view(np.uint8)
    k2u = np.ascontiguousarray(k2).view(np.uint8)
    in_maps = [{"x": s, "k0": k0u, "k1": k1u, "k2": k2u,
                "s0": np.float32(s0).reshape(1, 1), "s1": np.float32(s1).reshape(1, 1),
                "s2": np.float32(s2).reshape(1, 1), "ident": _IDENT} for s in shards]
    res = run_bass_kernel_spmd(nc, in_maps, core_ids=list(range(N_CORES)))
    return np.stack([r["xo"].reshape(S, D) for r in res.results])





_RMAGIC = 8388608.0


def emit_fp4_quant(P, v_ap, q_ap, F, scr, scrn, FMAX=1472):
    if F > FMAX:
        o = 0
        while o < F:
            w = min(FMAX, F - o)
            emit_fp4_quant(P, v_ap[:, o:o + w], q_ap[:, o:o + w], w, scr, scrn, FMAX)
            o += w
        return
    NB = F // 16
    bm = scrn[:, 0:NB]
    rs2 = scrn[:, NB:2 * NB]
    av = scr[:, 0:F]
    t2 = scr[:, F:2 * F]
    acc = scr[:, 2 * F:3 * F]
    sg = scr[:, 3 * F:4 * F]
    u = av
    v3 = v_ap.rearrange("p (g b) -> p g b", b=16)
    P.op("vector", lambda e: e.tensor_reduce(out=bm, in_=v3, axis=AX.X, op=ALU.max, apply_absolute_value=True))
    P.op("vector", lambda e: e.tensor_scalar(out=bm, in0=bm, scalar1=1.0 / 6.0, scalar2=1e-12, op0=ALU.mult, op1=ALU.max))
    P.op("vector", lambda e: e.reciprocal(out=rs2, in_=bm))
    P.op("vector", lambda e: e.tensor_scalar(out=rs2, in0=rs2, scalar1=2.0, scalar2=None, op0=ALU.mult))
    P.op("scalar", lambda e: e.activation(out=av, in_=v_ap, func=AF.Abs))
    P.op("scalar", lambda e: e.activation(out=sg, in_=v_ap, func=AF.Sign))
    P.op("vector", lambda e: e.tensor_tensor(out=t2.rearrange("p (g b) -> p g b", b=16), in0=av.rearrange("p (g b) -> p g b", b=16), in1=rs2.rearrange("p (g o) -> p g o", o=1).broadcast_to([128, NB, 16]), op=ALU.mult))
    P.op("vector", lambda e: e.tensor_scalar(out=u, in0=t2, scalar1=_RMAGIC, scalar2=_RMAGIC, op0=ALU.add, op1=ALU.subtract))
    P.op("vector", lambda e: e.tensor_scalar(out=acc, in0=u, scalar1=0.5, scalar2=2.0, op0=ALU.mult, op1=ALU.min))
    P.op("vector", lambda e: e.tensor_scalar(out=u, in0=t2, scalar1=0.5, scalar2=_RMAGIC, op0=ALU.mult, op1=ALU.add))
    P.op("vector", lambda e: e.tensor_scalar(out=u, in0=u, scalar1=_RMAGIC, scalar2=2.0, op0=ALU.subtract, op1=ALU.max))
    P.op("vector", lambda e: e.tensor_scalar(out=u, in0=u, scalar1=4.0, scalar2=2.0, op0=ALU.min, op1=ALU.subtract))
    P.op("vector", lambda e: e.tensor_tensor(out=acc, in0=acc, in1=u, op=ALU.add))
    P.op("vector", lambda e: e.tensor_scalar(out=u, in0=t2, scalar1=0.25, scalar2=_RMAGIC, op0=ALU.mult, op1=ALU.add))
    P.op("vector", lambda e: e.tensor_scalar(out=u, in0=u, scalar1=_RMAGIC, scalar2=2.0, op0=ALU.subtract, op1=ALU.mult))
    P.op("vector", lambda e: e.tensor_scalar(out=u, in0=u, scalar1=4.0, scalar2=6.0, op0=ALU.max, op1=ALU.min))
    P.op("vector", lambda e: e.scalar_tensor_tensor(out=acc, in0=u, scalar=-4.0, in1=acc, op0=ALU.add, op1=ALU.add))
    P.op("vector", lambda e: e.tensor_tensor(out=sg.rearrange("p (g b) -> p g b", b=16), in0=sg.rearrange("p (g b) -> p g b", b=16), in1=bm.rearrange("p (g o) -> p g o", o=1).broadcast_to([128, NB, 16]), op=ALU.mult))
    P.op("vector", lambda e: e.tensor_tensor(out=q_ap, in0=acc, in1=sg, op=ALU.mult))


def emit_l2ffn(P, IN):
    x = IN["x"]
    wg = IN["wg"]
    wu = IN["wu"]
    wd = IN["wd"]
    ident = IN["ident"]
    xo = IN["xo"]
    if "qwgT" in IN:
        qwgT, qwuT, qwdT = IN["qwgT"], IN["qwuT"], IN["qwdT"]
        quant_done = IN["quant_done"]
    else:
        qwgT = P.dtmp([D, 2816], name="qwgT")
        qwuT = P.dtmp([D, 2816], name="qwuT")
        qwdT = P.dtmp([2816, D], name="qwdT")
        quant_done = False

    HID = 2816
    NHD = HID // 128
    KD = D // 128
    TC = 512
    NTC = TOK // TC

    idsb = P.sb([128, 128], name="idsb")
    wrow = P.sb([128, HID], name="wrow")
    scr = P.sb([128, 4 * 1472], name="scr")
    scrn = P.sb([128, 2 * (1472 // 16)], name="scrn")
    xsb = P.sb([128, 4, D], name="xsb")
    xn = P.sb([128, D], name="xn")
    sq = P.sb([128, D], name="sq")
    stats = P.sb([128, 16], name="stats")
    qxT = P.sb([128, KD, TC], name="qxT")
    wfm2 = P.sb([128, NHD, 128], name="wfm2")
    wfmB = P.sb([128, NHD, 128], name="wfmB")
    wfm = wfm2
    htok = P.sb([128, 4, HID], name="htok")
    hq = htok
    hqT = P.sb([128, NHD, TC], name="hqT")
    hs = P.sb([128, TC], name="hs")
    dsb = P.sb([128, TC], name="dsb")

    psG = P.ps([128, 512], name="psG")
    psU = P.ps([128, 512], name="psU")
    psT = P.ps([128, 128], name="psT")

    P.dma(idsb[:, :], ident)

    def quant_weight(wsrc, qdstT, rows, cols):
        for rc in range(rows // 128):
            P.dma(wrow[:, 0:cols], wsrc.ap()[128 * rc:128 * (rc + 1), :])
            emit_fp4_quant(P, wrow[:, 0:cols], wrow[:, 0:cols], cols, scr, scrn)
            for cc in range(cols // 128):
                emit_transpose(P, idsb, wrow[:, 128 * cc:128 * (cc + 1)], psT, dsb[:, 0:128])
                P.dma(qdstT.ap()[128 * cc:128 * (cc + 1), 128 * rc:128 * (rc + 1)], dsb[:, 0:128], engine="gpsimd")

    if not quant_done:
        quant_weight(wg, qwgT, 2816, D)
        quant_weight(wu, qwuT, 2816, D)
        quant_weight(wd, qwdT, D, 2816)

    xr = x.rearrange("(tc n p) d -> tc p n d", p=128, n=4)
    xor_ = xo.rearrange("(tc n p) d -> tc p n d", p=128, n=4)
    for tc in range(NTC):
        P.dma(xsb[:, :, :], xr[tc])
        for t in range(4):
            emit_layernorm(P, xsb[:, t, :], xn[:, :], sq, stats)
            emit_fp4_quant(P, xn[:, :], xn[:, :], D, scr, scrn)
            for c in range(KD):
                emit_transpose(P, idsb, xn[:, 128 * c:128 * (c + 1)], psT,
                               qxT[:, c, 128 * t:128 * (t + 1)])
        _m0 = P.mark(); mkq = [_m0, _m0]
        for oc in range(NHD):
            P.dma(wfm[:, 0:KD, :], qwgT.ap()[:, 128 * oc:128 * (oc + 1)].rearrange("(c p) m -> p c m", p=128), after=mkq[0], site="l2f", pf=0)
            P.dma(wfmB[:, 0:KD, :], qwuT.ap()[:, 128 * oc:128 * (oc + 1)].rearrange("(c p) m -> p c m", p=128), after=mkq[1], site="l2f", pf=1)
            P.pfuse("l2f", 0)
            emit_mm_group(P, psG[:, :], [wfm[:, k, :] for k in range(KD)], [qxT[:, k, :] for k in range(KD)])
            mkq[0] = P.mark()
            P.pfuse("l2f", 1)
            emit_mm_group(P, psU[:, :], [wfmB[:, k, :] for k in range(KD)], [qxT[:, k, :] for k in range(KD)])
            mkq[1] = P.mark()
            P.op("scalar", lambda e: e.activation(out=hs[:, :], in_=psG[:, :], func=AF.Silu))
            P.op("vector", lambda e: e.tensor_tensor(out=hs[:, :], in0=hs[:, :], in1=psU[:, :], op=ALU.mult))
            for t in range(4):
                P.op("tensor", lambda e, _t=t: e.transpose(psT[:, :], hs[:, 128 * _t:128 * (_t + 1)], idsb[:, :]))
                P.op("vector", lambda e, _t=t, _oc=oc: e.tensor_copy(htok[:, _t, 128 * _oc:128 * (_oc + 1)], psT[:, :]))
        for t in range(4):
            emit_fp4_quant(P, htok[:, t, :], htok[:, t, :], HID, scr, scrn)
            for c in range(NHD):
                emit_transpose(P, idsb, htok[:, t, 128 * c:128 * (c + 1)], psT,
                               hqT[:, c, 128 * t:128 * (t + 1)])
        _m0 = P.mark(); mkq = [_m0, _m0]
        for oc2 in range(KD):
            b = oc2 % 2
            wsel = wfm2 if b == 0 else wfmB
            P.dma(wsel[:, :, :], qwdT.ap()[:, 128 * oc2:128 * (oc2 + 1)].rearrange("(c p) m -> p c m", p=128), after=mkq[b], site="l2f", pf=b)
            P.pfuse("l2f", b)
            emit_mm_group(P, psG[:, :], [wsel[:, k, :] for k in range(NHD)], [hqT[:, k, :] for k in range(NHD)])
            mkq[b] = P.mark()
            P.op("vector", lambda e: e.tensor_scalar(out=dsb[:, :], in0=psG[:, :], scalar1=0.5, scalar2=None, op0=ALU.mult))
            for t in range(4):
                P.op("tensor", lambda e, _t=t: e.transpose(psT[:, :], dsb[:, 128 * _t:128 * (_t + 1)], idsb[:, :]))
                P.op("vector", lambda e, _t=t, _oc2=oc2: e.tensor_tensor(
                    out=htok[:, _t, 128 * _oc2:128 * (_oc2 + 1)],
                    in0=xsb[:, _t, 128 * _oc2:128 * (_oc2 + 1)], in1=psT[:, :], op=ALU.add))
        P.dma(xor_[tc], htok[:, :, 0:D], engine="gpsimd")


def build_l2ffn():
    P = Prog()
    IN = {}
    IN["x"] = P.din("x", [TOK, D]).ap()
    IN["wu"] = P.din("wu", [2816, D])
    IN["wd"] = P.din("wd", [D, 2816])
    IN["ident"] = P.din("ident", [128, 128]).ap()
    IN["xo"] = P.dout("xo", [TOK, D]).ap()
    IN["wg"] = P.din("wg", [2816, D])
    emit_l2ffn(P, IN)
    return P.finalize()

def run_l2ffn(x, wg, wu, wd):
    nc = build_l2ffn()
    shards = _shard_tok(x)
    in_maps = [{"x": s, "wg": wg, "wu": wu, "wd": wd, "ident": _IDENT} for s in shards]
    res = run_bass_kernel_spmd(nc, in_maps, core_ids=list(range(N_CORES)))
    return np.stack([r["xo"].reshape(S, D) for r in res.results])



def emit_transpose_batch(P, ident, src_aps, psb, dst_ap_fn=None, evac_batch=None):
    n = len(src_aps)

    def fn(e):
        inst = None
        for i in range(n):
            inst = e.transpose(psb[:, 128 * i:128 * (i + 1)], src_aps[i], ident[:, :])
        return inst
    P.op("tensor", fn)
    if evac_batch is not None:
        P.op("vector", lambda e: e.tensor_copy(evac_batch, psb[:, 0:128 * n]))
    else:
        for i in range(n):
            P.op("vector", lambda e, _i=i: e.tensor_copy(dst_ap_fn(_i), psb[:, 128 * _i:128 * (_i + 1)]))


def emit_l3attn(P, IN):
    x = IN["x"]
    Wq = IN["Wq"]
    Wk = IN["Wk"]
    Wv = IN["Wv"]
    Wo = IN["Wo"]
    W1 = IN["W1"]
    W2 = IN["W2"]
    ln1w = IN["ln1w"]
    ln2w = IN["ln2w"]
    relb = IN["relb"]
    scale_h = IN["scale_h"]
    ident = IN["ident"]
    ranti = IN["ranti"]
    ET = IN["ET"]
    xo = IN["xo"]

    xrT_d = P.dtmp([D, TOK], name="xrTd")
    ao_d = P.dtmp([D, TOK], name="aod")
    tpad_d = P.dtmp([16, 2304], name="tpadd")

    KD = D // 128
    xsb = P.sb([128, NT, D], name="xsb")
    idsb = P.sb([128, 128], name="idsb")
    rasb = P.sb([128, 128], name="rasb")
    lnwsb = P.sb([128, 2, KD], name="lnwsb")
    scv = P.sb([128, 16], name="scv")
    sq = P.sb([128, D], name="sq")
    stats = P.sb([128, 32], name="stats")
    xr = P.sb([128, D], name="xr")
    wsb = P.sb([128, KD, 128], name="wsb")
    wsb2 = P.sb([128, KD, 128], name="wsb2")
    wsb3 = P.sb([128, KD, 128], name="wsb3")
    tpsb = P.sb([16, S], name="tpsb")
    relbsb = P.sb([32, 16], name="relbsb")
    ETsb = P.sb([32, S], name="ETsb")
    masksb = P.sb([128, 2, 128], name="masksb")
    U = P.sb([128, 21504], name="U")
    qtok = U[:, 0:2048].rearrange("p (n m) -> p n m", m=128)
    ktok = U[:, 2048:4096].rearrange("p (n m) -> p n m", m=128)
    vtok = U[:, 4096:6144].rearrange("p (n m) -> p n m", m=128)
    qfm = U[:, 6144:8192]
    kfm = U[:, 8192:10240]
    arow = U[:, 10240:12288]
    aT = U[:, 12288:14336]
    aosb = U[:, 14336:16384]
    biasW = U[:, 16384:18432].rearrange("p (n m) -> p n m", m=128)
    xrTl = U[:, 18432:19456].rearrange("p (n m) -> p n m", m=128)
    wobuf = U[:, 0:8192].rearrange("p (c o) -> p c o", o=1024)
    aol = U[:, 8192:9216].rearrange("p (c m) -> p c m", m=128)
    y1tmp = U[:, 9216:10240]
    zsb = U[:, 0:2048].rearrange("p (n d) -> p n d", d=1024)
    zT = U[:, 2048:4096].rearrange("p (c m) -> p c m", m=256)
    hfm = U[:, 4096:12288].rearrange("p (c m) -> p c m", m=256)
    w2sb = U[:, 12288:16384].rearrange("p (c m) -> p c m", m=128)
    dsb2 = U[:, 16384:16896]
    y1sb = U[:, 17920:19968].rearrange("p (n d) -> p n d", d=1024)
    w1l = U[:, 19968:20992].rearrange("p (c m) -> p c m", m=128)
    w1lB = P.sb([128, 8, 128], name="w1lB")
    w2sbB = P.sb([128, 32, 128], name="w2sbB")
    psA = P.ps([128, 512], name="psA")
    psB = P.ps([128, 512], name="psB")
    psS = P.ps([128, 2048], name="psS")
    psT4 = P.ps([128, 512], name="psT4")

    P.dma(idsb[:, :], ident)
    P.dma(rasb[:, :], ranti)
    P.dma(lnwsb[:, 0, :], dview(ln1w, 0, [[1, 128], [128, KD]]), allow_slow=True)
    P.dma(lnwsb[:, 1, :], dview(ln2w, 0, [[1, 128], [128, KD]]), allow_slow=True)
    P.dma(scv[:, :], dview(scale_h, 0, [[0, 128], [1, 16]]))
    P.dma(xsb[:, :, :], x.rearrange("(n p) d -> p n d", p=128))
    P.op("gpsimd", lambda e: e.iota(out=masksb.bitcast(I32)[:, 0, :], pattern=[[-1, 128]], base=0, channel_multiplier=1))
    P.op("vector", lambda e: e.tensor_scalar(out=masksb[:, 0, :], in0=masksb.bitcast(I32)[:, 0, :], scalar1=0.0, scalar2=None, op0=ALU.is_gt))
    P.op("vector", lambda e: e.tensor_scalar(out=masksb[:, 1, :], in0=masksb[:, 0, :], scalar1=1e6, scalar2=-1e6, op0=ALU.mult, op1=ALU.add))
    P.dma(relbsb[:, :], relb)
    P.dma(ETsb[:, :], ET)
    for c in range(4):
        emit_mm_group(P, psA[0:16, :], [relbsb[:, :]], [ETsb[:, 512 * c:512 * (c + 1)]])
        P.op("vector", lambda e, _c=c: e.tensor_copy(tpsb[:, 512 * _c:512 * (_c + 1)], psA[0:16, :]))
    P.op("vector", lambda e: e.memset(sq[0:16, 0:256], 0.0))
    P.dma(tpad_d.ap()[:, 128:2176], tpsb[:, :], engine="gpsimd")
    P.dma(tpad_d.ap()[:, 0:128], sq[0:16, 0:128], engine="gpsimd")
    P.dma(tpad_d.ap()[:, 2176:2304], sq[0:16, 128:256], engine="gpsimd")

    for t in range(NT):
        emit_layernorm(P, xsb[:, t, :], xr[:, :], sq, stats)
        emit_rmsnorm(P, xr[:, :], xr[:, :], sq, stats, 1e-6, D)
        for g in range(2):
            emit_transpose_batch(P, idsb, [xr[:, 128 * (4 * g + i):128 * (4 * g + i + 1)] for i in range(4)],
                                 psT4, evac_batch=sq[:, 0:512])
            P.dma(dview(xrT_d, (4 * g * 128) * TOK + 128 * t, [[TOK, 128], [128 * TOK, 4], [1, 128]]),
                  sq[:, 0:512].rearrange("p (c m) -> p c m", m=128), engine="gpsimd")

    for hc in range(KD):
        for w_dram, w_sb in ((Wq, wsb), (Wk, wsb2), (Wv, wsb3)):
            P.dma(w_sb[:, :, :], w_dram.ap()[:, 128 * hc:128 * (hc + 1)].rearrange("(c p) m -> p c m", p=128))
        for c in range(KD):
            for w_sb in (wsb, wsb2, wsb3):
                P.op("vector", lambda e, _c=c, _w=w_sb: e.tensor_scalar(out=_w[:, _c, :], in0=_w[:, _c, :], scalar1=lnwsb[:, 0, _c:_c + 1], scalar2=None, op0=ALU.mult))
        for t in range(NT):
            P.dma(xrTl[:, :, :], dview(xrT_d, 128 * t, [[TOK, 128], [128 * TOK, KD], [1, 128]]))
            emit_mm_group(P, psA[:, 0:128], [xrTl[:, c, :] for c in range(KD)], [wsb[:, c, :] for c in range(KD)])
            emit_mm_group(P, psA[:, 128:256], [xrTl[:, c, :] for c in range(KD)], [wsb2[:, c, :] for c in range(KD)])
            emit_mm_group(P, psA[:, 256:384], [xrTl[:, c, :] for c in range(KD)], [wsb3[:, c, :] for c in range(KD)])
            for (off, dst) in ((0, qtok), (128, ktok)):
                P.op("scalar", lambda e, _o=off: e.activation(out=sq[:, 0:128], in_=psA[:, _o:_o + 128], func=AF.Square))
                P.op("vector", lambda e: e.tensor_reduce(out=stats[:, 8:10], in_=sq[:, 0:128].rearrange("p (h d) -> p h d", d=64), axis=AX.X, op=ALU.add))
                P.op("vector", lambda e: e.tensor_scalar(out=stats[:, 8:10], in0=stats[:, 8:10], scalar1=1e-6, scalar2=None, op0=ALU.add))
                P.op("scalar", lambda e: e.activation(out=stats[:, 8:10], in_=stats[:, 8:10], func=AF.Sqrt))
                P.op("vector", lambda e: e.reciprocal(out=stats[:, 8:10], in_=stats[:, 8:10]))
                P.op("vector", lambda e, _o=off, _d=dst, _t=t: e.tensor_tensor(
                    out=_d[:, _t, :].rearrange("p (h d) -> p h d", d=64),
                    in0=psA[:, _o:_o + 128].rearrange("p (h d) -> p h d", d=64),
                    in1=stats[:, 8:10].rearrange("p (h o) -> p h o", o=1).broadcast_to([128, 2, 64]), op=ALU.mult))
            P.op("vector", lambda e, _t=t: e.tensor_copy(vtok[:, _t, :], psA[:, 256:384]))
        for t4 in range(NT // 4):
            emit_transpose_batch(P, idsb, [qtok[:, 4 * t4 + i, :] for i in range(4)], psT4,
                                 evac_batch=qfm[:, 512 * t4:512 * (t4 + 1)])
            emit_transpose_batch(P, idsb, [ktok[:, 4 * t4 + i, :] for i in range(4)], psT4,
                                 evac_batch=kfm[:, 512 * t4:512 * (t4 + 1)])
        for hh in range(2):
            h = 2 * hc + hh
            rb = hh * 64
            for dtile in range(NT):
                P.dma(biasW[:, dtile, :], dview(tpad_d, h * 2304 + 1 + 128 * dtile, [[1, 128], [1, 128]]))
            for it in range(NT):
                jts = list(range(NT)) if it == 0 else list(range(it + 1))
                mm_jts = [0] if it == 0 else jts

                def qk_bias_fn(e, _it=it, _jts=tuple(mm_jts), _rb=rb):
                    inst = None
                    for jt in _jts:
                        e.matmul(psS[:, 128 * jt:128 * (jt + 1)],
                                 qfm[_rb:_rb + 64, 128 * _it:128 * (_it + 1)],
                                 kfm[_rb:_rb + 64, 128 * jt:128 * (jt + 1)],
                                 start=True, stop=False)
                        inst = e.matmul(psS[:, 128 * jt:128 * (jt + 1)],
                                        biasW[:, _it - jt, :], rasb[:, :],
                                        start=False, stop=True)
                    return inst
                P.op("tensor", qk_bias_fn)
                mmw = 128 * len(mm_jts)
                P.op("vector", lambda e, _w=mmw, _h=h: e.tensor_scalar(out=arow[:, 0:_w], in0=psS[:, 0:_w], scalar1=scv[:, _h:_h + 1], scalar2=None, op0=ALU.mult))
                P.op("vector", lambda e, _it=it: e.tensor_tensor(out=arow[:, 128 * _it:128 * (_it + 1)], in0=arow[:, 128 * _it:128 * (_it + 1)], in1=masksb[:, 0, :], op=ALU.mult))
                P.op("vector", lambda e, _it=it: e.tensor_tensor(out=arow[:, 128 * _it:128 * (_it + 1)], in0=arow[:, 128 * _it:128 * (_it + 1)], in1=masksb[:, 1, :], op=ALU.add))
                if it == 0:
                    P.op("vector", lambda e: e.memset(arow[:, 128:2048], -1e6))
                elif it < NT - 1:
                    P.op("vector", lambda e, _it=it: e.memset(arow[:, 128 * (_it + 1):2048], -1e6))
                P.op("vector", lambda e: e.tensor_reduce(out=stats[:, 12:13], in_=arow[:, :], axis=AX.X, op=ALU.max))
                P.op("vector", lambda e: e.tensor_scalar(out=stats[:, 13:14], in0=stats[:, 12:13], scalar1=-1.0, scalar2=None, op0=ALU.mult))
                P.op("scalar", lambda e: e.activation(out=arow[:, :], in_=arow[:, :], func=AF.Exp, bias=stats[:, 13:14], accum_out=stats[:, 14:15]))
                P.op("vector", lambda e: e.reciprocal(out=stats[:, 14:15], in_=stats[:, 14:15]))
                P.op("vector", lambda e: e.tensor_scalar(out=arow[:, :], in0=arow[:, :], scalar1=stats[:, 14:15], scalar2=None, op0=ALU.mult))
                kts = jts
                for b0 in range(0, len(kts), 4):
                    grp = kts[b0:b0 + 4]
                    emit_transpose_batch(P, idsb, [arow[:, 128 * kt:128 * (kt + 1)] for kt in grp], psT4,
                                         evac_batch=aT[:, 128 * grp[0]:128 * (grp[0] + len(grp))])

                def av_fn(e, _kts=tuple(kts), _rb=rb):
                    inst = None
                    for i, kt in enumerate(_kts):
                        inst = e.matmul(psB[0:64, 0:128], vtok[:, kt, _rb:_rb + 64], aT[:, 128 * kt:128 * (kt + 1)],
                                        start=(i == 0), stop=(i == len(_kts) - 1))
                    return inst
                P.op("tensor", av_fn)
                P.op("vector", lambda e, _rb=rb, _it=it: e.tensor_copy(aosb[_rb:_rb + 64, 128 * _it:128 * (_it + 1)], psB[0:64, 0:128]))
        P.dma(ao_d.ap()[128 * hc:128 * (hc + 1), :], aosb[:, :], engine="gpsimd")

    P.dma(wobuf[:, :, :], Wo.ap().rearrange("(c p) o -> p c o", p=128))
    for t in range(NT):
        P.dma(aol[:, :, :], dview(ao_d, 128 * t, [[TOK, 128], [128 * TOK, KD], [1, 128]]))
        emit_layernorm(P, xsb[:, t, :], xr[:, :], sq, stats)
        for oh in range(2):
            emit_mm_group(P, psA[:, :], [aol[:, c, :] for c in range(KD)],
                          [wobuf[:, c, 512 * oh:512 * (oh + 1)] for c in range(KD)])
            P.op("vector", lambda e, _oh=oh: e.tensor_tensor(
                out=y1tmp[:, 512 * _oh:512 * (_oh + 1)], in0=xr[:, 512 * _oh:512 * (_oh + 1)],
                in1=psA[:, :], op=ALU.add))
        P.dma(dview(xrT_d, t * 128 * D, [[D, 128], [1, D]]), y1tmp[:, :], engine="gpsimd")

    for tcD in range(8):
        P.dma(y1sb[:, :, :], dview(xrT_d, tcD * 2 * 128 * D, [[D, 128], [128 * D, 2], [1, D]]))
        for t in range(2):
            emit_rmsnorm(P, y1sb[:, t, :], zsb[:, t, :], sq, stats, 1e-6, D)
        for t in range(2):
            for g in range(2):
                emit_transpose_batch(P, idsb, [zsb[:, t, 128 * (4 * g + i):128 * (4 * g + i + 1)] for i in range(4)], psT4,
                                     dst_ap_fn=lambda i, _t=t, _g=g: zT[:, 4 * _g + i, 128 * _t:128 * (_t + 1)])
        for c in range(KD):
            P.op("vector", lambda e, _c=c: e.tensor_scalar(out=zT[:, _c, :], in0=zT[:, _c, :], scalar1=lnwsb[:, 1, _c:_c + 1], scalar2=None, op0=ALU.mult))
        _m0 = P.mark(); mk3 = [_m0, _m0]
        for oc in range(32):
            b = oc % 2
            w1sel = w1l if b == 0 else w1lB
            P.dma(w1sel[:, :, :], W1.ap()[:, 128 * oc:128 * (oc + 1)].rearrange("(c p) m -> p c m", p=128), after=mk3[b], site="l3a", pf=b)
            P.pfuse("l3a", b)
            emit_mm_group(P, psA[:, 0:256], [w1sel[:, c, :] for c in range(KD)], [zT[:, c, :] for c in range(KD)])
            mk3[b] = P.mark()
            P.op("vector", lambda e, _oc=oc: e.tensor_scalar(out=hfm[:, _oc, :], in0=psA[:, 0:256], scalar1=0.0, scalar2=None, op0=ALU.max))
        for t in range(2):
            tt_ = 2 * tcD + t
            P.op("vector", lambda e, _t=t, _tt=tt_: e.tensor_tensor(out=y1sb[:, _t, :], in0=y1sb[:, _t, :], in1=xsb[:, _tt, :], op=ALU.add))
        _m0 = P.mark(); mk4 = [_m0, _m0]
        for oc2 in range(KD):
            b = oc2 % 2
            w2sel = w2sb if b == 0 else w2sbB
            P.dma(w2sel[:, :, :], W2.ap()[:, 128 * oc2:128 * (oc2 + 1)].rearrange("(c p) m -> p c m", p=128), after=mk4[b], site="l3a", pf=b)
            P.pfuse("l3a", b)
            emit_mm_group(P, psA[:, 0:256], [w2sel[:, c, :] for c in range(32)], [hfm[:, c, :] for c in range(32)])
            mk4[b] = P.mark()
            P.op("vector", lambda e: e.tensor_copy(dsb2[:, 0:256], psA[:, 0:256]))
            for t in range(2):
                P.op("tensor", lambda e, _t=t: e.transpose(psT4[:, 0:128], dsb2[:, 128 * _t:128 * (_t + 1)], idsb[:, :]))
                P.op("vector", lambda e, _t=t, _oc2=oc2: e.tensor_tensor(
                    out=zsb[:, _t, 128 * _oc2:128 * (_oc2 + 1)],
                    in0=y1sb[:, _t, 128 * _oc2:128 * (_oc2 + 1)], in1=psT4[:, 0:128], op=ALU.add))
        P.dma(xo.rearrange("(tc n p) d -> tc p n d", p=128, n=2)[tcD], zsb[:, :, :], engine="gpsimd")


def build_l3attn():
    P = Prog()
    IN = {}
    IN["x"] = P.din("x", [TOK, D]).ap()
    IN["Wq"] = P.din("Wq", [D, D])
    IN["Wk"] = P.din("Wk", [D, D])
    IN["Wv"] = P.din("Wv", [D, D])
    IN["Wo"] = P.din("Wo", [D, D])
    IN["W1"] = P.din("W1", [D, 4096])
    IN["W2"] = P.din("W2", [4096, D])
    IN["ln1w"] = P.din("ln1w", [1, D])
    IN["ln2w"] = P.din("ln2w", [1, D])
    IN["relb"] = P.din("relb", [32, 16]).ap()
    IN["scale_h"] = P.din("scale_h", [1, 16])
    IN["ident"] = P.din("ident", [128, 128]).ap()
    IN["ranti"] = P.din("ranti", [128, 128]).ap()
    IN["ET"] = P.din("ET", [32, S]).ap()
    IN["xo"] = P.dout("xo", [TOK, D]).ap()
    emit_l3attn(P, IN)
    return P.finalize()

def _bucket_rel(d):
    d = np.asarray(d)
    max_exact = 16
    is_small = d < max_exact
    safe = np.maximum(d, 1).astype(np.float32)
    large = max_exact + (np.log(safe / np.float32(max_exact)) / np.float32(math.log(128.0 / max_exact)) * 16).astype(np.int32)
    large = np.minimum(large, 31)
    return np.where(is_small, d, large)


def _build_ET():
    d = np.arange(S)
    b = _bucket_rel(d)
    ET = np.zeros((32, S), np.float32)
    ET[b, d] = 1.0
    return ET


_RANTI = np.eye(128, dtype=np.float32)[::-1].copy()


def run_l3attn(x, Wq, Wk, Wv, rel_bias, scale, Wo, ln1w, ln2w, W1, W2):
    nc = build_l3attn()
    shards = _shard_tok(x)
    ET = _build_ET()
    in_maps = [{"x": s, "Wq": Wq, "Wk": Wk, "Wv": Wv, "Wo": Wo, "W1": W1, "W2": W2,
                "ln1w": ln1w.reshape(1, D), "ln2w": ln2w.reshape(1, D),
                "relb": rel_bias, "scale_h": scale.reshape(1, 16),
                "ident": _IDENT, "ranti": _RANTI, "ET": ET} for s in shards]
    res = run_bass_kernel_spmd(nc, in_maps, core_ids=list(range(N_CORES)))
    return np.stack([r["xo"].reshape(S, D) for r in res.results])



_L0_OFF = [0, 2048, 3072, 3584, 3840, 3968]
_L0_N = [2048, 1024, 512, 256, 128, 64]


def emit_l0attn(P, IN):
    x = IN["x"]
    Wq = IN["Wq"]
    Wk = IN["Wk"]
    Wv = IN["Wv"]
    bq = IN["bq"]
    bk = IN["bk"]
    bv = IN["bv"]
    ident = IN["ident"]
    mdiag = IN["mdiag"]
    msib = IN["msib"]
    xo = IN["xo"]

    xnT_d = P.dtmp([D, TOK], name="xnTd")
    ao_d = P.dtmp([TOK, D], name="aod")

    KD = D // 128
    xsb = P.sb([128, NT, D], name="xsb")
    idsb = P.sb([128, 128], name="idsb")
    mdsb = P.sb([128, 2, 128], name="mdsb")
    mssb = P.sb([128, 2, 128], name="mssb")
    bsb = P.sb([1, 3, D], name="bsb")
    onesb = P.sb([1, 512], name="onesb")
    sq = P.sb([128, D], name="sq")
    stats = P.sb([128, 32], name="stats")
    xr = P.sb([128, D], name="xr")
    xnTl = P.sb([128, KD, 512], name="xnTl")
    wq_sb = P.sb([128, KD, 64], name="wq_sb")
    wk_sb = P.sb([128, KD, 64], name="wk_sb")
    wv_sb = P.sb([128, KD, 64], name="wv_sb")
    qfm = P.sb([64, 4096], name="qfm")
    kfm = P.sb([64, 4096], name="kfm")
    vfm = P.sb([64, 4096], name="vfm")
    vtok = P.sb([128, 32, 65], name="vtok")
    Ybuf = P.sb([65, 4096], name="Ybuf")
    arow = P.sb([128, 128], name="arow")
    aTs = P.sb([128, 128], name="aTs")
    ytk = P.sb([128, 65], name="ytk")
    psS = P.ps([128, 128], name="psS")
    psT = P.ps([128, 512], name="psT")
    psY = P.ps([128, 128], name="psY")
    psQ = P.ps([64, 512], name="psQ")
    psK = P.ps([64, 512], name="psK")

    P.dma(idsb[:, :], ident)
    P.dma(mdsb[:, :, :], mdiag)
    P.dma(mssb[:, :, :], msib)
    P.dma(bsb[:, 0, :], bq)
    P.dma(bsb[:, 1, :], bk)
    P.dma(bsb[:, 2, :], bv)
    P.op("vector", lambda e: e.memset(onesb[:, :], 1.0))
    P.op("vector", lambda e: e.memset(vtok[:, :, 64:65].rearrange("p a b -> p (a b)"), 1.0))
    P.dma(xsb[:, :, :], x.rearrange("(n p) d -> p n d", p=128))

    for t in range(NT):
        emit_layernorm(P, xsb[:, t, :], xr[:, :], sq, stats)
        for g in range(2):
            emit_transpose_batch(P, idsb, [xr[:, 128 * (4 * g + i):128 * (4 * g + i + 1)] for i in range(4)],
                                 psT, evac_batch=sq[:, 0:512])
            P.dma(dview(xnT_d, (4 * g * 128) * TOK + 128 * t, [[TOK, 128], [128 * TOK, 4], [1, 128]]),
                  sq[:, 0:512].rearrange("p (c m) -> p c m", m=128), engine="gpsimd")

    for h in range(16):
        hb = 64 * h
        P.dma(wq_sb[:, :, :], Wq.ap()[:, hb:hb + 64].rearrange("(c p) m -> p c m", p=128))
        P.dma(wk_sb[:, :, :], Wk.ap()[:, hb:hb + 64].rearrange("(c p) m -> p c m", p=128))
        P.dma(wv_sb[:, :, :], Wv.ap()[:, hb:hb + 64].rearrange("(c p) m -> p c m", p=128))
        for ch in range(4):
            P.dma(xnTl[:, :, :], dview(xnT_d, 512 * ch, [[TOK, 128], [128 * TOK, KD], [1, 512]]))

            def qk_fn(e, _ps, _w, _bcol, _hb=hb):
                for c in range(KD):
                    e.matmul(_ps[:, :], _w[:, c, :], xnTl[:, c, :], start=(c == 0), stop=False)
                return e.matmul(_ps[:, :], bsb[:, _bcol, _hb:_hb + 64], onesb[:, :], start=False, stop=True)
            P.op("tensor", lambda e, f=qk_fn: f(e, psQ, wq_sb, 0))
            P.op("vector", lambda e, _ch=ch: e.tensor_scalar(out=qfm[:, 512 * _ch:512 * (_ch + 1)], in0=psQ[:, :], scalar1=0.125, scalar2=None, op0=ALU.mult))
            P.op("tensor", lambda e, f=qk_fn: f(e, psK, wk_sb, 1))
            P.op("vector", lambda e, _ch=ch: e.tensor_copy(kfm[:, 512 * _ch:512 * (_ch + 1)], psK[:, :]))
            P.op("tensor", lambda e, f=qk_fn: f(e, psQ, wv_sb, 2))
            P.op("vector", lambda e, _ch=ch: e.tensor_copy(vfm[:, 512 * _ch:512 * (_ch + 1)], psQ[:, :]))
        for l in range(5):
            o0, n0, o1 = _L0_OFF[l], _L0_N[l], _L0_OFF[l + 1]
            n1 = n0 // 2
            for buf, avg in ((qfm, True), (kfm, True), (vfm, False)):
                P.op("vector", lambda e, _b=buf, _o0=o0, _o1=o1, _n1=n1: e.tensor_tensor(
                    out=_b[:, _o1:_o1 + _n1],
                    in0=_b[:, _o0:_o0 + 2 * _n1].rearrange("p (a two) -> p a two", two=2)[:, :, 0],
                    in1=_b[:, _o0:_o0 + 2 * _n1].rearrange("p (a two) -> p a two", two=2)[:, :, 1], op=ALU.add))
                if avg:
                    P.op("vector", lambda e, _b=buf, _o1=o1, _n1=n1: e.tensor_scalar(out=_b[:, _o1:_o1 + _n1], in0=_b[:, _o1:_o1 + _n1], scalar1=0.5, scalar2=None, op0=ALU.mult))
        cidx = 0
        chunk_list = []
        for l in range(6):
            o, n = _L0_OFF[l], _L0_N[l]
            for c0 in range(0, n, 128):
                npos = min(128, n - c0)
                chunk_list.append((l, c0, npos, cidx))
                P.op("tensor", lambda e, _o=o, _c0=c0, _n=npos: e.transpose(psY[0:_n, 0:64], vfm[:, _o + _c0:_o + _c0 + _n], idsb[0:64, 0:64]))
                P.op("vector", lambda e, _i=cidx, _n=npos: e.tensor_copy(vtok[0:_n, _i, 0:64], psY[0:_n, 0:64]))
                cidx += 1
        for (l, c0, npos, vi) in chunk_list:
            o = _L0_OFF[l]
            msk = mdsb if l == 0 else mssb
            P.op("tensor", lambda e, _o=o, _c0=c0, _n=npos: e.matmul(
                psS[0:_n, 0:_n], qfm[:, _o + _c0:_o + _c0 + _n], kfm[:, _o + _c0:_o + _c0 + _n], start=True, stop=True))
            P.op("vector", lambda e, _m=msk, _n=npos: e.tensor_tensor(out=arow[0:_n, 0:_n], in0=psS[0:_n, 0:_n], in1=_m[0:_n, 0, 0:_n], op=ALU.mult))
            P.op("vector", lambda e, _m=msk, _n=npos: e.tensor_tensor(out=arow[0:_n, 0:_n], in0=arow[0:_n, 0:_n], in1=_m[0:_n, 1, 0:_n], op=ALU.add))
            P.op("vector", lambda e, _n=npos: e.tensor_reduce(out=stats[0:_n, 12:13], in_=arow[0:_n, 0:_n], axis=AX.X, op=ALU.max))
            P.op("vector", lambda e, _n=npos: e.tensor_scalar(out=stats[0:_n, 13:14], in0=stats[0:_n, 12:13], scalar1=-1.0, scalar2=None, op0=ALU.mult))
            P.op("scalar", lambda e, _n=npos: e.activation(out=arow[0:_n, 0:_n], in_=arow[0:_n, 0:_n], func=AF.Exp, bias=stats[0:_n, 13:14]))
            P.op("tensor", lambda e, _n=npos: e.transpose(psY[0:_n, 0:_n], arow[0:_n, 0:_n], idsb[0:_n, 0:_n]))
            P.op("vector", lambda e, _n=npos: e.tensor_copy(aTs[0:_n, 0:_n], psY[0:_n, 0:_n]))
            P.op("tensor", lambda e, _vi=vi, _n=npos: e.matmul(psY[0:65, 0:_n], vtok[0:_n, _vi, 0:65], aTs[0:_n, 0:_n], start=True, stop=True))
            P.op("vector", lambda e, _o=o, _c0=c0, _n=npos: e.tensor_copy(Ybuf[:, _o + _c0:_o + _c0 + _n], psY[0:65, 0:_n]))
        for l in range(4, -1, -1):
            o, n = _L0_OFF[l], _L0_N[l]
            oc, ncn = _L0_OFF[l + 1], _L0_N[l + 1]
            P.op("vector", lambda e, _o=o, _n=n, _oc=oc, _nc=ncn: e.tensor_tensor(
                out=Ybuf[:, _o:_o + _n].rearrange("p (a two) -> p a two", two=2),
                in0=Ybuf[:, _o:_o + _n].rearrange("p (a two) -> p a two", two=2),
                in1=Ybuf[:, _oc:_oc + _nc].rearrange("p (a o) -> p a o", o=1).broadcast_to([65, _nc, 2]), op=ALU.add))
        for c0 in range(0, 2048, 128):
            P.op("tensor", lambda e, _c0=c0: e.transpose(psY[0:128, 0:65], Ybuf[:, _c0:_c0 + 128], idsb[0:65, 0:65]))
            P.op("vector", lambda e: e.tensor_copy(ytk[:, :], psY[0:128, 0:65]))
            P.op("vector", lambda e: e.tensor_scalar(out=stats[:, 16:17], in0=ytk[:, 64:65], scalar1=1e-8, scalar2=None, op0=ALU.add))
            P.op("vector", lambda e: e.reciprocal(out=stats[:, 16:17], in_=stats[:, 16:17]))
            P.op("vector", lambda e: e.tensor_scalar(out=ytk[:, 0:64], in0=ytk[:, 0:64], scalar1=stats[:, 16:17], scalar2=None, op0=ALU.mult))
            P.dma(ao_d.ap()[c0:c0 + 128, hb:hb + 64], ytk[:, 0:64], engine="gpsimd")

    for t in range(NT):
        P.dma(xr[:, :], ao_d.ap()[128 * t:128 * (t + 1), :])
        P.op("vector", lambda e, _t=t: e.tensor_tensor(out=xr[:, :], in0=xr[:, :], in1=xsb[:, _t, :], op=ALU.add))
        P.dma(xo[128 * t:128 * (t + 1), :], xr[:, :], engine="gpsimd")


def build_l0attn():
    P = Prog()
    IN = {}
    IN["x"] = P.din("x", [TOK, D]).ap()
    IN["Wq"] = P.din("Wq", [D, D])
    IN["Wk"] = P.din("Wk", [D, D])
    IN["Wv"] = P.din("Wv", [D, D])
    IN["bq"] = P.din("bq", [1, D]).ap()
    IN["bk"] = P.din("bk", [1, D]).ap()
    IN["bv"] = P.din("bv", [1, D]).ap()
    IN["ident"] = P.din("ident", [128, 128]).ap()
    IN["mdiag"] = P.din("mdiag", [128, 2, 128]).ap()
    IN["msib"] = P.din("msib", [128, 2, 128]).ap()
    IN["xo"] = P.dout("xo", [TOK, D]).ap()
    emit_l0attn(P, IN)
    return P.finalize()

def _l0_masks():
    i = np.arange(128)
    bi = i // 16
    Md = (bi[:, None] == bi[None, :]).astype(np.float32)
    Ms = (bi[:, None] ^ 1 == bi[None, :]).astype(np.float32)
    out = []
    for M in (Md, Ms):
        neg = (1.0 - M) * np.float32(-1e30)
        out.append(np.stack([M, neg], axis=1).astype(np.float32))
    return out


def run_l0attn(x, Wq, bq, Wk, bk, Wv, bv):
    nc = build_l0attn()
    shards = _shard_tok(x)
    Md, Ms = _l0_masks()
    in_maps = [{"x": s, "Wq": Wq, "Wk": Wk, "Wv": Wv,
                "bq": bq.reshape(1, D), "bk": bk.reshape(1, D), "bv": bv.reshape(1, D),
                "ident": _IDENT, "mdiag": Md, "msib": Ms} for s in shards]
    res = run_bass_kernel_spmd(nc, in_maps, core_ids=list(range(N_CORES)))
    return np.stack([r["xo"].reshape(S, D) for r in res.results])



def emit_l1attn(P, IN):
    DI = 2048
    NC = DI // 128
    x = IN["x"]
    WinT = IN["WinT"]
    convw = IN["convw"]
    WxT = IN["WxT"]
    WdtT = IN["WdtT"]
    bdt = IN["bdt"]
    Alog = IN["Alog"]
    Dp = IN["Dp"]
    WoutT = IN["WoutT"]
    normw = IN["normw"]
    ident = IN["ident"]
    xo = IN["xo"]

    u_d = P.dtmp([DI, TOK], name="ud")
    sz_d = P.dtmp([DI, TOK], name="szd")
    delta_d = P.dtmp([DI, TOK], name="deltad")
    du_d = P.dtmp([DI, TOK], name="dud")
    yg_d = P.dtmp([DI, TOK], name="ygd")
    BT_d = P.dtmp([64, TOK], name="BTd")
    CT_d = P.dtmp([64, TOK], name="CTd")

    KD = D // 128
    idsb = P.sb([128, 128], name="idsb")
    onesb = P.sb([1, 512], name="onesb")
    xtile = P.sb([128, D], name="xtile")
    sq = P.sb([128, D], name="sq")
    stats = P.sb([128, 32], name="stats")
    lnw = P.sb([128, KD, 1], name="lnw")
    wst = P.sb([128, KD, 128], name="wst")
    cwsb = P.sb([128, 8], name="cwsb")
    Btok = P.sb([128, NT, 64], name="Btok")
    Ctok = P.sb([128, NT, 64], name="Ctok")
    BT = P.sb([64, TOK], name="BT")
    CT = P.sb([64, TOK], name="CT")
    A4 = P.sb([128, 4, 64], name="A4")
    Bbc1_t = P.sb([128, 2048], name="Bbc1")
    Cbc1_t = P.sb([128, 2048], name="Cbc1")
    dpv = P.sb([128, NC], name="dpv")
    bdv = P.sb([128, NC], name="bdv")
    U1 = P.sb([128, 16384], name="U1")
    U2 = P.sb([128, 18432], name="U2")
    xrT = U1[:, 0:16384].rearrange("p (c m) -> p c m", m=2048)
    delta4 = U1[:, 0:8192].rearrange("p (c m) -> p c m", m=2048)
    du4 = U1[:, 8192:16384].rearrange("p (c m) -> p c m", m=2048)
    wout = U1[:, 0:16384].rearrange("p (c m) -> p c m", m=1024)
    xmp = U2[:, 0:2052]
    zst = U2[:, 2052:4100]
    WxS = U2[:, 0:3072].rearrange("p (c m) -> p c m", m=192)
    uload = U2[:, 3072:5120].rearrange("p (c m) -> p c m", m=128)
    xdb = U2[:, 5120:5312]
    WdtS = U2[0:64, 5312:7360].rearrange("p (c m) -> p c m", m=128)
    dtT = U2[0:64, 7360:9408]
    dsb = U2[:, 9408:11456]
    sp1 = U2[:, 11456:13504]
    sp2 = U2[:, 13504:15552]
    uch = U2[:, 15552:17600]
    Bbc2 = [U2[:, 0:2048], Bbc1_t[:, :]]
    Cbc2 = [U2[:, 2048:4096], Cbc1_t[:, :]]
    dA2b_t = P.sb([128, 2048], name="dA2b")
    dA2 = [U2[:, 4096:6144], dA2b_t[:, :]]
    tmpb = U2[:, 6144:8192]
    hb = U2[:, 8192:10240]
    y4 = U2[:, 10240:18432].rearrange("p (c m) -> p c m", m=2048)
    psM = P.ps([128, 512], name="psM")
    psT = P.ps([128, 512], name="psT")
    psB2 = P.ps([128, 512], name="psB2")
    psB4 = P.ps([128, 2048], name="psB4")

    P.dma(idsb[:, :], ident)
    P.op("vector", lambda e: e.memset(onesb[:, :], 1.0))
    P.dma(lnw[:, :, 0:1], dview(normw, 0, [[1, 128], [128, KD]]), allow_slow=True)
    P.dma(dpv[:, :], dview(Dp, 0, [[1, 128], [128, NC]]), allow_slow=True)
    P.dma(bdv[:, :], dview(bdt, 0, [[1, 128], [128, NC]]), allow_slow=True)

    for t in range(NT):
        P.dma(xtile[:, :], x[128 * t:128 * (t + 1), :])
        emit_layernorm(P, xtile[:, :], xtile[:, :], sq, stats)
        emit_rmsnorm(P, xtile[:, :], xtile[:, :], sq, stats, 1.1920928955078125e-07, D)
        for g in range(2):
            emit_transpose_batch(P, idsb, [xtile[:, 128 * (4 * g + i):128 * (4 * g + i + 1)] for i in range(4)],
                                 psT, dst_ap_fn=lambda i, _g=g, _t=t: xrT[:, 4 * _g + i, 128 * _t:128 * (_t + 1)])

    for oc in range(2 * NC):
        P.dma(wst[:, :, :], WinT.ap()[:, 128 * oc:128 * (oc + 1)].rearrange("(c p) m -> p c m", p=128))
        for c in range(KD):
            P.op("vector", lambda e, _c=c: e.tensor_scalar(out=wst[:, _c, :], in0=wst[:, _c, :], scalar1=lnw[:, _c, 0:1], scalar2=None, op0=ALU.mult))
        if oc < NC:
            P.op("vector", lambda e: e.memset(xmp[:, 0:3], 0.0))
            for tc in range(4):
                emit_mm_group(P, psM[:, :], [wst[:, c, :] for c in range(KD)], [xrT[:, c, 512 * tc:512 * (tc + 1)] for c in range(KD)])
                P.op("vector", lambda e, _tc=tc: e.tensor_copy(xmp[:, 3 + 512 * _tc:3 + 512 * (_tc + 1)], psM[:, :]))
            P.dma(cwsb[:, 0:5], convw.ap()[128 * oc:128 * (oc + 1), :])
            P.op("vector", lambda e: e.tensor_scalar(out=zst[:, 0:2048], in0=xmp[:, 0:2048], scalar1=cwsb[:, 0:1], scalar2=None, op0=ALU.mult))
            for r in range(1, 4):
                P.op("vector", lambda e, _r=r: e.scalar_tensor_tensor(out=zst[:, 0:2048], in0=xmp[:, _r:_r + 2048], scalar=cwsb[:, _r:_r + 1], in1=zst[:, 0:2048], op0=ALU.mult, op1=ALU.add))
            P.op("vector", lambda e: e.tensor_scalar(out=zst[:, 0:2048], in0=zst[:, 0:2048], scalar1=cwsb[:, 4:5], scalar2=None, op0=ALU.add))
            P.op("scalar", lambda e: e.activation(out=zst[:, 0:2048], in_=zst[:, 0:2048], func=AF.Silu))
            P.dma(u_d.ap()[128 * oc:128 * (oc + 1), :], zst[:, 0:2048], engine="gpsimd")
        else:
            for tc in range(4):
                emit_mm_group(P, psM[:, :], [wst[:, c, :] for c in range(KD)], [xrT[:, c, 512 * tc:512 * (tc + 1)] for c in range(KD)])
                P.op("scalar", lambda e, _tc=tc: e.activation(out=zst[:, 512 * _tc:512 * (_tc + 1)], in_=psM[:, :], func=AF.Silu))
            P.dma(sz_d.ap()[128 * (oc - NC):128 * (oc - NC + 1), :], zst[:, 0:2048], engine="gpsimd")

    P.dma(WxS[:, :, :], WxT.ap().rearrange("(c p) m -> p c m", p=128))
    for t in range(NT):
        P.dma(uload[:, :, :], dview(u_d, 128 * t, [[TOK, 128], [128 * TOK, NC], [1, 128]]))
        emit_mm_group(P, psM[:, 0:192], [uload[:, c, :] for c in range(NC)], [WxS[:, c, :] for c in range(NC)])
        P.op("vector", lambda e: e.tensor_copy(xdb[:, :], psM[:, 0:192]))
        P.op("tensor", lambda e: e.transpose(psT[0:64, 0:128], xdb[:, 0:64], idsb[:, :]))
        P.op("vector", lambda e, _t=t: e.tensor_copy(dtT[:, 128 * _t:128 * (_t + 1)], psT[0:64, 0:128]))
        P.op("vector", lambda e, _t=t: e.tensor_copy(Btok[:, _t, :], xdb[:, 64:128]))
        P.op("vector", lambda e, _t=t: e.tensor_copy(Ctok[:, _t, :], xdb[:, 128:192]))
    for t in range(NT):
        P.op("tensor", lambda e, _t=t: e.transpose(psT[0:64, 0:128], Btok[:, _t, :], idsb[:, :]))
        P.op("vector", lambda e, _t=t: e.tensor_copy(BT[:, 128 * _t:128 * (_t + 1)], psT[0:64, 0:128]))
        P.op("tensor", lambda e, _t=t: e.transpose(psT[0:64, 0:128], Ctok[:, _t, :], idsb[:, :]))
        P.op("vector", lambda e, _t=t: e.tensor_copy(CT[:, 128 * _t:128 * (_t + 1)], psT[0:64, 0:128]))
    P.dma(BT_d.ap()[:, :], BT[:, :], engine="gpsimd")
    P.dma(CT_d.ap()[:, :], CT[:, :], engine="gpsimd")

    P.dma(WdtS[:, :, :], WdtT.ap().rearrange("o (c m) -> o c m", m=128))
    for cc in range(NC):
        for tc in range(4):
            P.op("tensor", lambda e, _cc=cc, _tc=tc: e.matmul(psM[:, :], WdtS[:, _cc, :], dtT[:, 512 * _tc:512 * (_tc + 1)], start=True, stop=True))
            P.op("vector", lambda e, _tc=tc: e.tensor_copy(dsb[:, 512 * _tc:512 * (_tc + 1)], psM[:, :]))
        P.op("vector", lambda e, _cc=cc: e.tensor_scalar(out=dsb[:, :], in0=dsb[:, :], scalar1=bdv[:, _cc:_cc + 1], scalar2=None, op0=ALU.add))
        P.op("scalar", lambda e: e.activation(out=sp1[:, :], in_=dsb[:, :], func=AF.Abs))
        P.op("scalar", lambda e: e.activation(out=sp1[:, :], in_=sp1[:, :], func=AF.Exp, scale=-1.0))
        P.op("vector", lambda e: e.tensor_scalar(out=sp1[:, :], in0=sp1[:, :], scalar1=1.0, scalar2=None, op0=ALU.add))
        P.op("scalar", lambda e: e.activation(out=sp1[:, :], in_=sp1[:, :], func=AF.Ln))
        P.op("vector", lambda e: e.tensor_scalar(out=sp2[:, :], in0=dsb[:, :], scalar1=0.0, scalar2=None, op0=ALU.max))
        P.op("vector", lambda e: e.tensor_tensor(out=dsb[:, :], in0=sp1[:, :], in1=sp2[:, :], op=ALU.add))
        P.dma(delta_d.ap()[128 * cc:128 * (cc + 1), :], dsb[:, :], engine="gpsimd")
        P.dma(uch[:, :], u_d.ap()[128 * cc:128 * (cc + 1), :])
        P.op("vector", lambda e: e.tensor_tensor(out=dsb[:, :], in0=dsb[:, :], in1=uch[:, :], op=ALU.mult))
        P.dma(du_d.ap()[128 * cc:128 * (cc + 1), :], dsb[:, :], engine="gpsimd")

    P.settag("zScan")
    for g4 in range(NC // 4):
        for j in range(4):
            cc = 4 * g4 + j
            P.dma(delta4[:, j, :], delta_d.ap()[128 * cc:128 * (cc + 1), :])
            P.dma(du4[:, j, :], du_d.ap()[128 * cc:128 * (cc + 1), :])
            P.dma(A4[:, j, :], Alog.ap()[128 * cc:128 * (cc + 1), :])
        mbc_rows = [P.mark(), P.mark()]
        P.op("scalar", lambda e: e.activation(out=A4[:, :, :], in_=A4[:, :, :], func=AF.Exp))
        P.op("vector", lambda e: e.tensor_scalar(out=A4[:, :, :], in0=A4[:, :, :], scalar1=-1.0, scalar2=None, op0=ALU.mult))
        _mm = P.mark(); mexp = [_mm, _mm]
        for n in range(64):
            b2 = n % 2
            P.dma(Bbc2[b2], dview(BT_d, n * TOK, [[0, 128], [1, TOK]]), after=mbc_rows[b2], site="mam", pf=4 + b2)
            P.dma(Cbc2[b2], dview(CT_d, n * TOK, [[0, 128], [1, TOK]]), after=mbc_rows[b2], site="mam", pf=4 + b2)
            first = True
            for j in range(4):
                jb = j % 2
                P.op("scalar", lambda e, _j=j, _n=n, _b=jb: e.activation(out=dA2[_b][:, :], in_=delta4[:, _j, :], func=AF.Exp, scale=A4[:, _j, _n:_n + 1]),
                     after=mexp[jb], site="mam", pf=jb)
                if first:
                    P.pfuse("mam", 4 + b2)
                    first = False
                P.op("vector", lambda e, _j=j, _b2=b2: e.tensor_tensor(out=tmpb[:, :], in0=du4[:, _j, :], in1=Bbc2[_b2], op=ALU.mult))
                P.pfuse("mam", jb, unit=1)
                P.op("vector", lambda e, _b=jb: e.tensor_tensor_scan(out=hb[:, :], data0=dA2[_b][:, :], data1=tmpb[:, :], initial=0.0, op0=ALU.mult, op1=ALU.add))
                mexp[jb] = P.mark()
                P.op("vector", lambda e, _b2=b2: e.tensor_tensor(out=hb[:, :], in0=hb[:, :], in1=Cbc2[_b2], op=ALU.mult))
                if n == 0:
                    P.op("vector", lambda e, _j=j: e.tensor_copy(y4[:, _j, :], hb[:, :]))
                else:
                    P.op("vector", lambda e, _j=j: e.tensor_tensor(out=y4[:, _j, :], in0=y4[:, _j, :], in1=hb[:, :], op=ALU.add))
            mbc_rows[b2] = P.mark()
        for j in range(4):
            cc = 4 * g4 + j
            P.dma(dA2[0][:, :], u_d.ap()[128 * cc:128 * (cc + 1), :])
            P.op("vector", lambda e, _j=j, _cc=cc: e.scalar_tensor_tensor(out=y4[:, _j, :], in0=dA2[0][:, :], scalar=dpv[:, _cc:_cc + 1], in1=y4[:, _j, :], op0=ALU.mult, op1=ALU.add))
            P.dma(dA2[0][:, :], sz_d.ap()[128 * cc:128 * (cc + 1), :])
            P.op("vector", lambda e, _j=j: e.tensor_tensor(out=y4[:, _j, :], in0=y4[:, _j, :], in1=dA2[0][:, :], op=ALU.mult))
            P.dma(yg_d.ap()[128 * cc:128 * (cc + 1), :], y4[:, j, :], engine="gpsimd")

    P.settag("zOut")
    P.dma(wout[:, :, :], WoutT.ap().rearrange("(c p) m -> p c m", p=128))
    for t in range(NT):
        P.dma(uload[:, :, :], dview(yg_d, 128 * t, [[TOK, 128], [128 * TOK, NC], [1, 128]]))
        P.dma(xtile[:, :], x[128 * t:128 * (t + 1), :])
        emit_layernorm(P, xtile[:, :], Bbc2[0][:, 0:1024], sq, stats)
        P.op("vector", lambda e: e.tensor_tensor(out=xtile[:, :], in0=xtile[:, :], in1=Bbc2[0][:, 0:1024], op=ALU.add))
        for oh in range(2):
            emit_mm_group(P, psM[:, :], [uload[:, c, :] for c in range(NC)], [wout[:, c, 512 * oh:512 * (oh + 1)] for c in range(NC)])
            P.op("vector", lambda e, _oh=oh: e.tensor_tensor(out=xtile[:, 512 * _oh:512 * (_oh + 1)], in0=xtile[:, 512 * _oh:512 * (_oh + 1)], in1=psM[:, :], op=ALU.add))
        P.dma(xo[128 * t:128 * (t + 1), :], xtile[:, :], engine="gpsimd")


def build_l1attn():
    P = Prog()
    DI = 2048
    IN = {}
    IN["x"] = P.din("x", [TOK, D]).ap()
    IN["WxT"] = P.din("WxT", [DI, 192])
    IN["WdtT"] = P.din("WdtT", [64, DI])
    IN["bdt"] = P.din("bdt", [DI, 1])
    IN["Alog"] = P.din("Alog", [DI, 64])
    IN["Dp"] = P.din("Dp", [DI, 1])
    IN["WoutT"] = P.din("WoutT", [DI, D])
    IN["normw"] = P.din("normw", [1, D])
    IN["ident"] = P.din("ident", [128, 128]).ap()
    IN["xo"] = P.dout("xo", [TOK, D]).ap()
    IN["WinT"] = P.din("WinT", [D, 2 * DI])
    IN["convw"] = P.din("convw", [DI, 5])
    emit_l1attn(P, IN)
    return P.finalize()

def run_l1attn(x, norm_w, W_in, conv_w, conv_b, W_x, W_dt, b_dt, A_log, Dp_, W_out):
    nc = build_l1attn()
    shards = _shard_tok(x)
    WinT = np.ascontiguousarray(W_in.T)
    cw = np.concatenate([conv_w.reshape(2048, 4), conv_b.reshape(2048, 1)], axis=1).astype(np.float32)
    WxT = np.ascontiguousarray(W_x.T)
    WdtT = np.ascontiguousarray(W_dt.T)
    WoutT = np.ascontiguousarray(W_out.T)
    in_maps = [{"x": s, "WinT": WinT, "convw": cw, "WxT": WxT, "WdtT": WdtT,
                "bdt": b_dt.reshape(2048, 1), "Alog": A_log, "Dp": Dp_.reshape(2048, 1),
                "WoutT": WoutT, "normw": norm_w.reshape(1, D), "ident": _IDENT} for s in shards]
    res = run_bass_kernel_spmd(nc, in_maps, core_ids=list(range(N_CORES)))
    return np.stack([r["xo"].reshape(S, D) for r in res.results])



def emit_l2attn(P, IN):
    H = 16
    x = IN["x"]
    wqkvT = IN["wqkvT"]
    wpT = IN["wpT"]
    bproj = IN["bproj"]
    rkeT = IN["rkeT"]
    rqeT = IN["rqeT"]
    rk02 = IN["rk02"]
    rq02 = IN["rq02"]
    rv0in = IN["rv0in"]
    rvmid = IN["rvmid"]
    G1 = IN["G1"].ap()
    G2 = IN["G2"].ap()
    EXP3 = IN["EXP3"].ap()
    ident = IN["ident"]
    xo = IN["xo"]

    xnT_d = P.dtmp([D, TOK], name="xnTd")
    ao_d = P.dtmp([D, TOK], name="aod")
    qst_d = P.dtmp([TOK, 512], name="qstd")
    kst_d = P.dtmp([TOK, 512], name="kstd")
    wnd_d = P.dtmp([128, 512], name="wndd")

    KD = D // 128
    xsb = P.sb([128, NT, D], name="xsb")
    idsb = P.sb([128, 128], name="idsb")
    sq = P.sb([128, D], name="sq")
    stats = P.sb([128, 32], name="stats")
    xr = P.sb([128, D], name="xr")
    xnTl = P.sb([128, KD, 512], name="xnTl")
    wsl = P.sb([128, KD, 192], name="wsl")
    G1sb = P.sb([63, 512], name="G1sb")
    G2sb = P.sb([63, 512], name="G2sb")
    EXP3sb = P.sb([63, 384], name="EXP3sb")
    rtab = P.sb([63, 64], name="rtab")
    r02 = P.sb([64, 4], name="r02")
    rv2 = P.sb([1, 3, 64], name="rv2")
    RKE = P.sb([64, 512], name="RKE")
    RQE = P.sb([64, 512], name="RQE")
    RVx = P.sb([128, 3, 64], name="RVx")
    qaug = P.sb([67, TOK], name="qaug")
    kaug = P.sb([67, TOK], name="kaug")
    vtok = P.sb([128, NT, 64], name="vtok")
    qlc62 = P.sb([128, NT, 1], name="qlc62")
    stg = P.sb([2, TOK], name="stg")
    tm2 = P.sb([128, 2], name="tm2")
    onesr = P.sb([1, 128], name="onesr")
    ones128 = P.sb([128, 1], name="ones128")
    zrow = P.sb([128, 256], name="zrow")
    arow = P.sb([128, TOK], name="arow")
    aT = P.sb([128, TOK], name="aT")
    Pext = P.sb([128, 384], name="Pext")
    gq = P.sb([128, 384], name="gq")
    gqQ = P.sb([128, 2, 384], name="gqQ")
    gqK = P.sb([128, 2, 3, 128], name="gqK")
    s62a = P.sb([1, 128], name="s62a")
    wsuf = P.sb([128, 1], name="wsuf")
    aosb = P.sb([128, 64], name="aosb")
    aosb2 = P.sb([64, 128], name="aosb2")
    wpsb = P.sb([128, KD, D], name="wpsb")
    bpsb = P.sb([1, D], name="bpsb")
    aol = P.sb([128, KD, 128], name="aol")
    psA = P.ps([128, 512], name="psA")
    psS = P.ps([128, 2048], name="psS")
    psT4 = P.ps([128, 512], name="psT4")
    psV = P.ps([128, 64], name="psV")
    psR = P.ps([1, 512], name="psR")

    P.dma(idsb[:, :], ident)
    P.dma(G1sb[:, :], G1)
    P.dma(G2sb[:, :], G2)
    P.dma(EXP3sb[:, :], EXP3)
    P.dma(xsb[:, :, :], x.rearrange("(n p) d -> p n d", p=128))
    P.op("vector", lambda e: e.memset(onesr[:, :], 1.0))
    P.op("vector", lambda e: e.memset(ones128[:, :], 1.0))
    P.op("vector", lambda e: e.memset(zrow[:, :], 0.0))

    for t in range(NT):
        emit_layernorm(P, xsb[:, t, :], xr[:, :], sq, stats)
        for g in range(2):
            emit_transpose_batch(P, idsb, [xr[:, 128 * (4 * g + i):128 * (4 * g + i + 1)] for i in range(4)],
                                 psT4, evac_batch=sq[:, 0:512])
            P.dma(dview(xnT_d, (4 * g * 128) * TOK + 128 * t, [[TOK, 128], [128 * TOK, 4], [1, 128]]),
                  sq[:, 0:512].rearrange("p (c m) -> p c m", m=128), engine="gpsimd")

    for h in range(H):
        P.dma(rtab[:, :], rkeT.ap()[h])
        P.op("tensor", lambda e: e.matmul(psA[0:64, :], rtab[:, :], G1sb[:, :], start=True, stop=True))
        P.op("vector", lambda e: e.tensor_copy(RKE[:, :], psA[0:64, :]))
        P.dma(rtab[:, :], rqeT.ap()[h])
        P.op("tensor", lambda e: e.matmul(psA[0:64, :], rtab[:, :], G2sb[:, :], start=True, stop=True))
        P.op("vector", lambda e: e.tensor_scalar(out=RQE[:, :], in0=psA[0:64, :], scalar1=0.125, scalar2=None, op0=ALU.mult))
        P.dma(r02[:, 0:2], rk02.ap()[h])
        P.dma(r02[:, 2:4], rq02.ap()[h])
        P.op("vector", lambda e: e.tensor_scalar(out=r02[:, 2:4], in0=r02[:, 2:4], scalar1=0.125, scalar2=None, op0=ALU.mult))
        P.dma(rv2[:, 0:2, :].rearrange("a b c -> a (b c)"), rv0in.ap()[h].rearrange("b c -> (b c)").unsqueeze(0))
        P.op("vector", lambda e: e.tensor_tensor(out=rv2[:, 2, :], in0=rv2[:, 1, :], in1=rv2[:, 0, :], op=ALU.subtract))
        P.dma(rtab[:, :], rvmid.ap()[h])
        for uc in range(3):
            P.op("tensor", lambda e, _uc=uc: e.matmul(psA[0:128, 0:64], EXP3sb[:, 128 * _uc:128 * (_uc + 1)], rtab[:, :], start=True, stop=True))
            P.op("vector", lambda e, _uc=uc: e.tensor_copy(RVx[:, _uc, :], psA[0:128, 0:64]))
        P.op("vector", lambda e: e.memset(qaug[64:67, :], 1.0))
        P.op("vector", lambda e: e.memset(kaug[64:67, :], 1.0))
        for t in range(4):
            P.dma(xnTl[:, :, :], dview(xnT_d, 512 * t, [[TOK, 128], [128 * TOK, KD], [1, 512]]))
            if t == 0:
                P.dma(wsl[:, :, 0:64], wqkvT.ap()[:, 64 * h:64 * h + 64].rearrange("(c p) m -> p c m", p=128))
                P.dma(wsl[:, :, 64:128], wqkvT.ap()[:, D + 64 * h:D + 64 * h + 64].rearrange("(c p) m -> p c m", p=128))
                P.dma(wsl[:, :, 128:192], wqkvT.ap()[:, 2 * D + 64 * h:2 * D + 64 * h + 64].rearrange("(c p) m -> p c m", p=128))
            emit_mm_group(P, psA[0:64, :], [wsl[:, c, 0:64] for c in range(KD)], [xnTl[:, c, :] for c in range(KD)])
            P.op("vector", lambda e, _t=t: e.tensor_scalar(out=qaug[0:64, 512 * _t:512 * (_t + 1)], in0=psA[0:64, :], scalar1=0.125, scalar2=None, op0=ALU.mult))
            emit_mm_group(P, psA[0:64, :], [wsl[:, c, 64:128] for c in range(KD)], [xnTl[:, c, :] for c in range(KD)])
            P.op("vector", lambda e, _t=t: e.tensor_copy(kaug[0:64, 512 * _t:512 * (_t + 1)], psA[0:64, :]))
            for tt_ in range(4):
                emit_mm_group(P, psA[:, 0:64], [xnTl[:, c, 128 * tt_:128 * (tt_ + 1)] for c in range(KD)], [wsl[:, c, 128:192] for c in range(KD)])
                P.op("vector", lambda e, _t=t, _tt=tt_: e.tensor_copy(vtok[:, 4 * _t + _tt, :], psA[:, 0:64]))
        for t in range(NT):
            P.op("tensor", lambda e, _t=t: e.matmul(psA[:, 0:2], qaug[0:64, 128 * _t:128 * (_t + 1)], r02[:, 0:2], start=True, stop=True))
            P.op("vector", lambda e: e.tensor_copy(tm2[:, :], psA[:, 0:2]))
            P.op("vector", lambda e: e.tensor_tensor(out=tm2[:, 1:2], in0=tm2[:, 1:2], in1=tm2[:, 0:1], op=ALU.subtract))
            P.op("vector", lambda e, _t=t: e.tensor_copy(qlc62[:, _t, :], tm2[:, 1:2]))
            P.op("tensor", lambda e: e.transpose(psT4[0:2, 0:128], tm2[:, :], idsb[:, :]))
            P.op("vector", lambda e, _t=t: e.tensor_copy(stg[:, 128 * _t:128 * (_t + 1)], psT4[0:2, 0:128]))
        P.dma(qaug[64:65, :], stg[0:1, :], engine="gpsimd")
        for t in range(NT):
            P.op("tensor", lambda e, _t=t: e.matmul(psA[:, 0:2], kaug[0:64, 128 * _t:128 * (_t + 1)], r02[:, 2:4], start=True, stop=True))
            P.op("vector", lambda e: e.tensor_copy(tm2[:, :], psA[:, 0:2]))
            P.op("vector", lambda e: e.tensor_tensor(out=tm2[:, 1:2], in0=tm2[:, 1:2], in1=tm2[:, 0:1], op=ALU.subtract))
            P.op("tensor", lambda e: e.transpose(psT4[0:2, 0:128], tm2[:, :], idsb[:, :]))
            P.op("vector", lambda e, _t=t: e.tensor_copy(stg[:, 128 * _t:128 * (_t + 1)], psT4[0:2, 0:128]))
        P.dma(kaug[65:67, :], stg[:, :], engine="gpsimd")
        for t in range(NT):
            P.op("tensor", lambda e, _t=t: e.matmul(psA[:, :], qaug[0:64, 128 * _t:128 * (_t + 1)], RKE[:, :], start=True, stop=True))
            P.op("vector", lambda e: e.tensor_copy(sq[:, 0:512], psA[:, :]))
            P.dma(qst_d.ap()[128 * t:128 * (t + 1), :], sq[:, 0:512], engine="gpsimd")
        for t in range(NT):
            P.op("tensor", lambda e, _t=t: e.matmul(psA[:, :], kaug[0:64, 128 * _t:128 * (_t + 1)], RQE[:, :], start=True, stop=True))
            P.op("vector", lambda e: e.tensor_copy(sq[:, 0:512], psA[:, :]))
            P.dma(kst_d.ap()[128 * t:128 * (t + 1), :], sq[:, 0:512], engine="gpsimd")

        def _near(itf):
            return [jt for jt in (itf - 1, itf, itf + 1) if 0 <= jt < NT]

        def pf_issue(itf, aft):
            nearf = _near(itf)
            w0f = 128 * nearf[0]
            widthf = 128 * len(nearf)
            nnf = len(nearf)
            P.dma(gqQ[:, itf % 2, 0:widthf],
                  dview(qst_d, (128 * itf) * 512 + w0f - 128 * itf + 256, [[511, 128], [1, widthf]]),
                  after=aft, site="l2a", pf=itf % 2)
            P.dma(gqK[:, itf % 2, 0:nnf, :],
                  dview(kst_d, (128 * nearf[0]) * 511 + 255 + 128 * itf, [[511, 128], [128 * 511, nnf], [1, 128]]),
                  after=aft, site="l2a", pf=itf % 2)

        mqk = P.mark()
        pf_issue(0, mqk)
        pf_issue(1, mqk)
        for it in range(NT):
            near = _near(it)
            slab = list(range(it + 2, NT))

            def qk_fn(e, _it=it, _slab=tuple(slab)):
                inst = None
                for jt in range(NT):
                    kk = 67 if jt in _slab else 66
                    inst = e.matmul(psS[:, 128 * jt:128 * (jt + 1)],
                                    qaug[0:kk, 128 * _it:128 * (_it + 1)],
                                    kaug[0:kk, 128 * jt:128 * (jt + 1)],
                                    start=True, stop=True)
                return inst
            P.op("tensor", qk_fn)
            P.op("vector", lambda e: e.tensor_copy(arow[:, :], psS[:, :]))
            if slab:
                P.op("vector", lambda e, _it=it, _s0=128 * (it + 2): e.tensor_scalar(
                    out=arow[:, _s0:2048], in0=arow[:, _s0:2048], scalar1=qlc62[:, _it, 0:1], scalar2=None, op0=ALU.add))
            w0 = 128 * (near[0])
            wn = 128 * len(near) + 256 if False else 128 * (len(near) + 2)
            width = 128 * len(near) + 0
            P.pfuse("l2a", it % 2)
            P.op("vector", lambda e, _w0=w0, _w=width, _p=it % 2: e.tensor_tensor(out=arow[:, _w0:_w0 + _w], in0=arow[:, _w0:_w0 + _w], in1=gqQ[:, _p, 0:_w], op=ALU.add))
            nn = len(near)

            def ktr(e, _nn=nn, _p=it % 2):
                inst = None
                for ji in range(_nn):
                    inst = e.transpose(psT4[:, 128 * ji:128 * (ji + 1)], gqK[:, _p, ji, :], idsb[:, :])
                return inst
            P.op("tensor", ktr)
            P.op("vector", lambda e, _n0=near[0], _nn=nn: e.tensor_tensor(
                out=arow[:, 128 * _n0:128 * (_n0 + _nn)], in0=arow[:, 128 * _n0:128 * (_n0 + _nn)],
                in1=psT4[:, 0:128 * _nn], op=ALU.add))
            if it + 2 < NT:
                pf_issue(it + 2, P.mark())
            P.op("vector", lambda e: e.tensor_reduce(out=stats[:, 12:13], in_=arow[:, :], axis=AX.X, op=ALU.max))
            P.op("vector", lambda e: e.tensor_scalar(out=stats[:, 13:14], in0=stats[:, 12:13], scalar1=-1.0, scalar2=None, op0=ALU.mult))
            P.op("scalar", lambda e: e.activation(out=arow[:, :], in_=arow[:, :], func=AF.Exp, bias=stats[:, 13:14], accum_out=stats[:, 14:15]))
            P.op("vector", lambda e: e.reciprocal(out=stats[:, 14:15], in_=stats[:, 14:15]))
            P.op("vector", lambda e: e.tensor_scalar(out=arow[:, :], in0=arow[:, :], scalar1=stats[:, 14:15], scalar2=None, op0=ALU.mult))
            cw0 = 0 if it > 0 else 128
            if cw0 > 0:
                P.dma(wnd_d.ap()[:, 0:cw0], zrow[:, 0:cw0], engine="gpsimd")
            P.dma(wnd_d.ap()[:, cw0:cw0 + 128 * len(near)], arow[:, 128 * near[0]:128 * (near[0] + len(near))], engine="gpsimd")
            cend = cw0 + 128 * len(near)
            zhi = 512 if (h == 0 and it == 0) else (384 if it == NT - 1 else cend)
            if cend < zhi:
                P.dma(wnd_d.ap()[:, cend:zhi], zrow[:, 0:zhi - cend], engine="gpsimd")
            P.dma(Pext[:, :], dview(wnd_d, 0, [[513, 128], [1, 384]]))
            for b0 in range(0, NT, 4):
                emit_transpose_batch(P, idsb, [arow[:, 128 * (b0 + i):128 * (b0 + i + 1)] for i in range(4)], psT4,
                                     evac_batch=aT[:, 128 * b0:128 * (b0 + 4)])
            if slab:
                def sl_fn(e, _slab=tuple(slab)):
                    inst = None
                    for i_, jt in enumerate(_slab):
                        inst = e.matmul(psR[0:1, 0:128], ones128[:, :], aT[:, 128 * jt:128 * (jt + 1)],
                                        start=(i_ == 0), stop=(i_ == len(_slab) - 1))
                    return inst
                P.op("tensor", sl_fn)
                P.op("vector", lambda e: e.tensor_copy(s62a[:, :], psR[0:1, 0:128]))
            else:
                P.op("vector", lambda e: e.memset(s62a[:, :], 0.0))
            P.op("vector", lambda e: e.tensor_reduce(out=wsuf[:, :], in_=Pext[:, 159:384], axis=AX.X, op=ALU.add))
            P.op("tensor", lambda e: e.transpose(psR[0:1, 0:128], wsuf[:, :], idsb[:, :]))
            P.op("vector", lambda e: e.tensor_tensor(out=s62a[:, :], in0=s62a[:, :], in1=psR[0:1, 0:128], op=ALU.add))
            emit_transpose_batch(P, idsb, [Pext[:, 128 * i:128 * (i + 1)] for i in range(3)], psT4,
                                 evac_batch=gq[:, 0:384])

            def v_fn(e):
                for jt in range(NT):
                    e.matmul(psV[:, :], aT[:, 128 * jt:128 * (jt + 1)], vtok[:, jt, :], start=(jt == 0), stop=False)
                for uc in range(3):
                    e.matmul(psV[:, :], gq[:, 0:384].rearrange("p (c m) -> p c m", m=128)[:, uc, :] if False else gq[:, 128 * uc:128 * (uc + 1)], RVx[:, uc, :], start=False, stop=False)
                e.matmul(psV[:, :], onesr[:, :], rv2[:, 0, :], start=False, stop=False)
                return e.matmul(psV[:, :], s62a[:, :], rv2[:, 2, :], start=False, stop=True)
            P.op("tensor", v_fn)
            P.op("vector", lambda e: e.tensor_copy(aosb[:, :], psV[:, :]))
            P.op("tensor", lambda e: e.transpose(psT4[0:64, 0:128], aosb[:, :], idsb[:, :]))
            P.op("vector", lambda e: e.tensor_copy(aosb2[0:64, :], psT4[0:64, 0:128]))
            P.dma(ao_d.ap()[64 * h:64 * h + 64, 128 * it:128 * (it + 1)], aosb2[0:64, :], engine="gpsimd")

    P.dma(wpsb[:, :, :], wpT.ap().rearrange("(c p) o -> p c o", p=128))
    P.dma(bpsb[:, :], bproj.ap())
    for t in range(NT):
        P.dma(aol[:, :, :], dview(ao_d, 128 * t, [[TOK, 128], [128 * TOK, KD], [1, 128]]))
        for oh in range(2):
            def pj_fn(e, _oh=oh):
                for c in range(KD):
                    e.matmul(psA[:, :], aol[:, c, :], wpsb[:, c, 512 * _oh:512 * (_oh + 1)], start=(c == 0), stop=False)
                return e.matmul(psA[:, :], onesr[:, :], bpsb[:, 512 * _oh:512 * (_oh + 1)], start=False, stop=True)
            P.op("tensor", pj_fn)
            P.op("vector", lambda e, _t=t, _oh=oh: e.tensor_tensor(out=xsb[:, _t, 512 * _oh:512 * (_oh + 1)], in0=xsb[:, _t, 512 * _oh:512 * (_oh + 1)], in1=psA[:, :], op=ALU.add))
        P.dma(xo[128 * t:128 * (t + 1), :], xsb[:, t, :], engine="gpsimd")


def build_l2attn():
    P = Prog()
    H = 16
    IN = {}
    IN["x"] = P.din("x", [TOK, D]).ap()
    IN["bproj"] = P.din("bproj", [1, D])
    IN["rqeT"] = P.din("rqeT", [H, 63, 64])
    IN["rq02"] = P.din("rq02", [H, 64, 2])
    IN["ident"] = P.din("ident", [128, 128]).ap()
    IN["xo"] = P.dout("xo", [TOK, D]).ap()
    IN["wqkvT"] = P.din("wqkvT", [D, 3 * D])
    IN["wpT"] = P.din("wpT", [D, D])
    IN["rkeT"] = P.din("rkeT", [H, 63, 64])
    IN["rk02"] = P.din("rk02", [H, 64, 2])
    IN["rv0in"] = P.din("rv0in", [H, 2, 64])
    IN["rvmid"] = P.din("rvmid", [H, 63, 64])
    IN["G1"] = P.din("G1", [63, 512])
    IN["G2"] = P.din("G2", [63, 512])
    IN["EXP3"] = P.din("EXP3", [63, 384])
    emit_l2attn(P, IN)
    return P.finalize()

def _l2_consts():
    def g(d):
        return np.clip(d, -31, 31) + 31
    u = np.arange(512)
    G1 = np.zeros((63, 512), np.float32)
    b1 = g(u - 256)
    G1[b1, u] = 1.0
    G1[0, :] -= 1.0
    G2 = np.zeros((63, 512), np.float32)
    b2 = g(u - 255)
    G2[b2, u] = 1.0
    G2[62, :] -= 1.0
    u3 = np.arange(384)
    E3 = np.zeros((63, 384), np.float32)
    d3 = u3 - 128
    b3 = g(d3)
    band = (np.abs(d3) <= 30)
    E3[b3[band], u3[band]] = 1.0
    E3[0, u3[band]] -= 1.0
    return G1, G2, E3


def run_l2attn(x, w_qkv, w_proj, b_proj, rpe_q, rpe_k, rpe_v):
    nc = build_l2attn()
    shards = _shard_tok(x)
    G1, G2, E3 = _l2_consts()
    in_maps = [{"x": s,
                "wqkvT": np.ascontiguousarray(w_qkv.T),
                "wpT": np.ascontiguousarray(w_proj.T),
                "bproj": b_proj.reshape(1, D),
                "rkeT": np.ascontiguousarray(np.swapaxes(rpe_k, 1, 2)),
                "rqeT": np.ascontiguousarray(np.swapaxes(rpe_q, 1, 2)),
                "rk02": np.ascontiguousarray(rpe_k[:, :, [0, 62]]),
                "rq02": np.ascontiguousarray(rpe_q[:, :, [62, 0]]),
                "rv0in": np.ascontiguousarray(rpe_v[:, [0, 62], :]),
                "rvmid": np.ascontiguousarray(rpe_v),
                "G1": G1, "G2": G2, "EXP3": E3, "ident": _IDENT} for s in shards]
    res = run_bass_kernel_spmd(nc, in_maps, core_ids=list(range(N_CORES)))
    return np.stack([r["xo"].reshape(S, D) for r in res.results])


def emit_l0ffn_f1(P, IN):
    R = 2048
    x = IN["x"]
    wqkvT = IN["wqkvT"]
    wprojT = IN["wprojT"]
    routW = IN["routW"]
    g1 = IN["g1"]
    b1 = IN["b1"]
    g2 = IN["g2"]
    b2 = IN["b2"]
    m8 = IN["m8"]
    ident = IN["ident"]
    h1o = IN["h1o"]
    ln2o = IN["ln2o"]
    oh1o = IN["oh1o"]
    oh2o = IN["oh2o"]
    p12o = IN["p12o"]

    KD = D // 128
    NTl = R // 128
    idsb = P.sb([128, 128], name="idsb")
    msb = P.sb([128, 2, 128], name="msb")
    xsb = P.sb([128, NTl, D], name="xsb")
    gbc = P.sb([128, 4, D], name="gbc")
    grow = P.sb([1, 4, D], name="grow")
    sq = P.sb([128, D], name="sq")
    stats = P.sb([128, 32], name="stats")
    ln1 = P.sb([128, D], name="ln1")
    lnT = P.sb([128, KD, 128], name="lnT")
    wsb = P.sb([128, KD, 512], name="wsb")
    qkv = P.sb([128, 1536], name="qkv")
    qfm = P.sb([64, 128], name="qfm")
    kfm = P.sb([64, 128], name="kfm")
    arow2 = P.sb([128, 128], name="arow2")
    aT2 = P.sb([128, 128], name="aT2")
    ao = P.sb([128, NTl, D], name="ao")
    onesr = P.sb([1, 512], name="onesr")
    rw = P.sb([128, KD, 8], name="rw")
    lg = P.sb([128, 8], name="lg")
    psA = P.ps([128, 512], name="psA")
    psT = P.ps([128, 512], name="psT")
    psS2 = P.ps([128, 128], name="psS2")

    P.dma(idsb[:, :], ident)
    P.dma(msb[:, :, :], m8)
    P.op("vector", lambda e: e.memset(onesr[:, :], 1.0))
    for i, gb in enumerate((g1, b1, g2, b2)):
        P.dma(grow[:, i, :], gb.ap())
    for i in range(4):
        for c in range(2):
            P.op("tensor", lambda e, _i=i, _c=c: e.matmul(psA[:, :], onesr[0:1, 0:128], grow[:, _i, 512 * _c:512 * (_c + 1)], start=True, stop=True))
            P.op("vector", lambda e, _i=i, _c=c: e.tensor_copy(gbc[:, _i, 512 * _c:512 * (_c + 1)], psA[:, :]))

    for t in range(NTl):
        P.dma(ln1[:, :], x[128 * t:128 * (t + 1), :])
        emit_layernorm(P, ln1[:, :], xsb[:, t, :], sq, stats)

    for t in range(NTl):
        emit_layernorm(P, xsb[:, t, :], ln1[:, :], sq, stats)
        P.op("vector", lambda e: e.tensor_tensor(out=ln1[:, :], in0=ln1[:, :], in1=gbc[:, 0, :], op=ALU.mult))
        P.op("vector", lambda e: e.tensor_tensor(out=ln1[:, :], in0=ln1[:, :], in1=gbc[:, 1, :], op=ALU.add))
        for g in range(2):
            emit_transpose_batch(P, idsb, [ln1[:, 128 * (4 * g + i):128 * (4 * g + i + 1)] for i in range(4)], psT,
                                 dst_ap_fn=lambda i, _g=g: lnT[:, 4 * _g + i, :])
        for oc in range(3):
            P.dma(wsb[:, :, :], wqkvT.ap()[:, 512 * oc:512 * (oc + 1)].rearrange("(c p) m -> p c m", p=128))
            emit_mm_group(P, psA[:, :], [lnT[:, c, :] for c in range(KD)], [wsb[:, c, :] for c in range(KD)])
            P.op("vector", lambda e, _oc=oc: e.tensor_copy(qkv[:, 512 * _oc:512 * (_oc + 1)], psA[:, :]))
        for g4 in range(4):
            base = 384 * g4
            P.op("tensor", lambda e, _b=base: e.transpose(psT[0:64, 0:128], qkv[:, _b + 256:_b + 320], idsb[:, :]))
            P.op("vector", lambda e: e.tensor_copy(kfm[:, :], psT[0:64, 0:128]))
            for hq in range(4):
                qb = base + 64 * hq
                P.op("tensor", lambda e, _qb=qb: e.transpose(psT[0:64, 0:128], qkv[:, _qb:_qb + 64], idsb[:, :]))
                P.op("vector", lambda e: e.tensor_copy(qfm[:, :], psT[0:64, 0:128]))
                P.op("tensor", lambda e: e.matmul(psS2[:, :], qfm[:, :], kfm[:, :], start=True, stop=True))
                P.op("vector", lambda e: e.tensor_scalar(out=arow2[:, :], in0=psS2[:, :], scalar1=0.125, scalar2=None, op0=ALU.mult))
                P.op("vector", lambda e: e.tensor_tensor(out=arow2[:, :], in0=arow2[:, :], in1=msb[:, 0, :], op=ALU.mult))
                P.op("vector", lambda e: e.tensor_tensor(out=arow2[:, :], in0=arow2[:, :], in1=msb[:, 1, :], op=ALU.add))
                P.op("vector", lambda e: e.tensor_reduce(out=stats[:, 12:13], in_=arow2[:, :], axis=AX.X, op=ALU.max))
                P.op("vector", lambda e: e.tensor_scalar(out=stats[:, 13:14], in0=stats[:, 12:13], scalar1=-1.0, scalar2=None, op0=ALU.mult))
                P.op("scalar", lambda e: e.activation(out=arow2[:, :], in_=arow2[:, :], func=AF.Exp, bias=stats[:, 13:14], accum_out=stats[:, 14:15]))
                P.op("vector", lambda e: e.reciprocal(out=stats[:, 14:15], in_=stats[:, 14:15]))
                P.op("vector", lambda e: e.tensor_scalar(out=arow2[:, :], in0=arow2[:, :], scalar1=stats[:, 14:15], scalar2=None, op0=ALU.mult))
                P.op("tensor", lambda e: e.transpose(psS2[:, :], arow2[:, :], idsb[:, :]))
                P.op("vector", lambda e: e.tensor_copy(aT2[:, :], psS2[:, :]))
                P.op("tensor", lambda e, _b=base: e.matmul(psS2[:, 0:64], aT2[:, :], qkv[:, _b + 320:_b + 384], start=True, stop=True))
                P.op("vector", lambda e, _t=t, _g=g4, _h=hq: e.tensor_copy(ao[:, _t, 256 * _g + 64 * _h:256 * _g + 64 * _h + 64], psS2[:, 0:64]))

    for t in range(NTl):
        for g in range(2):
            emit_transpose_batch(P, idsb, [ao[:, t, 128 * (4 * g + i):128 * (4 * g + i + 1)] for i in range(4)], psT,
                                 dst_ap_fn=lambda i, _g=g: lnT[:, 4 * _g + i, :])
        for oc in range(2):
            P.dma(wsb[:, :, :], wprojT.ap()[:, 512 * oc:512 * (oc + 1)].rearrange("(c p) m -> p c m", p=128))
            emit_mm_group(P, psA[:, :], [lnT[:, c, :] for c in range(KD)], [wsb[:, c, :] for c in range(KD)])
            P.op("vector", lambda e, _t=t, _oc=oc: e.tensor_tensor(out=xsb[:, _t, 512 * _oc:512 * (_oc + 1)], in0=xsb[:, _t, 512 * _oc:512 * (_oc + 1)], in1=psA[:, :], op=ALU.add))
        P.dma(h1o[128 * t:128 * (t + 1), :], xsb[:, t, :], engine="gpsimd")
        emit_layernorm(P, xsb[:, t, :], ln1[:, :], sq, stats)
        P.op("vector", lambda e: e.tensor_tensor(out=ln1[:, :], in0=ln1[:, :], in1=gbc[:, 2, :], op=ALU.mult))
        P.op("vector", lambda e: e.tensor_tensor(out=ln1[:, :], in0=ln1[:, :], in1=gbc[:, 3, :], op=ALU.add))
        P.dma(ln2o[128 * t:128 * (t + 1), :], ln1[:, :], engine="gpsimd")
        for g in range(2):
            emit_transpose_batch(P, idsb, [ln1[:, 128 * (4 * g + i):128 * (4 * g + i + 1)] for i in range(4)], psT,
                                 dst_ap_fn=lambda i, _g=g: lnT[:, 4 * _g + i, :])
        if t == 0:
            P.dma(rw[:, :, :], routW.ap().rearrange("(c p) m -> p c m", p=128))
        emit_mm_group(P, psA[:, 0:8], [lnT[:, c, :] for c in range(KD)], [rw[:, c, :] for c in range(KD)])
        P.op("vector", lambda e: e.tensor_copy(lg[:, :], psA[:, 0:8]))
        P.op("vector", lambda e: e.tensor_reduce(out=stats[:, 16:17], in_=lg[:, :], axis=AX.X, op=ALU.max))
        P.op("vector", lambda e: e.tensor_scalar(out=sq[:, 0:8], in0=lg[:, :], scalar1=stats[:, 16:17], scalar2=None, op0=ALU.is_equal))
        P.dma(oh1o[128 * t:128 * (t + 1), :], sq[:, 0:8], engine="gpsimd")
        P.op("vector", lambda e: e.scalar_tensor_tensor(out=sq[:, 8:16], in0=sq[:, 0:8], scalar=-1e30, in1=lg[:, :], op0=ALU.mult, op1=ALU.add))
        P.op("vector", lambda e: e.tensor_reduce(out=stats[:, 17:18], in_=sq[:, 8:16], axis=AX.X, op=ALU.max))
        P.op("vector", lambda e: e.tensor_scalar(out=sq[:, 16:24], in0=sq[:, 8:16], scalar1=stats[:, 17:18], scalar2=None, op0=ALU.is_equal))
        P.dma(oh2o[128 * t:128 * (t + 1), :], sq[:, 16:24], engine="gpsimd")
        P.op("vector", lambda e: e.tensor_tensor(out=stats[:, 18:19], in0=stats[:, 16:17], in1=stats[:, 17:18], op=ALU.subtract))
        P.op("scalar", lambda e: e.activation(out=stats[:, 20:21], in_=stats[:, 18:19], func=AF.Sigmoid))
        P.op("vector", lambda e: e.tensor_scalar(out=stats[:, 19:20], in0=stats[:, 18:19], scalar1=-1.0, scalar2=None, op0=ALU.mult))
        P.op("scalar", lambda e: e.activation(out=stats[:, 21:22], in_=stats[:, 19:20], func=AF.Sigmoid))
        P.dma(p12o[128 * t:128 * (t + 1), :], stats[:, 20:22], engine="gpsimd")


def build_l0ffn_f1():
    P = Prog()
    R = 2048
    IN = {}
    IN["x"] = P.din("x", [R, D]).ap()
    IN["wqkvT"] = P.din("wqkvT", [D, 1536])
    IN["wprojT"] = P.din("wprojT", [D, D])
    IN["routW"] = P.din("routW", [D, 8])
    IN["g1"] = P.din("g1", [1, D])
    IN["b1"] = P.din("b1", [1, D])
    IN["g2"] = P.din("g2", [1, D])
    IN["b2"] = P.din("b2", [1, D])
    IN["m8"] = P.din("m8", [128, 2, 128]).ap()
    IN["ident"] = P.din("ident", [128, 128]).ap()
    IN["h1o"] = P.dout("h1o", [R, D]).ap()
    IN["ln2o"] = P.dout("ln2o", [R, D]).ap()
    IN["oh1o"] = P.dout("oh1o", [R, 8]).ap()
    IN["oh2o"] = P.dout("oh2o", [R, 8]).ap()
    IN["p12o"] = P.dout("p12o", [R, 2]).ap()
    emit_l0ffn_f1(P, IN)
    return P.finalize()

def emit_l0ffn_f3(P, IN):
    NA = 32768
    CAP = 5120
    KD = D // 128
    ln2 = IN["ln2"]
    ass = IN["ass"]
    ohsel = IN["ohsel"]
    pa = IN["pa"]
    tids = IN["tids"]
    esel = IN["esel"]
    w1 = IN["w1"]
    w2 = IN["w2"]
    tris = IN["tris"]
    ident = IN["ident"]
    moeo = IN["moeo"]
    slot_d = P.dtmp([CAP + 128, 2], name="slotd")

    idsb = P.sb([128, 128], name="idsb")
    trisb = P.sb([128, 128], name="trisb")
    ones1x = P.sb([1, 128], name="ones1x")
    ones128c = P.sb([128, 1], name="ones128c")
    eselbc = P.sb([128, 8], name="eselbc")
    eselr = P.sb([1, 8], name="eselr")
    runr = P.sb([1, 8], name="runr")
    assl = P.sb([128, 16, 8], name="assl")
    ohl = P.sb([128, 16], name="ohl")
    pal = P.sb([128, 16], name="pal")
    tidl = P.sb([128, 16], name="tidl")
    tmp8 = P.sb([128, 8], name="tmp8")
    wtile = P.sb([128, 16], name="wtile")
    kvt = P.sb([128, 16], name="kvt")
    sc2 = P.sb([128, 2], name="sc2")
    sci = P.sb([128, 1], I32, name="sci")
    sci16 = P.sb([128, 16], I32, name="sci16")
    w1sb = P.sb([128, KD, 2048], name="w1sb")
    w2sb = P.sb([128, 16, 1024], name="w2sb")
    gath = P.sb([128, D], name="gath")
    gT = P.sb([128, 8, 128], name="gT")
    hfm = P.sb([128, 16, 128], name="hfm")
    outr = P.sb([128, D], name="outr")
    zrow2 = P.sb([128, D], name="zrow2")
    psA = P.ps([128, 512], name="psA")
    psW = P.ps([128, 8], name="psW")
    psR = P.ps([1, 8], name="psR")
    psT = P.ps([128, 512], name="psT")

    P.dma(idsb[:, :], ident)
    P.dma(trisb[:, :], tris)
    P.dma(eselr[:, :], esel)
    P.op("vector", lambda e: e.memset(ones1x[:, :], 1.0))
    P.op("vector", lambda e: e.memset(ones128c[:, :], 1.0))
    P.op("vector", lambda e: e.memset(runr[:, :], 0.0))
    P.op("vector", lambda e: e.memset(zrow2[:, :], 0.0))
    P.op("tensor", lambda e: e.matmul(psW[:, :], ones1x[:, :], eselr[:, :], start=True, stop=True))
    P.op("vector", lambda e: e.tensor_copy(eselbc[:, :], psW[:, :]))

    for t in range(B * S // 128):
        P.dma(moeo[128 * t:128 * (t + 1), :], zrow2[:, :], engine="gpsimd")
    for b0 in range(0, CAP + 128, 128):
        n = min(128, CAP + 128 - b0)
        P.dma(slot_d.ap()[b0:b0 + n, :], zrow2[0:n, 0:2], engine="gpsimd")
    pcol = P.sb([128, 1], name="pcol")
    P.op("gpsimd", lambda e: e.iota(out=sci[:, :], pattern=[[0, 1]], base=CAP, channel_multiplier=1))
    P.op("vector", lambda e: e.tensor_copy(pcol[:, :], sci[:, :]))

    for big in range(16):
        P.dma(assl[:, :, :], ass[2048 * big:2048 * (big + 1), :].rearrange("(c p) e -> p c e", p=128))
        P.dma(ohl[:, :], ohsel[big])
        P.dma(pal[:, :], pa[big])
        P.dma(tidl[:, :], tids[big])
        for j in range(16):
            def pf(e, _j=j):
                e.matmul(psW[:, :], trisb[:, :], assl[:, _j, :], start=True, stop=False)
                return e.matmul(psW[:, :], ones1x[:, :], runr[:, :], start=False, stop=True)
            P.op("tensor", pf)
            P.op("vector", lambda e: e.tensor_tensor(out=tmp8[:, :], in0=psW[:, :], in1=eselbc[:, :], op=ALU.mult))
            P.op("vector", lambda e, _j=j: e.tensor_reduce(out=wtile[:, _j:_j + 1], in_=tmp8[:, :], axis=AX.X, op=ALU.add))
            P.op("tensor", lambda e, _j=j: e.matmul(psR[:, :], ones128c[:, :], assl[:, _j, :], start=True, stop=True))
            P.op("vector", lambda e: e.tensor_tensor(out=runr[:, :], in0=runr[:, :], in1=psR[:, :], op=ALU.add))
        P.op("vector", lambda e: e.tensor_scalar(out=kvt[:, :], in0=wtile[:, :], scalar1=float(CAP), scalar2=None, op0=ALU.is_lt))
        P.op("vector", lambda e: e.tensor_tensor(out=kvt[:, :], in0=kvt[:, :], in1=ohl[:, :], op=ALU.mult))
        P.op("vector", lambda e: e.scalar_tensor_tensor(out=wtile[:, :], in0=wtile[:, :], scalar=pcol[:, :], in1=kvt[:, :], op0=ALU.subtract, op1=ALU.mult))
        P.op("vector", lambda e: e.tensor_scalar(out=wtile[:, :], in0=wtile[:, :], scalar1=pcol[:, :], scalar2=None, op0=ALU.add))
        P.op("vector", lambda e: e.tensor_copy(sci16[:, :], wtile[:, :]))
        for j in range(16):
            P.op("gpsimd", lambda e, _j=j: e.indirect_dma_start(
                out=slot_d.ap()[:, :], out_offset=bass.IndirectOffsetOnAxis(ap=sci16[0:128, _j:_j + 1], axis=0),
                in_=metl[:, _j, 8:10], in_offset=None), weight=16)

    P.dma(w1sb[:, :, :], w1.ap().rearrange("(c p) m -> p c m", p=128))
    P.dma(w2sb[:, :, :], w2.ap().rearrange("(c p) m -> p c m", p=128))
    GC = 0.7978845608028654
    for ch in range(CAP // 128):
        P.dma(sc2[:, :], slot_d.ap()[128 * ch:128 * (ch + 1), :])
        P.op("vector", lambda e: e.tensor_copy(sci[:, :], sc2[:, 0:1]))
        P.op("gpsimd", lambda e: e.indirect_dma_start(
            out=gath[:, :], out_offset=None, in_=ln2.ap(),
            in_offset=bass.IndirectOffsetOnAxis(ap=sci[0:128, 0:1], axis=0)), weight=16)
        for g in range(2):
            emit_transpose_batch(P, idsb, [gath[:, 128 * (4 * g + i):128 * (4 * g + i + 1)] for i in range(4)], psT,
                                 dst_ap_fn=lambda i, _g=g: gT[:, 4 * _g + i, :])
        for fc in range(16):
            emit_mm_group(P, psA[:, 0:128], [w1sb[:, c, 128 * fc:128 * (fc + 1)] for c in range(KD)], [gT[:, c, :] for c in range(KD)])
            P.op("scalar", lambda e: e.activation(out=gsc[:, 0, :], in_=psA[:, 0:128], func=AF.Square))
            P.op("vector", lambda e: e.tensor_scalar(out=gsc[:, 0, :], in0=gsc[:, 0, :], scalar1=0.044715, scalar2=1.0, op0=ALU.mult, op1=ALU.add))
            P.op("vector", lambda e: e.tensor_tensor(out=gsc[:, 0, :], in0=gsc[:, 0, :], in1=psA[:, 0:128], op=ALU.mult))
            P.op("scalar", lambda e: e.activation(out=gsc[:, 1, :], in_=gsc[:, 0, :], func=AF.Tanh, scale=GC))
            P.op("vector", lambda e: e.tensor_scalar(out=gsc[:, 2, :], in0=psA[:, 0:128], scalar1=0.5, scalar2=None, op0=ALU.mult))
            P.op("vector", lambda e, _fc=fc: e.scalar_tensor_tensor(out=hfm[:, _fc, :], in0=gsc[:, 1, :], scalar=1.0, in1=gsc[:, 2, :], op0=ALU.add, op1=ALU.mult))
        for oh in range(2):
            emit_mm_group(P, psA[:, :], [hfm[:, c, :] for c in range(16)], [w2sb[:, c, 512 * oh:512 * (oh + 1)] for c in range(16)])
            P.op("vector", lambda e, _oh=oh: e.tensor_scalar(out=outr[:, 512 * _oh:512 * (_oh + 1)], in0=psA[:, :], scalar1=sc2[:, 1:2], scalar2=None, op0=ALU.mult))
        P.op("gpsimd", lambda e: e.indirect_dma_start(
            out=moeo, out_offset=bass.IndirectOffsetOnAxis(ap=sci[0:128, 0:1], axis=0),
            in_=outr[:, :], in_offset=None, compute_op=ALU.add), weight=16)


def build_l0ffn_f3():
    P = Prog()
    NA = 32768
    IN = {}
    IN["ln2"] = P.din("ln2", [B * S, D])
    IN["ass"] = P.din("ass", [NA, 8]).ap()
    IN["ohsel"] = P.din("ohsel", [16, 128, 16]).ap()
    IN["pa"] = P.din("pa", [16, 128, 16]).ap()
    IN["tids"] = P.din("tids", [16, 128, 16]).ap()
    IN["esel"] = P.din("esel", [1, 8]).ap()
    IN["w1"] = P.din("w1", [D, 2048])
    IN["w2"] = P.din("w2", [2048, D])
    IN["tris"] = P.din("tris", [128, 128]).ap()
    IN["ident"] = P.din("ident", [128, 128]).ap()
    IN["moeo"] = P.dout("moeo", [B * S, D]).ap()
    emit_l0ffn_f3(P, IN)
    return P.finalize()

def emit_l0ffn_f4(P, IN):
    x = IN["x"]
    h1 = IN["h1"]
    moes = [P.din(f"moe{e}", [TOK, D]).ap() for e in range(8)]
    xo = IN["xo"]
    a = P.sb([128, D], name="a")
    bsb = P.sb([128, D], name="bsb")
    for t in range(NT):
        P.dma(a[:, :], h1[128 * t:128 * (t + 1), :])
        for e8 in range(8):
            P.dma(bsb[:, :], moes[e8][128 * t:128 * (t + 1), :])
            P.op("vector", lambda e: e.tensor_tensor(out=a[:, :], in0=a[:, :], in1=bsb[:, :], op=ALU.add))
        P.dma(bsb[:, :], x[128 * t:128 * (t + 1), :])
        P.op("vector", lambda e: e.scalar_tensor_tensor(out=a[:, :], in0=a[:, :], scalar=0.5, in1=bsb[:, :], op0=ALU.mult, op1=ALU.add))
        P.dma(xo[128 * t:128 * (t + 1), :], a[:, :], engine="gpsimd")


def build_l0ffn_f4():
    P = Prog()
    IN = {}
    IN["x"] = P.din("x", [TOK, D]).ap()
    IN["h1"] = P.din("h1", [TOK, D]).ap()
    IN["xo"] = P.dout("xo", [TOK, D]).ap()
    emit_l0ffn_f4(P, IN)
    return P.finalize()

def _l0ffn_consts():
    r = np.arange(128)
    blk = r // 8
    s_ = r % 8
    keep = ((blk[:, None] == blk[None, :]) & (s_[None, :] <= s_[:, None])).astype(np.float32)
    neg = (1.0 - keep) * np.float32(-1e30)
    m8 = np.stack([keep, neg], axis=1).astype(np.float32)
    tris = (np.arange(128)[:, None] < np.arange(128)[None, :]).astype(np.float32)
    return m8, tris


def run_l0ffn(x, ln1w, ln1b, ln2w, ln2b, qkv_w, proj_w, rout_w, moe_w1, moe_w2):
    m8, tris = _l0ffn_consts()
    nc1 = build_l0ffn_f1()
    xs = []
    for c in range(N_CORES):
        sl = x[:, 256 * c:256 * (c + 1), :]
        xs.append(np.ascontiguousarray(np.transpose(sl, (1, 0, 2)).reshape(2048, D)))
    common1 = {"wqkvT": np.ascontiguousarray(qkv_w.T), "wprojT": np.ascontiguousarray(proj_w.T),
               "routW": np.ascontiguousarray(rout_w), "g1": ln1w.reshape(1, D), "b1": ln1b.reshape(1, D),
               "g2": ln2w.reshape(1, D), "b2": ln2b.reshape(1, D), "m8": m8, "ident": _IDENT}
    in1 = [{**common1, "x": s} for s in xs]
    r1 = run_bass_kernel_spmd(nc1, in1, core_ids=list(range(N_CORES)))

    def glob(key, width):
        parts = []
        for c in range(N_CORES):
            a = r1.results[c][key].reshape(256, 8, width)
            parts.append(np.transpose(a, (1, 0, 2)))
        return np.ascontiguousarray(np.concatenate(parts, axis=1).reshape(B * S, width))

    h1g = glob("h1o", D)
    ln2g = glob("ln2o", D)
    oh1 = glob("oh1o", 8)
    oh2 = glob("oh2o", 8)
    p12 = glob("p12o", 2)
    ass = np.ascontiguousarray(np.stack([oh1, oh2], axis=1).reshape(2 * B * S, 8))
    pa = np.ascontiguousarray(p12.reshape(2 * B * S, 1))
    tids = np.ascontiguousarray((np.arange(2 * B * S) // 2).astype(np.float32).reshape(-1, 1))
    nc3 = build_l0ffn_f3()
    in3 = []
    for e in range(N_CORES):
        esel = np.zeros((1, 8), np.float32); esel[0, e] = 1.0
        def _blk(v):
            return np.ascontiguousarray(v.reshape(16, 16, 128).transpose(0, 2, 1))
        in3.append({"ln2": ln2g, "ass": ass, "ohsel": _blk(ass[:, e]),
                    "pa": _blk(pa[:, 0]), "tids": _blk(tids[:, 0]), "esel": esel,
                    "w1": np.ascontiguousarray(moe_w1[e]), "w2": np.ascontiguousarray(moe_w2[e]),
                    "tris": tris, "ident": _IDENT})
    r3 = run_bass_kernel_spmd(nc3, in3, core_ids=list(range(N_CORES)))
    moes = [r3.results[e]["moeo"] for e in range(N_CORES)]
    nc4 = build_l0ffn_f4()
    in4 = []
    for c in range(N_CORES):
        d = {"x": np.ascontiguousarray(x[c].reshape(TOK, D)),
             "h1": np.ascontiguousarray(h1g.reshape(B, S, D)[c])}
        for e in range(8):
            d[f"moe{e}"] = np.ascontiguousarray(moes[e].reshape(B, S, D)[c])
        in4.append(d)
    r4 = run_bass_kernel_spmd(nc4, in4, core_ids=list(range(N_CORES)))
    return np.stack([r["xo"].reshape(S, D) for r in r4.results])




def _npdt(arr):
    if arr.dtype == np.float32:
        return F32
    if arr.dtype == np.int32:
        return I32
    if arr.dtype == np.uint8:
        return U8
    raise ValueError(arr.dtype)


def run_l123_fused(x, I):
    P = Prog()
    host = {}

    def din(name, arr):
        h = P.din(name, list(arr.shape), _npdt(arr))
        host[name] = arr
        return h

    identh = din("c_ident", _IDENT)
    l1f = {
        "k0": din("l1f_k0", np.ascontiguousarray(I["L1_ffn_k0"]).view(np.uint8)),
        "k1": din("l1f_k1", np.ascontiguousarray(I["L1_ffn_k1"]).view(np.uint8)),
        "k2": din("l1f_k2", np.ascontiguousarray(I["L1_ffn_k2"]).view(np.uint8)),
        "s0": din("l1f_s0", np.float32(I["L1_ffn_s0"]).reshape(1, 1)),
        "s1": din("l1f_s1", np.float32(I["L1_ffn_s1"]).reshape(1, 1)),
        "s2": din("l1f_s2", np.float32(I["L1_ffn_s2"]).reshape(1, 1)),
        "ident": identh,
    }
    cw = np.concatenate([I["L1_attn_conv_w"].reshape(2048, 4), I["L1_attn_conv_b"].reshape(2048, 1)], axis=1).astype(np.float32)
    l1a = {
        "WinT": din("l1a_WinT", np.ascontiguousarray(I["L1_attn_W_in"].T)),
        "convw": din("l1a_convw", cw),
        "WxT": din("l1a_WxT", np.ascontiguousarray(I["L1_attn_W_x"].T)),
        "WdtT": din("l1a_WdtT", np.ascontiguousarray(I["L1_attn_W_dt"].T)),
        "bdt": din("l1a_bdt", np.ascontiguousarray(I["L1_attn_b_dt"].reshape(2048, 1))),
        "Alog": din("l1a_Alog", np.ascontiguousarray(I["L1_attn_A_log"])),
        "Dp": din("l1a_Dp", np.ascontiguousarray(I["L1_attn_Dp"].reshape(2048, 1))),
        "WoutT": din("l1a_WoutT", np.ascontiguousarray(I["L1_attn_W_out"].T)),
        "normw": din("l1a_normw", np.ascontiguousarray(I["L1_attn_norm_w"].reshape(1, D))),
        "ident": identh,
    }
    l2f = {
        "wg": din("l2f_wg", np.ascontiguousarray(I["L2_ffn_w_gate"])),
        "wu": din("l2f_wu", np.ascontiguousarray(I["L2_ffn_w_up"])),
        "wd": din("l2f_wd", np.ascontiguousarray(I["L2_ffn_w_down"])),
        "ident": identh,
    }
    G1, G2, E3 = _l2_consts()
    l2a = {
        "wqkvT": din("l2a_wqkvT", np.ascontiguousarray(I["L2_attn_w_qkv"].T)),
        "wpT": din("l2a_wpT", np.ascontiguousarray(I["L2_attn_w_proj"].T)),
        "bproj": din("l2a_bproj", np.ascontiguousarray(I["L2_attn_b_proj"].reshape(1, D))),
        "rkeT": din("l2a_rkeT", np.ascontiguousarray(np.swapaxes(I["L2_attn_rpe_k_table"], 1, 2))),
        "rqeT": din("l2a_rqeT", np.ascontiguousarray(np.swapaxes(I["L2_attn_rpe_q_table"], 1, 2))),
        "rk02": din("l2a_rk02", np.ascontiguousarray(I["L2_attn_rpe_k_table"][:, :, [0, 62]])),
        "rq02": din("l2a_rq02", np.ascontiguousarray(I["L2_attn_rpe_q_table"][:, :, [62, 0]])),
        "rv0in": din("l2a_rv0in", np.ascontiguousarray(I["L2_attn_rpe_v_table"][:, [0, 62], :])),
        "rvmid": din("l2a_rvmid", np.ascontiguousarray(I["L2_attn_rpe_v_table"])),
        "G1": din("l2a_G1", G1), "G2": din("l2a_G2", G2), "EXP3": din("l2a_E3", E3),
        "ident": identh,
    }
    l3f = {
        "wgT": din("l3f_wgT", np.ascontiguousarray(I["L3_ffn_w_gate"].T)),
        "wuT": din("l3f_wuT", np.ascontiguousarray(I["L3_ffn_w_up"].T)),
        "wdT": din("l3f_wdT", np.ascontiguousarray(I["L3_ffn_w_down"].T)),
        "ident": identh,
    }
    l3a = {
        "Wq": din("l3a_Wq", np.ascontiguousarray(I["L3_attn_Wq"])),
        "Wk": din("l3a_Wk", np.ascontiguousarray(I["L3_attn_Wk"])),
        "Wv": din("l3a_Wv", np.ascontiguousarray(I["L3_attn_Wv"])),
        "Wo": din("l3a_Wo", np.ascontiguousarray(I["L3_attn_Wo"])),
        "W1": din("l3a_W1", np.ascontiguousarray(I["L3_attn_W1"])),
        "W2": din("l3a_W2", np.ascontiguousarray(I["L3_attn_W2"])),
        "ln1w": din("l3a_ln1w", np.ascontiguousarray(I["L3_attn_ln1_w"].reshape(1, D))),
        "ln2w": din("l3a_ln2w", np.ascontiguousarray(I["L3_attn_ln2_w"].reshape(1, D))),
        "relb": din("l3a_relb", np.ascontiguousarray(I["L3_attn_rel_bias"])),
        "scale_h": din("l3a_scale", np.ascontiguousarray(I["L3_attn_scale"].reshape(1, 16))),
        "ET": din("l3a_ET", _build_ET()),
        "ranti": din("l3a_ranti", _RANTI),
        "ident": identh,
    }
    l3a["relb"] = l3a["relb"].ap()
    l3a["ET"] = l3a["ET"].ap()
    l3a["ranti"] = l3a["ranti"].ap()

    x_in = P.din("xin", [TOK, D])
    xo = P.dout("xof", [TOK, D])
    chain = [x_in] + [P.dtmp([TOK, D], name=f"chain{i}") for i in range(8)] + [xo]
    seq = [
        (emit_l1ffn, l1f), (emit_l1attn, l1a), (emit_l1ffn, l1f),
        (emit_l2ffn, l2f), (emit_l2attn, l2a), (emit_l2ffn, l2f),
        (emit_l3ffn, l3f), (emit_l3attn, l3a), (emit_l3ffn, l3f),
    ]
    for i, (fn, ins) in enumerate(seq):
        P.scope_open(f"s{i}_")
        IN = dict(ins)
        IN["ident"] = identh.ap()
        IN["x"] = chain[i].ap()
        IN["xo"] = chain[i + 1].ap()
        fn(P, IN)
        P.scope_close()
    nc = P.finalize()
    shards = _shard_tok(x)
    in_maps = [{**host, "xin": s} for s in shards]
    res = run_bass_kernel_spmd(nc, in_maps, core_ids=list(range(N_CORES)))
    return np.stack([r["xof"].reshape(S, D) for r in res.results])



def emit_l0ffn_fused(P, IN):
    R = TOK
    KD = D // 128
    NTl = NT
    CAP = 5120
    x = IN["x"]
    xo = IN["xo"]
    ident = IN["ident"]
    wqkvT, wprojT, routW = IN["wqkvT"], IN["wprojT"], IN["routW"]
    g1, b1, g2, b2 = IN["g1"], IN["b1"], IN["g2"], IN["b2"]
    tris, esel, mrow, offs = IN["tris"], IN["esel"], IN["mrow"], IN["offs"]
    w1, w2 = IN["w1"], IN["w2"]
    ccin, ccout = IN["ccin"], IN["ccout"]

    qkv_d = P.dtmp([R, 1536], name="qkvd")
    ao_d = P.dtmp([R, D], name="aod")
    h1_d = P.dtmp([R, D], name="h1d")
    meta_d = P.dtmp([4096, 12], name="metad")
    slot_d = P.dtmp([CAP + 128, 2], name="slotd")

    idsb = P.sb([128, 128], name="idsb")
    trisb = P.sb([128, 128], name="trisb")
    onesr = P.sb([1, 512], name="onesr")
    ones128c = P.sb([128, 1], name="ones128c")
    gbc = P.sb([128, 4, D], name="gbc")
    grow = P.sb([1, 4, D], name="grow")
    sq = P.sb([128, D], name="sq")
    stats = P.sb([128, 32], name="stats")
    ln1 = P.sb([128, D], name="ln1")
    lnT = P.sb([128, KD, 128], name="lnT")
    zrow = P.sb([128, D], name="zrow")
    mbc = P.sb([128, 8], name="mbc")
    offb = P.sb([128, 4], name="offb")
    iob = P.sb([128, 1], I32, name="iob")
    idxf = P.sb([128, 1], name="idxf")
    idxi = P.sb([128, 1], I32, name="idxi")
    eselbc = P.sb([128, 8], name="eselbc")
    eselr = P.sb([1, 8], name="eselr")
    runr = P.sb([1, 8], name="runr")
    psA = P.ps([128, 512], name="psA")
    psT = P.ps([128, 512], name="psT")
    psW = P.ps([128, 8], name="psW")
    psR = P.ps([1, 8], name="psR")

    cc512_in = dview(ccin, 0, [[512, 16384], [1, 512]])
    cc512_out = dview(ccout, 0, [[512, 16384], [1, 512]])

    P.dma(idsb[:, :], ident)
    P.dma(trisb[:, :], tris.ap())
    P.dma(eselr[:, :], esel.ap())
    P.dma(mbc[:, :], dview(mrow, 0, [[0, 128], [1, 8]]))
    P.dma(offb[:, :], dview(offs, 0, [[0, 128], [1, 4]]))
    P.op("vector", lambda e: e.memset(onesr[:, :], 1.0))
    P.op("vector", lambda e: e.memset(ones128c[:, :], 1.0))
    P.op("vector", lambda e: e.memset(runr[:, :], 0.0))
    P.op("vector", lambda e: e.memset(zrow[:, :], 0.0))
    P.op("tensor", lambda e: e.matmul(psW[:, :], onesr[0:1, 0:128], eselr[:, :], start=True, stop=True))
    P.op("vector", lambda e: e.tensor_copy(eselbc[:, :], psW[:, :]))
    P.op("gpsimd", lambda e: e.iota(out=iob[:, :], pattern=[[0, 1]], base=0, channel_multiplier=1))
    for i, gb in enumerate((g1, b1, g2, b2)):
        P.dma(grow[:, i, :], gb.ap())
    for i in range(4):
        for c in range(2):
            P.op("tensor", lambda e, _i=i, _c=c: e.matmul(psA[:, :], onesr[0:1, 0:128], grow[:, _i, 512 * _c:512 * (_c + 1)], start=True, stop=True))
            P.op("vector", lambda e, _i=i, _c=c: e.tensor_copy(gbc[:, _i, 512 * _c:512 * (_c + 1)], psA[:, :]))

    def mk_idx(t, offcol):
        P.op("vector", lambda e, _t=t, _o=offcol: e.tensor_scalar(out=idxf[:, :], in0=iob[:, :], scalar1=float(128 * _t), scalar2=offb[:, _o:_o + 1], op0=ALU.add, op1=ALU.add))
        P.op("vector", lambda e: e.tensor_copy(idxi[:, :], idxf[:, :]))

    P.settag("zA")
    P.sub_open("pA_")
    xnl = P.sb([128, NTl, D], name="xnl")
    wsb = P.sb([128, KD, 512], name="wsb")
    qkv = P.sb([128, 1536], name="qkv")
    kvp = P.sb([128, 512], name="kvp")
    for t in range(64):
        P.dma(ccin.ap()[128 * t:128 * (t + 1), :], zrow[:, :], engine="gpsimd")
    for t in range(NTl):
        P.dma(ln1[:, :], x[128 * t:128 * (t + 1), :])
        emit_layernorm(P, ln1[:, :], xnl[:, t, :], sq, stats)
        emit_layernorm(P, xnl[:, t, :], ln1[:, :], sq, stats)
        P.op("vector", lambda e: e.tensor_tensor(out=ln1[:, :], in0=ln1[:, :], in1=gbc[:, 0, :], op=ALU.mult))
        P.op("vector", lambda e: e.tensor_tensor(out=ln1[:, :], in0=ln1[:, :], in1=gbc[:, 1, :], op=ALU.add))
        for g in range(2):
            emit_transpose_batch(P, idsb, [ln1[:, 128 * (4 * g + i):128 * (4 * g + i + 1)] for i in range(4)], psT,
                                 dst_ap_fn=lambda i, _g=g: lnT[:, 4 * _g + i, :])
        for oc in range(3):
            P.dma(wsb[:, :, :], wqkvT.ap()[:, 512 * oc:512 * (oc + 1)].rearrange("(c p) m -> p c m", p=128))
            emit_mm_group(P, psA[:, :], [lnT[:, c, :] for c in range(KD)], [wsb[:, c, :] for c in range(KD)])
            P.op("vector", lambda e, _oc=oc: e.tensor_copy(qkv[:, 512 * _oc:512 * (_oc + 1)], psA[:, :]))
        P.dma(qkv_d.ap()[128 * t:128 * (t + 1), :], qkv[:, :], engine="gpsimd")
        P.op("vector", lambda e: e.tensor_copy(kvp.rearrange("p (g d) -> p g d", d=64)[:, 0:4, :], qkv.rearrange("p (g w) -> p g w", w=384)[:, :, 256:320]))
        P.op("vector", lambda e: e.tensor_copy(kvp.rearrange("p (g d) -> p g d", d=64)[:, 4:8, :], qkv.rearrange("p (g w) -> p g w", w=384)[:, :, 320:384]))
        mk_idx(t, 0)
        P.op("gpsimd", lambda e: e.indirect_dma_start(
            out=cc512_in, out_offset=bass.IndirectOffsetOnAxis(ap=idxi[0:128, 0:1], axis=0),
            in_=kvp[:, :], in_offset=None), weight=16)
    P.settag("zCC1")
    P.op("gpsimd", lambda e: e.collective_compute(
        "AllReduce", ALU.add, replica_groups=[list(range(8))],
        ins=[ccin.ap()[0:8192, :].opt()], outs=[ccout.ap()[0:8192, :].opt()]), weight=1, kind="dma")
    P.settag("zB")

    qblk = P.sb([128, 4, 256], name="qblk")
    kblk = P.sb([128, 8, 64], name="kblk")
    vblk = P.sb([128, 8, 64], name="vblk")
    scb = P.sb([128, 8, 64], name="scb")
    sc8 = P.sb([128, 8], name="sc8")
    aot = P.sb([128, D], name="aot")
    for t in range(NTl):
        P.dma(qblk[:, :, :], dview(qkv_d, 128 * t * 1536, [[1536, 128], [384, 4], [1, 256]]))
        for g4 in range(4):
            P.dma(kblk[:, :, :], dview(cc512_out, 128 * t * 512 + 64 * g4, [[512, 128], [1048576, 8], [1, 64]]))
            P.dma(vblk[:, :, :], dview(cc512_out, 128 * t * 512 + 256 + 64 * g4, [[512, 128], [1048576, 8], [1, 64]]))
            for hq in range(4):
                qv = qblk[:, g4, 64 * hq:64 * (hq + 1)]
                P.op("vector", lambda e, _q=qv: e.tensor_tensor(
                    out=scb[:, :, :], in0=_q.rearrange("p (o d) -> p o d", o=1).broadcast_to([128, 8, 64]),
                    in1=kblk[:, :, :], op=ALU.mult))
                P.op("vector", lambda e: e.tensor_reduce(out=sc8[:, :], in_=scb[:, :, :], axis=AX.X, op=ALU.add))
                P.op("vector", lambda e: e.tensor_scalar(out=sc8[:, :], in0=sc8[:, :], scalar1=0.125, scalar2=None, op0=ALU.mult))
                P.op("vector", lambda e: e.tensor_tensor(out=sc8[:, :], in0=sc8[:, :], in1=mbc[:, :], op=ALU.add))
                P.op("vector", lambda e: e.tensor_reduce(out=stats[:, 12:13], in_=sc8[:, :], axis=AX.X, op=ALU.max))
                P.op("vector", lambda e: e.tensor_scalar(out=stats[:, 13:14], in0=stats[:, 12:13], scalar1=-1.0, scalar2=None, op0=ALU.mult))
                P.op("scalar", lambda e: e.activation(out=sc8[:, :], in_=sc8[:, :], func=AF.Exp, bias=stats[:, 13:14], accum_out=stats[:, 14:15]))
                P.op("vector", lambda e: e.reciprocal(out=stats[:, 14:15], in_=stats[:, 14:15]))
                P.op("vector", lambda e: e.tensor_scalar(out=sc8[:, :], in0=sc8[:, :], scalar1=stats[:, 14:15], scalar2=None, op0=ALU.mult))
                P.op("vector", lambda e: e.tensor_tensor(
                    out=scb[:, :, :], in0=sc8.rearrange("p (s o) -> p s o", o=1).broadcast_to([128, 8, 64]),
                    in1=vblk[:, :, :], op=ALU.mult))
                P.op("vector", lambda e, _g=g4, _h=hq: e.tensor_reduce(
                    out=aot[:, 256 * _g + 64 * _h:256 * _g + 64 * _h + 64],
                    in_=scb[:, :, :].transpose([0, 2, 1]), axis=AX.X, op=ALU.add))
        P.dma(ao_d.ap()[128 * t:128 * (t + 1), :], aot[:, :], engine="gpsimd")

    P.settag("zC")
    for t in range(131):
        P.dma(ccin.ap()[128 * t:128 * (t + 1), :], zrow[:, :], engine="gpsimd")
    rw = P.sb([128, KD, 8], name="rw")
    lg = P.sb([128, 8], name="lg")
    P.dma(rw[:, :, :], routW.ap().rearrange("(c p) m -> p c m", p=128))
    for t in range(NTl):
        P.dma(aot[:, :], ao_d.ap()[128 * t:128 * (t + 1), :])
        for g in range(2):
            emit_transpose_batch(P, idsb, [aot[:, 128 * (4 * g + i):128 * (4 * g + i + 1)] for i in range(4)], psT,
                                 dst_ap_fn=lambda i, _g=g: lnT[:, 4 * _g + i, :])
        for oc in range(2):
            P.dma(wsb[:, :, :], wprojT.ap()[:, 512 * oc:512 * (oc + 1)].rearrange("(c p) m -> p c m", p=128))
            emit_mm_group(P, psA[:, :], [lnT[:, c, :] for c in range(KD)], [wsb[:, c, :] for c in range(KD)])
            P.op("vector", lambda e, _t=t, _oc=oc: e.tensor_tensor(out=xnl[:, _t, 512 * _oc:512 * (_oc + 1)], in0=xnl[:, _t, 512 * _oc:512 * (_oc + 1)], in1=psA[:, :], op=ALU.add))
        P.dma(h1_d.ap()[128 * t:128 * (t + 1), :], xnl[:, t, :], engine="gpsimd")
        emit_layernorm(P, xnl[:, t, :], ln1[:, :], sq, stats)
        P.op("vector", lambda e: e.tensor_tensor(out=ln1[:, :], in0=ln1[:, :], in1=gbc[:, 2, :], op=ALU.mult))
        P.op("vector", lambda e: e.tensor_tensor(out=ln1[:, :], in0=ln1[:, :], in1=gbc[:, 3, :], op=ALU.add))
        mk_idx(t, 1)
        P.op("gpsimd", lambda e: e.indirect_dma_start(
            out=ccin.ap()[:, :], out_offset=bass.IndirectOffsetOnAxis(ap=idxi[0:128, 0:1], axis=0),
            in_=ln1[:, :], in_offset=None), weight=16)
        for g in range(2):
            emit_transpose_batch(P, idsb, [ln1[:, 128 * (4 * g + i):128 * (4 * g + i + 1)] for i in range(4)], psT,
                                 dst_ap_fn=lambda i, _g=g: lnT[:, 4 * _g + i, :])
        emit_mm_group(P, psA[:, 0:8], [lnT[:, c, :] for c in range(KD)], [rw[:, c, :] for c in range(KD)])
        P.op("vector", lambda e: e.tensor_copy(lg[:, :], psA[:, 0:8]))
        P.op("vector", lambda e: e.tensor_reduce(out=stats[:, 16:17], in_=lg[:, :], axis=AX.X, op=ALU.max))
        P.op("vector", lambda e: e.tensor_scalar(out=sq[:, 0:8], in0=lg[:, :], scalar1=stats[:, 16:17], scalar2=None, op0=ALU.is_equal))
        P.op("vector", lambda e: e.scalar_tensor_tensor(out=sq[:, 8:16], in0=sq[:, 0:8], scalar=-1e30, in1=lg[:, :], op0=ALU.mult, op1=ALU.add))
        P.op("vector", lambda e: e.tensor_reduce(out=stats[:, 17:18], in_=sq[:, 8:16], axis=AX.X, op=ALU.max))
        P.op("vector", lambda e: e.tensor_scalar(out=sq[:, 16:24], in0=sq[:, 8:16], scalar1=stats[:, 17:18], scalar2=None, op0=ALU.is_equal))
        P.op("vector", lambda e: e.tensor_tensor(out=stats[:, 18:19], in0=stats[:, 16:17], in1=stats[:, 17:18], op=ALU.subtract))
        P.op("scalar", lambda e: e.activation(out=stats[:, 20:21], in_=stats[:, 18:19], func=AF.Sigmoid))
        P.op("vector", lambda e: e.tensor_scalar(out=stats[:, 19:20], in0=stats[:, 18:19], scalar1=-1.0, scalar2=None, op0=ALU.mult))
        P.op("scalar", lambda e: e.activation(out=stats[:, 21:22], in_=stats[:, 19:20], func=AF.Sigmoid))
        mk_idx(t, 3)
        P.dma(dview(meta_d, 128 * t * 24 + 0, [[24, 128], [1, 8]]), sq[:, 0:8], engine="gpsimd")
        P.dma(dview(meta_d, 128 * t * 24 + 12, [[24, 128], [1, 8]]), sq[:, 16:24], engine="gpsimd")
        P.dma(dview(meta_d, 128 * t * 24 + 8, [[24, 128], [1, 1]]), stats[:, 20:21], engine="gpsimd", allow_slow=True)
        P.dma(dview(meta_d, 128 * t * 24 + 20, [[24, 128], [1, 1]]), stats[:, 21:22], engine="gpsimd", allow_slow=True)
        P.dma(dview(meta_d, 128 * t * 24 + 9, [[24, 128], [1, 1]]), idxf[:, :], engine="gpsimd", allow_slow=True)
        P.dma(dview(meta_d, 128 * t * 24 + 21, [[24, 128], [1, 1]]), idxf[:, :], engine="gpsimd", allow_slow=True)
    P.dma(sq[0:48, :], dview(meta_d, 0, [[1024, 48], [1, 1024]]))
    P.op("vector", lambda e: e.tensor_scalar(out=idxf[:, :], in0=iob[:, :], scalar1=offb[:, 2:3], scalar2=None, op0=ALU.add))
    P.op("vector", lambda e: e.tensor_copy(idxi[:, :], idxf[:, :]))
    P.op("gpsimd", lambda e: e.indirect_dma_start(
        out=ccin.ap()[:, :], out_offset=bass.IndirectOffsetOnAxis(ap=idxi[0:48, 0:1], axis=0),
        in_=sq[0:48, :], in_offset=None), weight=16)
    P.sub_close()
    P.settag("zCC2")
    P.op("gpsimd", lambda e: e.collective_compute(
        "AllReduce", ALU.add, replica_groups=[list(range(8))],
        ins=[ccin.ap()[0:16768, :].opt()], outs=[ccout.ap()[0:16768, :].opt()]), weight=1, kind="dma")
    P.settag("zD")

    P.sub_open("pB_")
    metl = P.sb([128, 16, 12], name="metl")
    assl = metl[:, :, 0:8]
    ohl16 = P.sb([128, 16], name="ohl16")
    tmp8 = P.sb([128, 8], name="tmp8")
    wtile = P.sb([128, 16], name="wtile")
    kvt = P.sb([128, 16], name="kvt")
    sc2 = P.sb([128, 2], name="sc2")
    sci = P.sb([128, 1], I32, name="sci")
    sci16 = P.sb([128, 16], I32, name="sci16")
    pcol = P.sb([128, 1], name="pcol")
    w1sb = P.sb([128, KD, 2048], name="w1sb")
    w2sb = P.sb([128, 16, 1024], name="w2sb")
    gath = P.sb([128, D], name="gath")
    gT = P.sb([128, 8, 128], name="gT")
    hfm = P.sb([128, 16, 128], name="hfm")
    outr = P.sb([128, D], name="outr")
    for t in range(128):
        P.dma(ccin.ap()[128 * t:128 * (t + 1), :], zrow[:, :], engine="gpsimd")
    for b0 in range(0, CAP + 128, 128):
        n = min(128, CAP + 128 - b0)
        P.dma(slot_d.ap()[b0:b0 + n, :], zrow[0:n, 0:2], engine="gpsimd")
    P.op("gpsimd", lambda e: e.iota(out=sci[:, :], pattern=[[0, 1]], base=CAP, channel_multiplier=1))
    P.op("vector", lambda e: e.tensor_copy(pcol[:, :], sci[:, :]))
    P.op("vector", lambda e: e.memset(runr[:, :], 0.0))
    MBASE = 16384 * 1024
    for big in range(16):
        P.dma(metl[:, :, :], dview(ccout, MBASE + 2048 * big * 12, [[12, 128], [1536, 16], [1, 12]]))
        for j in range(16):
            def pf(e, _j=j):
                e.matmul(psW[:, :], trisb[:, :], assl[:, _j, :], start=True, stop=False)
                return e.matmul(psW[:, :], onesr[0:1, 0:128], runr[:, :], start=False, stop=True)
            P.op("tensor", pf)
            P.op("vector", lambda e: e.tensor_tensor(out=tmp8[:, :], in0=psW[:, :], in1=eselbc[:, :], op=ALU.mult))
            P.op("vector", lambda e, _j=j: e.tensor_reduce(out=wtile[:, _j:_j + 1], in_=tmp8[:, :], axis=AX.X, op=ALU.add))
            P.op("tensor", lambda e, _j=j: e.matmul(psR[:, :], ones128c[:, :], assl[:, _j, :], start=True, stop=True))
            P.op("vector", lambda e: e.tensor_tensor(out=runr[:, :], in0=runr[:, :], in1=psR[:, :], op=ALU.add))
            P.op("vector", lambda e, _j=j: e.tensor_tensor(out=tmp8[:, :], in0=assl[:, _j, :], in1=eselbc[:, :], op=ALU.mult))
            P.op("vector", lambda e, _j=j: e.tensor_reduce(out=ohl16[:, _j:_j + 1], in_=tmp8[:, :], axis=AX.X, op=ALU.add))
        P.op("vector", lambda e: e.tensor_scalar(out=kvt[:, :], in0=wtile[:, :], scalar1=float(CAP), scalar2=None, op0=ALU.is_lt))
        P.op("vector", lambda e: e.tensor_tensor(out=kvt[:, :], in0=kvt[:, :], in1=ohl16[:, :], op=ALU.mult))
        P.op("vector", lambda e: e.scalar_tensor_tensor(out=wtile[:, :], in0=wtile[:, :], scalar=pcol[:, :], in1=kvt[:, :], op0=ALU.subtract, op1=ALU.mult))
        P.op("vector", lambda e: e.tensor_scalar(out=wtile[:, :], in0=wtile[:, :], scalar1=pcol[:, :], scalar2=None, op0=ALU.add))
        P.op("vector", lambda e: e.tensor_copy(sci16[:, :], wtile[:, :]))
        for j in range(16):
            P.op("gpsimd", lambda e, _j=j: e.indirect_dma_start(
                out=slot_d.ap()[:, :], out_offset=bass.IndirectOffsetOnAxis(ap=sci16[0:128, _j:_j + 1], axis=0),
                in_=metl[:, _j, 8:10], in_offset=None), weight=16)
    P.settag("zE")
    P.dma(w1sb[:, :, :], w1.ap().rearrange("(c p) m -> p c m", p=128))
    P.dma(w2sb[:, :, :], w2.ap().rearrange("(c p) m -> p c m", p=128))
    GC = 0.7978845608028654
    gsc4 = P.sb([128, 3, 512], name="gsc4")
    for ch in range(CAP // 128):
        P.dma(sc2[:, :], slot_d.ap()[128 * ch:128 * (ch + 1), :])
        P.op("vector", lambda e: e.tensor_copy(sci[:, :], sc2[:, 1:2]))
        P.op("gpsimd", lambda e: e.indirect_dma_start(
            out=gath[:, :], out_offset=None, in_=ccout.ap()[:, :],
            in_offset=bass.IndirectOffsetOnAxis(ap=sci[0:128, 0:1], axis=0)), weight=16)
        for g in range(2):
            emit_transpose_batch(P, idsb, [gath[:, 128 * (4 * g + i):128 * (4 * g + i + 1)] for i in range(4)], psT,
                                 evac_batch=gT[:, 4 * g:4 * g + 4, :].rearrange("p a b -> p (a b)"))
        for f4 in range(4):
            def mm4(e, _f4=f4):
                inst = None
                for s4 in range(4):
                    fc = 4 * _f4 + s4
                    for c in range(KD):
                        inst = e.matmul(psA[:, 128 * s4:128 * (s4 + 1)], w1sb[:, c, 128 * fc:128 * (fc + 1)], gT[:, c, :],
                                        start=(c == 0), stop=(c == KD - 1))
                return inst
            P.op("tensor", mm4)
            P.op("scalar", lambda e: e.activation(out=gsc4[:, 0, :], in_=psA[:, :], func=AF.Square))
            P.op("vector", lambda e: e.tensor_scalar(out=gsc4[:, 0, :], in0=gsc4[:, 0, :], scalar1=0.044715, scalar2=1.0, op0=ALU.mult, op1=ALU.add))
            P.op("vector", lambda e: e.tensor_tensor(out=gsc4[:, 0, :], in0=gsc4[:, 0, :], in1=psA[:, :], op=ALU.mult))
            P.op("scalar", lambda e: e.activation(out=gsc4[:, 1, :], in_=gsc4[:, 0, :], func=AF.Tanh, scale=GC))
            P.op("vector", lambda e: e.tensor_scalar(out=gsc4[:, 2, :], in0=psA[:, :], scalar1=0.5, scalar2=None, op0=ALU.mult))
            P.op("vector", lambda e, _f4=f4: e.scalar_tensor_tensor(out=hfm[:, 4 * _f4:4 * (_f4 + 1), :].rearrange("p a b -> p (a b)"), in0=gsc4[:, 1, :], scalar=1.0, in1=gsc4[:, 2, :], op0=ALU.add, op1=ALU.mult))
        for oh in range(2):
            emit_mm_group(P, psA[:, :], [hfm[:, c, :] for c in range(16)], [w2sb[:, c, 512 * oh:512 * (oh + 1)] for c in range(16)])
            P.op("vector", lambda e, _oh=oh: e.tensor_scalar(out=outr[:, 512 * _oh:512 * (_oh + 1)], in0=psA[:, :], scalar1=sc2[:, 0:1], scalar2=None, op0=ALU.mult))
        P.op("gpsimd", lambda e: e.indirect_dma_start(
            out=ccin.ap()[:, :], out_offset=bass.IndirectOffsetOnAxis(ap=sci[0:128, 0:1], axis=0),
            in_=outr[:, :], in_offset=None, compute_op=ALU.add), weight=16)
    P.sub_close()
    P.settag("zCC3")
    P.op("gpsimd", lambda e: e.collective_compute(
        "AllReduce", ALU.add, replica_groups=[list(range(8))],
        ins=[ccin.ap()[0:16384, :].opt()], outs=[ccout.ap()[0:16384, :].opt()]), weight=1, kind="dma")
    P.settag("zF")

    P.sub_open("pC_")
    a1 = P.sb([128, D], name="a1")
    b1s = P.sb([128, D], name="b1s")
    for t in range(NTl):
        mk_idx(t, 3)
        P.op("gpsimd", lambda e: e.indirect_dma_start(
            out=a1[:, :], out_offset=None, in_=ccout.ap()[:, :],
            in_offset=bass.IndirectOffsetOnAxis(ap=idxi[0:128, 0:1], axis=0)), weight=16)
        P.dma(b1s[:, :], h1_d.ap()[128 * t:128 * (t + 1), :])
        P.op("vector", lambda e: e.tensor_tensor(out=a1[:, :], in0=a1[:, :], in1=b1s[:, :], op=ALU.add))
        P.dma(b1s[:, :], x[128 * t:128 * (t + 1), :])
        P.op("vector", lambda e: e.scalar_tensor_tensor(out=a1[:, :], in0=a1[:, :], scalar=0.5, in1=b1s[:, :], op0=ALU.mult, op1=ALU.add))
        P.dma(xo[128 * t:128 * (t + 1), :], a1[:, :], engine="gpsimd")
    P.sub_close()


def run_all_fused(x, I, only=None):
    P = Prog()
    host = {}

    def din(name, arr):
        a0 = arr[0] if isinstance(arr, list) else arr
        h = P.din(name, list(a0.shape), _npdt(a0))
        host[name] = arr
        return h

    identh = din("c_ident", _IDENT)
    m8c, trisc = _l0ffn_consts()
    offs_list = []
    esel_list = []
    mrow_list = []
    for c in range(N_CORES):
        es = np.zeros((1, 8), np.float32); es[0, c] = 1.0
        esel_list.append(es)
        mr = np.where(np.arange(8) <= c, 0.0, -1e30).astype(np.float32).reshape(1, 8)
        mrow_list.append(mr)
        offs_list.append(np.array([[2048. * c, 2048. * c, 16384. + 48. * c, 2048. * c]], np.float32))
    l0f = {
        "wqkvT": din("l0f_wqkvT", np.ascontiguousarray(I["L0_ffn_qkv_weight"].T)),
        "wprojT": din("l0f_wprojT", np.ascontiguousarray(I["L0_ffn_proj_weight"].T)),
        "routW": din("l0f_routW", np.ascontiguousarray(I["L0_ffn_router_weight"])),
        "g1": din("l0f_g1", I["L0_ffn_ln1_weight"].reshape(1, D)),
        "b1": din("l0f_b1", I["L0_ffn_ln1_bias"].reshape(1, D)),
        "g2": din("l0f_g2", I["L0_ffn_ln2_weight"].reshape(1, D)),
        "b2": din("l0f_b2", I["L0_ffn_ln2_bias"].reshape(1, D)),
        "tris": din("l0f_tris", trisc),
        "esel": din("l0f_esel", esel_list),
        "mrow": din("l0f_mrow", mrow_list),
        "offs": din("l0f_offs", offs_list),
        "w1": din("l0f_w1", [np.ascontiguousarray(I["L0_ffn_moe_w1"][c]) for c in range(N_CORES)]),
        "w2": din("l0f_w2", [np.ascontiguousarray(I["L0_ffn_moe_w2"][c]) for c in range(N_CORES)]),
        "ident": identh,
    }
    Mdk, Msk = _l0_masks()
    l0a = {
        "Wq": din("l0a_Wq", np.ascontiguousarray(I["L0_attn_Wq"])),
        "Wk": din("l0a_Wk", np.ascontiguousarray(I["L0_attn_Wk"])),
        "Wv": din("l0a_Wv", np.ascontiguousarray(I["L0_attn_Wv"])),
        "bq": din("l0a_bq", I["L0_attn_bq"].reshape(1, D)),
        "bk": din("l0a_bk", I["L0_attn_bk"].reshape(1, D)),
        "bv": din("l0a_bv", I["L0_attn_bv"].reshape(1, D)),
        "mdiag": din("l0a_mdiag", Mdk),
        "msib": din("l0a_msib", Msk),
        "ident": identh,
    }
    for k in ("bq", "bk", "bv", "mdiag", "msib"):
        l0a[k] = l0a[k].ap()
    l1f = {
        "k0": din("l1f_k0", np.ascontiguousarray(I["L1_ffn_k0"]).view(np.uint8)),
        "k1": din("l1f_k1", np.ascontiguousarray(I["L1_ffn_k1"]).view(np.uint8)),
        "k2": din("l1f_k2", np.ascontiguousarray(I["L1_ffn_k2"]).view(np.uint8)),
        "s0": din("l1f_s0", np.float32(I["L1_ffn_s0"]).reshape(1, 1)),
        "s1": din("l1f_s1", np.float32(I["L1_ffn_s1"]).reshape(1, 1)),
        "s2": din("l1f_s2", np.float32(I["L1_ffn_s2"]).reshape(1, 1)),
        "ident": identh,
    }
    cw = np.concatenate([I["L1_attn_conv_w"].reshape(2048, 4), I["L1_attn_conv_b"].reshape(2048, 1)], axis=1).astype(np.float32)
    l1a = {
        "WinT": din("l1a_WinT", np.ascontiguousarray(I["L1_attn_W_in"].T)),
        "convw": din("l1a_convw", cw),
        "WxT": din("l1a_WxT", np.ascontiguousarray(I["L1_attn_W_x"].T)),
        "WdtT": din("l1a_WdtT", np.ascontiguousarray(I["L1_attn_W_dt"].T)),
        "bdt": din("l1a_bdt", np.ascontiguousarray(I["L1_attn_b_dt"].reshape(2048, 1))),
        "Alog": din("l1a_Alog", np.ascontiguousarray(I["L1_attn_A_log"])),
        "Dp": din("l1a_Dp", np.ascontiguousarray(I["L1_attn_Dp"].reshape(2048, 1))),
        "WoutT": din("l1a_WoutT", np.ascontiguousarray(I["L1_attn_W_out"].T)),
        "normw": din("l1a_normw", np.ascontiguousarray(I["L1_attn_norm_w"].reshape(1, D))),
        "ident": identh,
    }
    l2f = {
        "wg": din("l2f_wg", np.ascontiguousarray(I["L2_ffn_w_gate"])),
        "wu": din("l2f_wu", np.ascontiguousarray(I["L2_ffn_w_up"])),
        "wd": din("l2f_wd", np.ascontiguousarray(I["L2_ffn_w_down"])),
        "ident": identh,
    }
    G1, G2, E3 = _l2_consts()
    l2a = {
        "wqkvT": din("l2a_wqkvT", np.ascontiguousarray(I["L2_attn_w_qkv"].T)),
        "wpT": din("l2a_wpT", np.ascontiguousarray(I["L2_attn_w_proj"].T)),
        "bproj": din("l2a_bproj", np.ascontiguousarray(I["L2_attn_b_proj"].reshape(1, D))),
        "rkeT": din("l2a_rkeT", np.ascontiguousarray(np.swapaxes(I["L2_attn_rpe_k_table"], 1, 2))),
        "rqeT": din("l2a_rqeT", np.ascontiguousarray(np.swapaxes(I["L2_attn_rpe_q_table"], 1, 2))),
        "rk02": din("l2a_rk02", np.ascontiguousarray(I["L2_attn_rpe_k_table"][:, :, [0, 62]])),
        "rq02": din("l2a_rq02", np.ascontiguousarray(I["L2_attn_rpe_q_table"][:, :, [62, 0]])),
        "rv0in": din("l2a_rv0in", np.ascontiguousarray(I["L2_attn_rpe_v_table"][:, [0, 62], :])),
        "rvmid": din("l2a_rvmid", np.ascontiguousarray(I["L2_attn_rpe_v_table"])),
        "G1": din("l2a_G1", G1), "G2": din("l2a_G2", G2), "EXP3": din("l2a_E3", E3),
        "ident": identh,
    }
    l3f = {
        "wgT": din("l3f_wgT", np.ascontiguousarray(I["L3_ffn_w_gate"].T)),
        "wuT": din("l3f_wuT", np.ascontiguousarray(I["L3_ffn_w_up"].T)),
        "wdT": din("l3f_wdT", np.ascontiguousarray(I["L3_ffn_w_down"].T)),
        "ident": identh,
    }
    l3a = {
        "Wq": din("l3a_Wq", np.ascontiguousarray(I["L3_attn_Wq"])),
        "Wk": din("l3a_Wk", np.ascontiguousarray(I["L3_attn_Wk"])),
        "Wv": din("l3a_Wv", np.ascontiguousarray(I["L3_attn_Wv"])),
        "Wo": din("l3a_Wo", np.ascontiguousarray(I["L3_attn_Wo"])),
        "W1": din("l3a_W1", np.ascontiguousarray(I["L3_attn_W1"])),
        "W2": din("l3a_W2", np.ascontiguousarray(I["L3_attn_W2"])),
        "ln1w": din("l3a_ln1w", np.ascontiguousarray(I["L3_attn_ln1_w"].reshape(1, D))),
        "ln2w": din("l3a_ln2w", np.ascontiguousarray(I["L3_attn_ln2_w"].reshape(1, D))),
        "relb": din("l3a_relb", np.ascontiguousarray(I["L3_attn_rel_bias"])),
        "scale_h": din("l3a_scale", np.ascontiguousarray(I["L3_attn_scale"].reshape(1, 16))),
        "ET": din("l3a_ET", _build_ET()),
        "ranti": din("l3a_ranti", _RANTI),
        "ident": identh,
    }
    l3a["relb"] = l3a["relb"].ap()
    l3a["ET"] = l3a["ET"].ap()
    l3a["ranti"] = l3a["ranti"].ap()

    ccin = P.dtmp([16768, 1024], name="ccin")
    ccout = P.dtmp([16768, 1024], name="ccout")
    l0f["ccin"] = ccin
    l0f["ccout"] = ccout

    x_in = P.din("xin", [TOK, D])
    xo = P.dout("xof", [TOK, D])
    seq = [
        (emit_l0ffn_fused, l0f), (emit_l0attn, l0a), (emit_l0ffn_fused, l0f),
        (emit_l1ffn, l1f), (emit_l1attn, l1a), (emit_l1ffn, l1f),
        (emit_l2ffn, l2f), (emit_l2attn, l2a), (emit_l2ffn, l2f),
        (emit_l3ffn, l3f), (emit_l3attn, l3a), (emit_l3ffn, l3f),
    ]
    if only is not None:
        seq = seq[only[0]:only[1]]
    qwg_d = P.dtmp([D, 2816], name="l2qwg")
    qwu_d = P.dtmp([D, 2816], name="l2qwu")
    qwd_d = P.dtmp([2816, D], name="l2qwd")
    chain = [x_in] + [P.dtmp([TOK, D], name=f"chain{i}") for i in range(len(seq) - 1)] + [xo]
    n_l2f = 0
    for i, (fn, ins) in enumerate(seq):
        P.scope_open(f"s{i}_")
        IN = dict(ins)
        IN["ident"] = identh.ap()
        IN["x"] = chain[i].ap()
        IN["xo"] = chain[i + 1].ap()
        if fn is emit_l2ffn:
            IN["qwgT"], IN["qwuT"], IN["qwdT"] = qwg_d, qwu_d, qwd_d
            IN["quant_done"] = n_l2f > 0
            n_l2f += 1
        fn(P, IN)
        P.scope_close()
    nc = P.finalize()
    shards = _shard_tok(x)
    in_maps = []
    for c in range(N_CORES):
        m = {}
        for k, v in host.items():
            m[k] = v[c] if isinstance(v, list) else v
        m["xin"] = shards[c]
        in_maps.append(m)
    res = run_bass_kernel_spmd(nc, in_maps, core_ids=list(range(N_CORES)))
    return np.stack([r["xof"].reshape(S, D) for r in res.results])


def kernel(**inputs):
    I = inputs
    x = np.ascontiguousarray(I["L0_attn_hidden_states"], dtype=np.float32)
    x = run_all_fused(x, I)
    return np.ascontiguousarray(x.astype(np.float32))


# version 22
# speedup vs baseline: 1.1343x; 1.1343x over previous
import math
import os
from contextlib import ExitStack
import numpy as np

os.environ.setdefault("GAUGE_NTFF_IGNORE_DMA", "1")
import concourse.bass as bass
import concourse.mybir as mybir
import bass_rust
from concourse.bass_utils import run_bass_kernel_spmd

F32 = mybir.dt.float32
I32 = mybir.dt.int32
U32 = mybir.dt.uint32
U8 = mybir.dt.uint8
AF = mybir.ActivationFunctionType
ALU = mybir.AluOpType
AX = mybir.AxisListType

N_CORES = 8
B, S, D = 8, 2048, 1024
TOK = S
NT = TOK // 128


def dview(t, offset, pairs):
    v = t.copy() if isinstance(t, bass_rust.AP) else t.ap().copy()
    v.offset = offset
    v.ap = bass_rust.VecI64Pair([list(p) for p in pairs])
    return v


class Prog:

    ENGINES = ("sync", "tensor", "vector", "scalar", "gpsimd")

    def __init__(self):
        self.nc = bass.Bass("TRN2", target_bir_lowering=False)
        self.steps = []
        self._uid = 0
        self.pf_counts = {}
        self.pf_units = {}
        self.pf_pending = None
        self._prefix = None
        self._stack = None

    _tag = ""

    def settag(self, t):
        self._tag = t

    def scope_open(self, prefix):
        self._prefix = prefix
        self._tag = ""
        self._stacks = [ExitStack()]

    def scope_close(self):
        while self._stacks:
            self._stacks.pop().close()
        self._prefix = None
        self._stacks = None

    def sub_open(self, tag):
        self._sub = tag
        self._stacks.append(ExitStack())

    def sub_close(self):
        self._stacks.pop().close()
        self._sub = None

    _sub = None
    _stacks = None

    def _nm(self, name, kind):
        nm = name or self.uid(kind)
        if self._sub:
            nm = self._sub + nm
        return (self._prefix + nm) if self._prefix else nm

    def uid(self, p):
        self._uid += 1
        return f"{p}{self._uid}"

    PREFETCH = os.environ.get("KPREFETCH", "all")

    def _pf(self, site):
        return self.PREFETCH == "all" or site in self.PREFETCH.split(",")

    def mark(self):
        return sum(s[1] for s in self.steps)

    def dma(self, out, in_, engine="sync", allow_slow=False, after=None, site=None, pf=None):
        if (after is not None or pf is not None) and site is not None and not self._pf(site):
            after = None
            pf = None
        tag = (self._prefix or "top") + self._tag
        if pf is not None:
            key = (self._prefix, site, pf)
            self.pf_units[key] = 16
            n = self.pf_counts.get(key, 0) + 1
            self.pf_counts[key] = n
            self.steps.append((engine, 0, lambda e: e.dma_start(out=out, in_=in_), "dma", tag, after, ("inc", key, 16)))
            return
        if allow_slow:
            self.steps.append((engine, 16, lambda e: e.dma_start(out=out, in_=in_, allow_slow_non_contiguous=True), "dma", tag, after, None))
        else:
            self.steps.append((engine, 16, lambda e: e.dma_start(out=out, in_=in_), "dma", tag, after, None))

    pf_units = {}

    def pfuse(self, site, pf, unit=16):
        if not self._pf(site):
            return
        key = (self._prefix, site, pf)
        self.pf_pending = ("wait", key, unit * self.pf_counts.get(key, 0))

    pf_counts = None
    pf_pending = None

    def op(self, engine, fn, weight=1, kind=None, after=None, site=None, pf=None):
        if (after is not None or pf is not None) and site is not None and not self._pf(site):
            after = None
            pf = None
        if pf is not None:
            key = (self._prefix, site, pf)
            self.pf_units[key] = 1
            n = self.pf_counts.get(key, 0) + 1
            self.pf_counts[key] = n
            self.steps.append((engine, 0, fn, "cmp", (self._prefix or "top") + self._tag, after, ("inc", key, 1)))
            return
        extra = self.pf_pending
        self.pf_pending = None
        self.steps.append((engine, weight, fn, kind or ("dma" if weight >= 16 else "cmp"), (self._prefix or "top") + self._tag, after, extra))

    def sb(self, shape, dtype=F32, name=None):
        nm = self._nm(name, "sb")
        if self._stacks:
            return self._stacks[-1].enter_context(self.nc.sbuf_tensor(nm, shape, dtype))
        return self.nc.alloc_sbuf_tensor(nm, shape, dtype)

    def ps(self, shape, dtype=F32, name=None):
        nm = self._nm(name, "ps")
        if self._stacks:
            return self._stacks[-1].enter_context(self.nc.psum_tensor(nm, shape, dtype))
        return self.nc.alloc_psum_tensor(nm, shape, dtype)

    def din(self, name, shape, dtype=F32):
        return self.nc.dram_tensor(name, shape, dtype, kind="ExternalInput")

    def dout(self, name, shape, dtype=F32):
        return self.nc.dram_tensor(name, shape, dtype, kind="ExternalOutput")

    def dtmp(self, shape, dtype=F32, name=None):
        return self.nc.dram_tensor(self._nm(name, "dt"), shape, dtype, kind="Internal")

    EPOCH = 30000

    def finalize(self):
        nc = self.nc
        acc = 0
        epochs = []
        bases = [0]
        skips = []
        pf_sems = {}
        for key in self.pf_counts:
            pf_sems[key] = nc.alloc_semaphore(f"pf{len(pf_sems)}")
        prev_eng, prev_kind = None, None
        for (eng, w, fn, kind, tag, after, extra) in self.steps:
            ep = acc // self.EPOCH
            while len(bases) <= ep:
                bases.append(acc)
            skips.append(False)
            epochs.append((ep, acc, w))
            acc += w
            prev_eng, prev_kind = eng, kind
        n_ep = len(bases)
        sems = [nc.alloc_semaphore(f"oseq{i}") for i in range(n_ep)]
        ep_tot = [0] * n_ep
        for (ep, cum, w) in epochs:
            ep_tot[ep] += w
        seqs = {e: [] for e in self.ENGINES}
        for i, (eng, w, fn, kind, tag, after, extra) in enumerate(self.steps):
            ep, cum, _ = epochs[i]
            wait_cum = cum if after is None else min(after, cum)
            wep = 0
            for k in range(len(bases) - 1, -1, -1):
                if wait_cum >= bases[k] and (k == 0 or bases[k] < wait_cum or wait_cum == 0):
                    pass
            wep = 0
            for k in range(len(bases)):
                if bases[k] <= wait_cum:
                    wep = k
            seqs[eng].append((ep, wep, wait_cum - bases[wep], w, fn, skips[i], tag, extra))
        with nc.Block() as blk:
            for eng in self.ENGINES:
                items = seqs[eng]

                def make(items, eng=eng):
                    def body(e):
                        last_ep = 0
                        for (ep, wep, wait, w, fn, skip, tag, extra) in items:
                            with nc.named_scope(tag):
                                if ep > last_ep:
                                    e.wait_ge(sems[ep - 1], ep_tot[ep - 1])
                                    last_ep = ep
                                if wait > 0 and not skip:
                                    e.wait_ge(sems[wep], wait)
                                if extra is not None and extra[0] == "wait" and extra[2] > 0:
                                    e.wait_ge(pf_sems[extra[1]], extra[2])
                                inst = fn(e)
                                if extra is not None and extra[0] == "inc":
                                    inst.then_inc(pf_sems[extra[1]], extra[2])
                                elif w > 0:
                                    inst.then_inc(sems[ep], w)
                        if eng == "sync":
                            for i2 in range(n_ep):
                                e.wait_ge(sems[i2], ep_tot[i2])
                            for key, s2 in pf_sems.items():
                                e.wait_ge(s2, self.pf_units.get(key, 16) * self.pf_counts[key])
                    return body

                getattr(blk, eng)(make(items))
        return nc



def emit_layernorm(P, x_ap, out_ap, sq, stats, eps=1e-5, D_=1024, gw=None, gb=None):
    sq = sq[:, 0:D_]
    s1 = stats[:, 0:1]
    s2 = stats[:, 1:2]
    mean = stats[:, 2:3]
    var = stats[:, 3:4]
    rstd = stats[:, 4:5]
    nb = stats[:, 5:6]
    P.op("vector", lambda e: e.tensor_reduce(out=s1, in_=x_ap, axis=AX.X, op=ALU.add))
    P.op("scalar", lambda e: e.activation(out=sq, in_=x_ap, func=AF.Square, accum_out=s2))
    P.op("vector", lambda e: e.tensor_scalar(out=mean, in0=s1, scalar1=1.0 / D_, scalar2=None, op0=ALU.mult))
    P.op("vector", lambda e: e.tensor_tensor(out=var, in0=mean, in1=mean, op=ALU.mult))
    P.op("vector", lambda e: e.scalar_tensor_tensor(out=var, in0=s2, scalar=1.0 / D_, in1=var, op0=ALU.mult, op1=ALU.subtract))
    P.op("vector", lambda e: e.tensor_scalar(out=var, in0=var, scalar1=float(eps), scalar2=None, op0=ALU.add))
    P.op("scalar", lambda e: e.activation(out=rstd, in_=var, func=AF.Sqrt))
    P.op("vector", lambda e: e.reciprocal(out=rstd, in_=rstd))
    P.op("vector", lambda e: e.tensor_tensor(out=nb, in0=mean, in1=rstd, op=ALU.mult))
    P.op("vector", lambda e: e.tensor_scalar(out=nb, in0=nb, scalar1=-1.0, scalar2=None, op0=ALU.mult))
    P.op("vector", lambda e: e.tensor_scalar(out=out_ap, in0=x_ap, scalar1=rstd, scalar2=nb, op0=ALU.mult, op1=ALU.add))


def emit_rmsnorm(P, x_ap, out_ap, sq, stats, eps, D_):
    sq = sq[:, 0:D_]
    s2 = stats[:, 1:2]
    var = stats[:, 3:4]
    rstd = stats[:, 4:5]
    P.op("scalar", lambda e: e.activation(out=sq, in_=x_ap, func=AF.Square, accum_out=s2))
    P.op("vector", lambda e: e.tensor_scalar(out=var, in0=s2, scalar1=1.0 / D_, scalar2=float(eps), op0=ALU.mult, op1=ALU.add))
    P.op("scalar", lambda e: e.activation(out=rstd, in_=var, func=AF.Sqrt))
    P.op("vector", lambda e: e.reciprocal(out=rstd, in_=rstd))
    P.op("vector", lambda e: e.tensor_scalar(out=out_ap, in0=x_ap, scalar1=rstd, scalar2=None, op0=ALU.mult))


def emit_transpose(P, ident, src_ap, pst, dst_ap):
    P.op("tensor", lambda e: e.transpose(pst[:, 0:128], src_ap, ident[:, :]))
    P.op("vector", lambda e: e.tensor_copy(dst_ap, pst[:, 0:128]))


def emit_mm_group(P, psum_ap, lhsT_aps, rhs_aps, start=True):
    n = len(lhsT_aps)

    def fn(e):
        inst = None
        for k in range(n):
            inst = e.matmul(psum_ap, lhsT_aps[k], rhs_aps[k],
                            start=(start and k == 0), stop=(k == n - 1))
        return inst
    P.op("tensor", fn)



def emit_l3ffn(P, IN):
    nc = P.nc
    x = IN["x"]
    wgT = IN["wgT"]
    wuT = IN["wuT"]
    wdT = IN["wdT"]
    ident = IN["ident"]
    xo = IN["xo"]

    HID = 3072
    NH_ = HID // 128
    TC = 512
    NTC = TOK // TC
    KD = D // 128

    xsb = P.sb([128, 4, D], name="xsb")
    idsb = P.sb([128, 128], name="idsb")
    xn = P.sb([128, 4, D], name="xn")
    xnT = P.sb([128, KD, TC], name="xnT")
    sq = P.sb([128, D], name="sq")
    stats = P.sb([128, 16], name="stats")
    wgi2 = [P.sb([128, NH_, 128], I32, name="wgi0"), P.sb([128, NH_, 128], I32, name="wgi1")]
    wgf = P.sb([128, KD, 128], name="wgf")
    wuf = P.sb([128, KD, 128], name="wuf")
    wdf = P.sb([128, NH_, 128], name="wdf")
    hsb = P.sb([128, NH_, TC], name="hsb")
    hs = P.sb([128, TC], name="hs")
    dsb = P.sb([128, TC], name="dsb")
    xout = P.sb([128, 4, D], name="xout")
    psG = P.ps([128, 512], name="psG")
    psU = P.ps([128, 512], name="psU")
    psD = P.ps([128, 512], name="psD")
    psT = P.ps([128, 128], name="psT")


    P.dma(idsb[:, :], ident)

    DQ = 1.0 / 127.0
    xr = x.rearrange("(tc n p) d -> tc p n d", p=128, n=4)
    for tc in range(NTC):
        P.dma(xsb[:, :, :], xr[tc])
        for t in range(4):
            emit_layernorm(P, xsb[:, t, :], xn[:, t, :], sq, stats)
            for c in range(KD):
                emit_transpose(P, idsb, xn[:, t, 0 + 128 * c:128 * (c + 1)], psT,
                               xnT[:, c, 128 * t:128 * (t + 1)])
        _m0 = P.mark(); mkg = [_m0, _m0]
        for oc in range(NH_):
            P.dma(wgi2[0][:, 0:KD, :], wgT.ap()[:, 128 * oc:128 * (oc + 1)].rearrange("(c p) m -> p c m", p=128), after=mkg[0], site="l3f", pf=0)
            P.pfuse("l3f", 0)
            P.op("vector", lambda e: e.tensor_scalar(out=wgf[:, :, :], in0=wgi2[0][:, 0:KD, :], scalar1=DQ, scalar2=None, op0=ALU.mult))
            mkg[0] = P.mark()
            P.dma(wgi2[1][:, 0:KD, :], wuT.ap()[:, 128 * oc:128 * (oc + 1)].rearrange("(c p) m -> p c m", p=128), after=mkg[1], site="l3f", pf=1)
            emit_mm_group(P, psG[:, :], [wgf[:, k, :] for k in range(KD)], [xnT[:, k, :] for k in range(KD)])
            P.pfuse("l3f", 1)
            P.op("vector", lambda e: e.tensor_scalar(out=wuf[:, :, :], in0=wgi2[1][:, 0:KD, :], scalar1=DQ, scalar2=None, op0=ALU.mult))
            mkg[1] = P.mark()
            emit_mm_group(P, psU[:, :], [wuf[:, k, :] for k in range(KD)], [xnT[:, k, :] for k in range(KD)])
            P.op("scalar", lambda e: e.activation(out=hs[:, :], in_=psG[:, :], func=AF.Silu))
            P.op("vector", lambda e, _oc=oc: e.tensor_tensor(out=hsb[:, _oc, :], in0=hs[:, :], in1=psU[:, :], op=ALU.mult))
        _m0 = P.mark(); mkd = [_m0, _m0]
        for oc2 in range(KD):
            b = oc2 % 2
            P.dma(wgi2[b][:, :, :], wdT.ap()[:, 128 * oc2:128 * (oc2 + 1)].rearrange("(c p) m -> p c m", p=128), after=mkd[b], site="l3f", pf=b)
            P.pfuse("l3f", b)
            P.op("vector", lambda e, _b=b: e.tensor_scalar(out=wdf[:, :, :], in0=wgi2[_b][:, :, :], scalar1=DQ, scalar2=None, op0=ALU.mult))
            mkd[b] = P.mark()
            emit_mm_group(P, psD[:, :], [wdf[:, k, :] for k in range(NH_)], [hsb[:, k, :] for k in range(NH_)])
            P.op("vector", lambda e: e.tensor_scalar(out=dsb[:, :], in0=psD[:, :], scalar1=0.5, scalar2=None, op0=ALU.mult))
            for t in range(4):
                P.op("tensor", lambda e, _t=t: e.transpose(psT[:, :], dsb[:, 128 * _t:128 * (_t + 1)], idsb[:, :]))
                P.op("vector", lambda e, _t=t, _oc2=oc2: e.tensor_tensor(
                    out=xout[:, _t, 128 * _oc2:128 * (_oc2 + 1)],
                    in0=xsb[:, _t, 128 * _oc2:128 * (_oc2 + 1)], in1=psT[:, :], op=ALU.add))
        P.dma(xo.rearrange("(tc n p) d -> tc p n d", p=128, n=4)[tc], xout[:, :, :], engine="gpsimd")




_IDENT = np.eye(128, dtype=np.float32)


def build_l3ffn():
    P = Prog()
    IN = {}
    IN["x"] = P.din("x", [TOK, D]).ap()
    IN["wgT"] = P.din("wgT", [D, 3072], I32)
    IN["wuT"] = P.din("wuT", [D, 3072], I32)
    IN["wdT"] = P.din("wdT", [3072, D], I32)
    IN["ident"] = P.din("ident", [128, 128]).ap()
    IN["xo"] = P.dout("xo", [TOK, D]).ap()
    emit_l3ffn(P, IN)
    return P.finalize()

def _shard_tok(x):
    return [np.ascontiguousarray(x[c].reshape(TOK, D)) for c in range(N_CORES)]


def run_l3ffn(x, wg, wu, wd):
    nc = build_l3ffn()
    wgT = np.ascontiguousarray(wg.T)
    wuT = np.ascontiguousarray(wu.T)
    wdT = np.ascontiguousarray(wd.T)
    shards = _shard_tok(x)
    in_maps = [{"x": s, "wgT": wgT, "wuT": wuT, "wdT": wdT, "ident": _IDENT} for s in shards]
    res = run_bass_kernel_spmd(nc, in_maps, core_ids=list(range(N_CORES)))
    return np.stack([r["xo"].reshape(S, D) for r in res.results])





def emit_l1ffn(P, IN):
    x = IN["x"]
    k0 = IN["k0"]
    k1 = IN["k1"]
    k2 = IN["k2"]
    s0 = IN["s0"]
    s1 = IN["s1"]
    s2 = IN["s2"]
    ident = IN["ident"]
    xo = IN["xo"]

    H1 = 4096
    NH1 = H1 // 128
    KD = D // 128
    TC = 512
    NTC = TOK // TC

    xsb = P.sb([128, 4, D], name="xsb")
    idsb = P.sb([128, 128], name="idsb")
    xn = P.sb([128, 4, D], name="xn")
    xnT = P.sb([128, KD, TC], name="xnT")
    sq = P.sb([128, D], name="sq")
    stats = P.sb([128, 16], name="stats")
    svec = P.sb([128, 8], name="svec")
    w8a = [P.sb([128, NH1, 128], U8, name="w8a0"), P.sb([128, NH1, 128], U8, name="w8a1")]
    w8b = [P.sb([128, KD, 128], U8, name="w8b0"), P.sb([128, KD, 128], U8, name="w8b1")]
    wf = P.sb([128, NH1, 128], name="wf")
    r1 = P.sb([128, NH1, TC], name="r1")
    r2 = P.sb([128, NH1, TC], name="r2")
    dsb = sq[:, 0:TC]
    xout = xn
    psH = P.ps([128, 512], name="psH")
    psT = P.ps([128, 128], name="psT")

    P.dma(idsb[:, :], ident)
    for i, st in enumerate((s0, s1, s2)):
        P.dma(svec[:, i:i + 1], dview(st, 0, [[0, 128], [1, 1]]))
    P.op("vector", lambda e: e.tensor_scalar(out=svec[:, 3:5], in0=svec[:, 0:2], scalar1=2.0, scalar2=None, op0=ALU.mult))

    xr = x.rearrange("(tc n p) d -> tc p n d", p=128, n=4)
    xor_ = xo.rearrange("(tc n p) d -> tc p n d", p=128, n=4)
    for tc in range(NTC):
        P.dma(xsb[:, :, :], xr[tc])
        for t in range(4):
            emit_layernorm(P, xsb[:, t, :], xn[:, t, :], sq, stats)
            for c in range(KD):
                emit_transpose(P, idsb, xn[:, t, 128 * c:128 * (c + 1)], psT,
                               xnT[:, c, 128 * t:128 * (t + 1)])
        _m0 = P.mark(); mka = [_m0, _m0]
        for oc in range(NH1):
            b = oc % 2
            P.dma(w8a[b][:, 0:KD, :], k0.ap()[0:D, 128 * oc:128 * (oc + 1)].rearrange("(c p) m -> p c m", p=128), after=mka[b], site="l1f", pf=b)
            P.dma(w8b[b][:, 0:KD, :], k0.ap()[D:2 * D, 128 * oc:128 * (oc + 1)].rearrange("(c p) m -> p c m", p=128), after=mka[b], site="l1f", pf=b)
            P.pfuse("l1f", b)
            P.op("vector", lambda e, _b=b: e.tensor_tensor(out=wf[:, 0:KD, :], in0=w8a[_b][:, 0:KD, :], in1=w8b[_b][:, 0:KD, :], op=ALU.subtract))
            mka[b] = P.mark()
            emit_mm_group(P, psH[:, 0:TC], [wf[:, k, :] for k in range(KD)], [xnT[:, k, :] for k in range(KD)])
            P.op("vector", lambda e, _oc=oc: e.tensor_scalar(out=r1[:, _oc, :], in0=psH[:, 0:TC], scalar1=svec[:, 3:4], scalar2=0.0, op0=ALU.mult, op1=ALU.max))
        _m0 = P.mark(); mka = [_m0, _m0]
        for oc in range(NH1):
            b = oc % 2
            P.dma(w8a[b][:, :, :], k1.ap()[:, 128 * oc:128 * (oc + 1)].rearrange("(c p) m -> p c m", p=128), after=mka[b], site="l1f", pf=b)
            P.pfuse("l1f", b)
            P.op("vector", lambda e, _b=b: e.tensor_scalar(out=wf[:, :, :], in0=w8a[_b][:, :, :], scalar1=2.0, scalar2=-1.0, op0=ALU.mult, op1=ALU.add))
            mka[b] = P.mark()
            emit_mm_group(P, psH[:, 0:TC], [wf[:, k, :] for k in range(NH1)], [r1[:, k, :] for k in range(NH1)])
            P.op("vector", lambda e, _oc=oc: e.tensor_scalar(out=r2[:, _oc, :], in0=psH[:, 0:TC], scalar1=svec[:, 4:5], scalar2=0.0, op0=ALU.mult, op1=ALU.max))
        _m0 = P.mark(); mka = [_m0, _m0]
        for oc2 in range(KD):
            b = oc2 % 2
            P.dma(w8a[b][:, :, :], k2.ap()[:, 128 * oc2:128 * (oc2 + 1)].rearrange("(c p) m -> p c m", p=128), after=mka[b], site="l1f", pf=b)
            P.pfuse("l1f", b)
            P.op("vector", lambda e, _b=b: e.tensor_scalar(out=wf[:, :, :], in0=w8a[_b][:, :, :], scalar1=2.0, scalar2=-1.0, op0=ALU.mult, op1=ALU.add))
            mka[b] = P.mark()
            emit_mm_group(P, psH[:, 0:TC], [wf[:, k, :] for k in range(NH1)], [r2[:, k, :] for k in range(NH1)])
            P.op("vector", lambda e: e.tensor_scalar(out=dsb[:, :], in0=psH[:, 0:TC], scalar1=svec[:, 2:3], scalar2=None, op0=ALU.mult))
            for t in range(4):
                P.op("tensor", lambda e, _t=t: e.transpose(psT[:, :], dsb[:, 128 * _t:128 * (_t + 1)], idsb[:, :]))
                P.op("vector", lambda e, _t=t, _oc2=oc2: e.tensor_tensor(
                    out=xout[:, _t, 128 * _oc2:128 * (_oc2 + 1)],
                    in0=xsb[:, _t, 128 * _oc2:128 * (_oc2 + 1)], in1=psT[:, :], op=ALU.add))
        P.dma(xor_[tc], xout[:, :, :], engine="gpsimd")


def build_l1ffn():
    P = Prog()
    IN = {}
    IN["x"] = P.din("x", [TOK, D]).ap()
    IN["k0"] = P.din("k0", [2 * D, 4096], U8)
    IN["k1"] = P.din("k1", [4096, 4096], U8)
    IN["k2"] = P.din("k2", [4096, D], U8)
    IN["s0"] = P.din("s0", [1, 1])
    IN["s1"] = P.din("s1", [1, 1])
    IN["s2"] = P.din("s2", [1, 1])
    IN["ident"] = P.din("ident", [128, 128]).ap()
    IN["xo"] = P.dout("xo", [TOK, D]).ap()
    emit_l1ffn(P, IN)
    return P.finalize()

def run_l1ffn(x, k0, k1, k2, s0, s1, s2):
    nc = build_l1ffn()
    shards = _shard_tok(x)
    k0u = np.ascontiguousarray(k0).view(np.uint8)
    k1u = np.ascontiguousarray(k1).view(np.uint8)
    k2u = np.ascontiguousarray(k2).view(np.uint8)
    in_maps = [{"x": s, "k0": k0u, "k1": k1u, "k2": k2u,
                "s0": np.float32(s0).reshape(1, 1), "s1": np.float32(s1).reshape(1, 1),
                "s2": np.float32(s2).reshape(1, 1), "ident": _IDENT} for s in shards]
    res = run_bass_kernel_spmd(nc, in_maps, core_ids=list(range(N_CORES)))
    return np.stack([r["xo"].reshape(S, D) for r in res.results])





_RMAGIC = 8388608.0


def emit_fp4_quant(P, v_ap, q_ap, F, scr, scrn, FMAX=1472):
    if F > FMAX:
        o = 0
        while o < F:
            w = min(FMAX, F - o)
            emit_fp4_quant(P, v_ap[:, o:o + w], q_ap[:, o:o + w], w, scr, scrn, FMAX)
            o += w
        return
    NB = F // 16
    bm = scrn[:, 0:NB]
    rs2 = scrn[:, NB:2 * NB]
    av = scr[:, 0:F]
    t2 = scr[:, F:2 * F]
    acc = scr[:, 2 * F:3 * F]
    sg = scr[:, 3 * F:4 * F]
    u = av
    v3 = v_ap.rearrange("p (g b) -> p g b", b=16)
    P.op("vector", lambda e: e.tensor_reduce(out=bm, in_=v3, axis=AX.X, op=ALU.max, apply_absolute_value=True))
    P.op("vector", lambda e: e.tensor_scalar(out=bm, in0=bm, scalar1=1.0 / 6.0, scalar2=1e-12, op0=ALU.mult, op1=ALU.max))
    P.op("vector", lambda e: e.reciprocal(out=rs2, in_=bm))
    P.op("vector", lambda e: e.tensor_scalar(out=rs2, in0=rs2, scalar1=2.0, scalar2=None, op0=ALU.mult))
    P.op("scalar", lambda e: e.activation(out=av, in_=v_ap, func=AF.Abs))
    P.op("scalar", lambda e: e.activation(out=sg, in_=v_ap, func=AF.Sign))
    P.op("vector", lambda e: e.tensor_tensor(out=t2.rearrange("p (g b) -> p g b", b=16), in0=av.rearrange("p (g b) -> p g b", b=16), in1=rs2.rearrange("p (g o) -> p g o", o=1).broadcast_to([128, NB, 16]), op=ALU.mult))
    P.op("vector", lambda e: e.tensor_scalar(out=u, in0=t2, scalar1=_RMAGIC, scalar2=_RMAGIC, op0=ALU.add, op1=ALU.subtract))
    P.op("vector", lambda e: e.tensor_scalar(out=acc, in0=u, scalar1=0.5, scalar2=2.0, op0=ALU.mult, op1=ALU.min))
    P.op("vector", lambda e: e.tensor_scalar(out=u, in0=t2, scalar1=0.5, scalar2=_RMAGIC, op0=ALU.mult, op1=ALU.add))
    P.op("vector", lambda e: e.tensor_scalar(out=u, in0=u, scalar1=_RMAGIC, scalar2=2.0, op0=ALU.subtract, op1=ALU.max))
    P.op("vector", lambda e: e.tensor_scalar(out=u, in0=u, scalar1=4.0, scalar2=2.0, op0=ALU.min, op1=ALU.subtract))
    P.op("vector", lambda e: e.tensor_tensor(out=acc, in0=acc, in1=u, op=ALU.add))
    P.op("vector", lambda e: e.tensor_scalar(out=u, in0=t2, scalar1=0.25, scalar2=_RMAGIC, op0=ALU.mult, op1=ALU.add))
    P.op("vector", lambda e: e.tensor_scalar(out=u, in0=u, scalar1=_RMAGIC, scalar2=2.0, op0=ALU.subtract, op1=ALU.mult))
    P.op("vector", lambda e: e.tensor_scalar(out=u, in0=u, scalar1=4.0, scalar2=6.0, op0=ALU.max, op1=ALU.min))
    P.op("vector", lambda e: e.scalar_tensor_tensor(out=acc, in0=u, scalar=-4.0, in1=acc, op0=ALU.add, op1=ALU.add))
    P.op("vector", lambda e: e.tensor_tensor(out=sg.rearrange("p (g b) -> p g b", b=16), in0=sg.rearrange("p (g b) -> p g b", b=16), in1=bm.rearrange("p (g o) -> p g o", o=1).broadcast_to([128, NB, 16]), op=ALU.mult))
    P.op("vector", lambda e: e.tensor_tensor(out=q_ap, in0=acc, in1=sg, op=ALU.mult))


def emit_l2ffn(P, IN):
    x = IN["x"]
    wg = IN["wg"]
    wu = IN["wu"]
    wd = IN["wd"]
    ident = IN["ident"]
    xo = IN["xo"]
    if "qwgT" in IN:
        qwgT, qwuT, qwdT = IN["qwgT"], IN["qwuT"], IN["qwdT"]
        quant_done = IN["quant_done"]
    else:
        qwgT = P.dtmp([D, 2816], name="qwgT")
        qwuT = P.dtmp([D, 2816], name="qwuT")
        qwdT = P.dtmp([2816, D], name="qwdT")
        quant_done = False

    HID = 2816
    NHD = HID // 128
    KD = D // 128
    TC = 512
    NTC = TOK // TC

    idsb = P.sb([128, 128], name="idsb")
    wrow = P.sb([128, HID], name="wrow")
    scr = P.sb([128, 4 * 1472], name="scr")
    scrn = P.sb([128, 2 * (1472 // 16)], name="scrn")
    xsb = P.sb([128, 4, D], name="xsb")
    xn = P.sb([128, D], name="xn")
    sq = P.sb([128, D], name="sq")
    stats = P.sb([128, 16], name="stats")
    qxT = P.sb([128, KD, TC], name="qxT")
    wfm2 = P.sb([128, NHD, 128], name="wfm2")
    wfmB = P.sb([128, NHD, 128], name="wfmB")
    wfm = wfm2
    htok = P.sb([128, 4, HID], name="htok")
    hq = htok
    hqT = P.sb([128, NHD, TC], name="hqT")
    hs = P.sb([128, TC], name="hs")
    dsb = P.sb([128, TC], name="dsb")

    psG = P.ps([128, 512], name="psG")
    psU = P.ps([128, 512], name="psU")
    psT = P.ps([128, 128], name="psT")

    P.dma(idsb[:, :], ident)

    def quant_weight(wsrc, qdstT, rows, cols):
        for rc in range(rows // 128):
            P.dma(wrow[:, 0:cols], wsrc.ap()[128 * rc:128 * (rc + 1), :])
            emit_fp4_quant(P, wrow[:, 0:cols], wrow[:, 0:cols], cols, scr, scrn)
            for cc in range(cols // 128):
                emit_transpose(P, idsb, wrow[:, 128 * cc:128 * (cc + 1)], psT, dsb[:, 0:128])
                P.dma(qdstT.ap()[128 * cc:128 * (cc + 1), 128 * rc:128 * (rc + 1)], dsb[:, 0:128], engine="gpsimd")

    if not quant_done:
        quant_weight(wg, qwgT, 2816, D)
        quant_weight(wu, qwuT, 2816, D)
        quant_weight(wd, qwdT, D, 2816)

    xr = x.rearrange("(tc n p) d -> tc p n d", p=128, n=4)
    xor_ = xo.rearrange("(tc n p) d -> tc p n d", p=128, n=4)
    for tc in range(NTC):
        P.dma(xsb[:, :, :], xr[tc])
        for t in range(4):
            emit_layernorm(P, xsb[:, t, :], xn[:, :], sq, stats)
            emit_fp4_quant(P, xn[:, :], xn[:, :], D, scr, scrn)
            for c in range(KD):
                emit_transpose(P, idsb, xn[:, 128 * c:128 * (c + 1)], psT,
                               qxT[:, c, 128 * t:128 * (t + 1)])
        _m0 = P.mark(); mkq = [_m0, _m0]
        for oc in range(NHD):
            P.dma(wfm[:, 0:KD, :], qwgT.ap()[:, 128 * oc:128 * (oc + 1)].rearrange("(c p) m -> p c m", p=128), after=mkq[0], site="l2f", pf=0)
            P.dma(wfmB[:, 0:KD, :], qwuT.ap()[:, 128 * oc:128 * (oc + 1)].rearrange("(c p) m -> p c m", p=128), after=mkq[1], site="l2f", pf=1)
            P.pfuse("l2f", 0)
            emit_mm_group(P, psG[:, :], [wfm[:, k, :] for k in range(KD)], [qxT[:, k, :] for k in range(KD)])
            mkq[0] = P.mark()
            P.pfuse("l2f", 1)
            emit_mm_group(P, psU[:, :], [wfmB[:, k, :] for k in range(KD)], [qxT[:, k, :] for k in range(KD)])
            mkq[1] = P.mark()
            P.op("scalar", lambda e: e.activation(out=hs[:, :], in_=psG[:, :], func=AF.Silu))
            P.op("vector", lambda e: e.tensor_tensor(out=hs[:, :], in0=hs[:, :], in1=psU[:, :], op=ALU.mult))
            for t in range(4):
                P.op("tensor", lambda e, _t=t: e.transpose(psT[:, :], hs[:, 128 * _t:128 * (_t + 1)], idsb[:, :]))
                P.op("vector", lambda e, _t=t, _oc=oc: e.tensor_copy(htok[:, _t, 128 * _oc:128 * (_oc + 1)], psT[:, :]))
        for t in range(4):
            emit_fp4_quant(P, htok[:, t, :], htok[:, t, :], HID, scr, scrn)
            for c in range(NHD):
                emit_transpose(P, idsb, htok[:, t, 128 * c:128 * (c + 1)], psT,
                               hqT[:, c, 128 * t:128 * (t + 1)])
        _m0 = P.mark(); mkq = [_m0, _m0]
        for oc2 in range(KD):
            b = oc2 % 2
            wsel = wfm2 if b == 0 else wfmB
            P.dma(wsel[:, :, :], qwdT.ap()[:, 128 * oc2:128 * (oc2 + 1)].rearrange("(c p) m -> p c m", p=128), after=mkq[b], site="l2f", pf=b)
            P.pfuse("l2f", b)
            emit_mm_group(P, psG[:, :], [wsel[:, k, :] for k in range(NHD)], [hqT[:, k, :] for k in range(NHD)])
            mkq[b] = P.mark()
            P.op("vector", lambda e: e.tensor_scalar(out=dsb[:, :], in0=psG[:, :], scalar1=0.5, scalar2=None, op0=ALU.mult))
            for t in range(4):
                P.op("tensor", lambda e, _t=t: e.transpose(psT[:, :], dsb[:, 128 * _t:128 * (_t + 1)], idsb[:, :]))
                P.op("vector", lambda e, _t=t, _oc2=oc2: e.tensor_tensor(
                    out=htok[:, _t, 128 * _oc2:128 * (_oc2 + 1)],
                    in0=xsb[:, _t, 128 * _oc2:128 * (_oc2 + 1)], in1=psT[:, :], op=ALU.add))
        P.dma(xor_[tc], htok[:, :, 0:D], engine="gpsimd")


def build_l2ffn():
    P = Prog()
    IN = {}
    IN["x"] = P.din("x", [TOK, D]).ap()
    IN["wu"] = P.din("wu", [2816, D])
    IN["wd"] = P.din("wd", [D, 2816])
    IN["ident"] = P.din("ident", [128, 128]).ap()
    IN["xo"] = P.dout("xo", [TOK, D]).ap()
    IN["wg"] = P.din("wg", [2816, D])
    emit_l2ffn(P, IN)
    return P.finalize()

def run_l2ffn(x, wg, wu, wd):
    nc = build_l2ffn()
    shards = _shard_tok(x)
    in_maps = [{"x": s, "wg": wg, "wu": wu, "wd": wd, "ident": _IDENT} for s in shards]
    res = run_bass_kernel_spmd(nc, in_maps, core_ids=list(range(N_CORES)))
    return np.stack([r["xo"].reshape(S, D) for r in res.results])



def emit_transpose_batch(P, ident, src_aps, psb, dst_ap_fn=None, evac_batch=None):
    n = len(src_aps)

    def fn(e):
        inst = None
        for i in range(n):
            inst = e.transpose(psb[:, 128 * i:128 * (i + 1)], src_aps[i], ident[:, :])
        return inst
    P.op("tensor", fn)
    if evac_batch is not None:
        P.op("vector", lambda e: e.tensor_copy(evac_batch, psb[:, 0:128 * n]))
    else:
        for i in range(n):
            P.op("vector", lambda e, _i=i: e.tensor_copy(dst_ap_fn(_i), psb[:, 128 * _i:128 * (_i + 1)]))


def emit_l3attn(P, IN):
    x = IN["x"]
    Wq = IN["Wq"]
    Wk = IN["Wk"]
    Wv = IN["Wv"]
    Wo = IN["Wo"]
    W1 = IN["W1"]
    W2 = IN["W2"]
    ln1w = IN["ln1w"]
    ln2w = IN["ln2w"]
    relb = IN["relb"]
    scale_h = IN["scale_h"]
    ident = IN["ident"]
    ranti = IN["ranti"]
    ET = IN["ET"]
    xo = IN["xo"]

    xrT_d = P.dtmp([D, TOK], name="xrTd")
    ao_d = P.dtmp([D, TOK], name="aod")
    tpad_d = P.dtmp([16, 2304], name="tpadd")

    KD = D // 128
    xsb = P.sb([128, NT, D], name="xsb")
    idsb = P.sb([128, 128], name="idsb")
    rasb = P.sb([128, 128], name="rasb")
    lnwsb = P.sb([128, 2, KD], name="lnwsb")
    scv = P.sb([128, 16], name="scv")
    sq = P.sb([128, D], name="sq")
    stats = P.sb([128, 32], name="stats")
    xr = P.sb([128, D], name="xr")
    wsb = P.sb([128, KD, 128], name="wsb")
    wsb2 = P.sb([128, KD, 128], name="wsb2")
    wsb3 = P.sb([128, KD, 128], name="wsb3")
    tpsb = P.sb([16, S], name="tpsb")
    relbsb = P.sb([32, 16], name="relbsb")
    ETsb = P.sb([32, S], name="ETsb")
    masksb = P.sb([128, 2, 128], name="masksb")
    U = P.sb([128, 21504], name="U")
    qtok = U[:, 0:2048].rearrange("p (n m) -> p n m", m=128)
    ktok = U[:, 2048:4096].rearrange("p (n m) -> p n m", m=128)
    vtok = U[:, 4096:6144].rearrange("p (n m) -> p n m", m=128)
    qfm = U[:, 6144:8192]
    kfm = U[:, 8192:10240]
    arow = U[:, 10240:12288]
    aT = U[:, 12288:14336]
    aosb = U[:, 14336:16384]
    biasW = U[:, 16384:18432].rearrange("p (n m) -> p n m", m=128)
    xrTl = U[:, 18432:19456].rearrange("p (n m) -> p n m", m=128)
    wobuf = U[:, 0:8192].rearrange("p (c o) -> p c o", o=1024)
    aol = U[:, 8192:9216].rearrange("p (c m) -> p c m", m=128)
    y1tmp = U[:, 9216:10240]
    zsb = U[:, 0:2048].rearrange("p (n d) -> p n d", d=1024)
    zT = U[:, 2048:4096].rearrange("p (c m) -> p c m", m=256)
    hfm = U[:, 4096:12288].rearrange("p (c m) -> p c m", m=256)
    w2sb = U[:, 12288:16384].rearrange("p (c m) -> p c m", m=128)
    dsb2 = U[:, 16384:16896]
    y1sb = U[:, 17920:19968].rearrange("p (n d) -> p n d", d=1024)
    w1l = U[:, 19968:20992].rearrange("p (c m) -> p c m", m=128)
    w1lB = P.sb([128, 8, 128], name="w1lB")
    w2sbB = P.sb([128, 32, 128], name="w2sbB")
    psA = P.ps([128, 512], name="psA")
    psB = P.ps([128, 512], name="psB")
    psS = P.ps([128, 2048], name="psS")
    psT4 = P.ps([128, 512], name="psT4")

    P.dma(idsb[:, :], ident)
    P.dma(rasb[:, :], ranti)
    P.dma(lnwsb[:, 0, :], dview(ln1w, 0, [[1, 128], [128, KD]]), allow_slow=True)
    P.dma(lnwsb[:, 1, :], dview(ln2w, 0, [[1, 128], [128, KD]]), allow_slow=True)
    P.dma(scv[:, :], dview(scale_h, 0, [[0, 128], [1, 16]]))
    P.dma(xsb[:, :, :], x.rearrange("(n p) d -> p n d", p=128))
    P.op("gpsimd", lambda e: e.iota(out=masksb.bitcast(I32)[:, 0, :], pattern=[[-1, 128]], base=0, channel_multiplier=1))
    P.op("vector", lambda e: e.tensor_scalar(out=masksb[:, 0, :], in0=masksb.bitcast(I32)[:, 0, :], scalar1=0.0, scalar2=None, op0=ALU.is_gt))
    P.op("vector", lambda e: e.tensor_scalar(out=masksb[:, 1, :], in0=masksb[:, 0, :], scalar1=1e6, scalar2=-1e6, op0=ALU.mult, op1=ALU.add))
    P.dma(relbsb[:, :], relb)
    P.dma(ETsb[:, :], ET)
    for c in range(4):
        emit_mm_group(P, psA[0:16, :], [relbsb[:, :]], [ETsb[:, 512 * c:512 * (c + 1)]])
        P.op("vector", lambda e, _c=c: e.tensor_copy(tpsb[:, 512 * _c:512 * (_c + 1)], psA[0:16, :]))
    P.op("vector", lambda e: e.memset(sq[0:16, 0:256], 0.0))
    P.dma(tpad_d.ap()[:, 128:2176], tpsb[:, :], engine="gpsimd")
    P.dma(tpad_d.ap()[:, 0:128], sq[0:16, 0:128], engine="gpsimd")
    P.dma(tpad_d.ap()[:, 2176:2304], sq[0:16, 128:256], engine="gpsimd")

    for t in range(NT):
        emit_layernorm(P, xsb[:, t, :], xr[:, :], sq, stats)
        emit_rmsnorm(P, xr[:, :], xr[:, :], sq, stats, 1e-6, D)
        for g in range(2):
            emit_transpose_batch(P, idsb, [xr[:, 128 * (4 * g + i):128 * (4 * g + i + 1)] for i in range(4)],
                                 psT4, evac_batch=sq[:, 0:512])
            P.dma(dview(xrT_d, (4 * g * 128) * TOK + 128 * t, [[TOK, 128], [128 * TOK, 4], [1, 128]]),
                  sq[:, 0:512].rearrange("p (c m) -> p c m", m=128), engine="gpsimd")

    for hc in range(KD):
        for w_dram, w_sb in ((Wq, wsb), (Wk, wsb2), (Wv, wsb3)):
            P.dma(w_sb[:, :, :], w_dram.ap()[:, 128 * hc:128 * (hc + 1)].rearrange("(c p) m -> p c m", p=128))
        for c in range(KD):
            for w_sb in (wsb, wsb2, wsb3):
                P.op("vector", lambda e, _c=c, _w=w_sb: e.tensor_scalar(out=_w[:, _c, :], in0=_w[:, _c, :], scalar1=lnwsb[:, 0, _c:_c + 1], scalar2=None, op0=ALU.mult))
        for t in range(NT):
            P.dma(xrTl[:, :, :], dview(xrT_d, 128 * t, [[TOK, 128], [128 * TOK, KD], [1, 128]]))
            emit_mm_group(P, psA[:, 0:128], [xrTl[:, c, :] for c in range(KD)], [wsb[:, c, :] for c in range(KD)])
            emit_mm_group(P, psA[:, 128:256], [xrTl[:, c, :] for c in range(KD)], [wsb2[:, c, :] for c in range(KD)])
            emit_mm_group(P, psA[:, 256:384], [xrTl[:, c, :] for c in range(KD)], [wsb3[:, c, :] for c in range(KD)])
            for (off, dst) in ((0, qtok), (128, ktok)):
                P.op("scalar", lambda e, _o=off: e.activation(out=sq[:, 0:128], in_=psA[:, _o:_o + 128], func=AF.Square))
                P.op("vector", lambda e: e.tensor_reduce(out=stats[:, 8:10], in_=sq[:, 0:128].rearrange("p (h d) -> p h d", d=64), axis=AX.X, op=ALU.add))
                P.op("vector", lambda e: e.tensor_scalar(out=stats[:, 8:10], in0=stats[:, 8:10], scalar1=1e-6, scalar2=None, op0=ALU.add))
                P.op("scalar", lambda e: e.activation(out=stats[:, 8:10], in_=stats[:, 8:10], func=AF.Sqrt))
                P.op("vector", lambda e: e.reciprocal(out=stats[:, 8:10], in_=stats[:, 8:10]))
                P.op("vector", lambda e, _o=off, _d=dst, _t=t: e.tensor_tensor(
                    out=_d[:, _t, :].rearrange("p (h d) -> p h d", d=64),
                    in0=psA[:, _o:_o + 128].rearrange("p (h d) -> p h d", d=64),
                    in1=stats[:, 8:10].rearrange("p (h o) -> p h o", o=1).broadcast_to([128, 2, 64]), op=ALU.mult))
            P.op("vector", lambda e, _t=t: e.tensor_copy(vtok[:, _t, :], psA[:, 256:384]))
        for t4 in range(NT // 4):
            emit_transpose_batch(P, idsb, [qtok[:, 4 * t4 + i, :] for i in range(4)], psT4,
                                 evac_batch=qfm[:, 512 * t4:512 * (t4 + 1)])
            emit_transpose_batch(P, idsb, [ktok[:, 4 * t4 + i, :] for i in range(4)], psT4,
                                 evac_batch=kfm[:, 512 * t4:512 * (t4 + 1)])
        for hh in range(2):
            h = 2 * hc + hh
            rb = hh * 64
            for dtile in range(NT):
                P.dma(biasW[:, dtile, :], dview(tpad_d, h * 2304 + 1 + 128 * dtile, [[1, 128], [1, 128]]))
            for it in range(NT):
                jts = list(range(NT)) if it == 0 else list(range(it + 1))
                mm_jts = [0] if it == 0 else jts

                def qk_bias_fn(e, _it=it, _jts=tuple(mm_jts), _rb=rb):
                    inst = None
                    for jt in _jts:
                        e.matmul(psS[:, 128 * jt:128 * (jt + 1)],
                                 qfm[_rb:_rb + 64, 128 * _it:128 * (_it + 1)],
                                 kfm[_rb:_rb + 64, 128 * jt:128 * (jt + 1)],
                                 start=True, stop=False)
                        inst = e.matmul(psS[:, 128 * jt:128 * (jt + 1)],
                                        biasW[:, _it - jt, :], rasb[:, :],
                                        start=False, stop=True)
                    return inst
                P.op("tensor", qk_bias_fn)
                mmw = 128 * len(mm_jts)
                P.op("vector", lambda e, _w=mmw, _h=h: e.tensor_scalar(out=arow[:, 0:_w], in0=psS[:, 0:_w], scalar1=scv[:, _h:_h + 1], scalar2=None, op0=ALU.mult))
                P.op("vector", lambda e, _it=it: e.tensor_tensor(out=arow[:, 128 * _it:128 * (_it + 1)], in0=arow[:, 128 * _it:128 * (_it + 1)], in1=masksb[:, 0, :], op=ALU.mult))
                P.op("vector", lambda e, _it=it: e.tensor_tensor(out=arow[:, 128 * _it:128 * (_it + 1)], in0=arow[:, 128 * _it:128 * (_it + 1)], in1=masksb[:, 1, :], op=ALU.add))
                if it == 0:
                    P.op("vector", lambda e: e.memset(arow[:, 128:2048], -1e6))
                elif it < NT - 1:
                    P.op("vector", lambda e, _it=it: e.memset(arow[:, 128 * (_it + 1):2048], -1e6))
                P.op("vector", lambda e: e.tensor_reduce(out=stats[:, 12:13], in_=arow[:, :], axis=AX.X, op=ALU.max))
                P.op("vector", lambda e: e.tensor_scalar(out=stats[:, 13:14], in0=stats[:, 12:13], scalar1=-1.0, scalar2=None, op0=ALU.mult))
                P.op("scalar", lambda e: e.activation(out=arow[:, :], in_=arow[:, :], func=AF.Exp, bias=stats[:, 13:14], accum_out=stats[:, 14:15]))
                P.op("vector", lambda e: e.reciprocal(out=stats[:, 14:15], in_=stats[:, 14:15]))
                P.op("vector", lambda e: e.tensor_scalar(out=arow[:, :], in0=arow[:, :], scalar1=stats[:, 14:15], scalar2=None, op0=ALU.mult))
                kts = jts
                for b0 in range(0, len(kts), 4):
                    grp = kts[b0:b0 + 4]
                    emit_transpose_batch(P, idsb, [arow[:, 128 * kt:128 * (kt + 1)] for kt in grp], psT4,
                                         evac_batch=aT[:, 128 * grp[0]:128 * (grp[0] + len(grp))])

                def av_fn(e, _kts=tuple(kts), _rb=rb):
                    inst = None
                    for i, kt in enumerate(_kts):
                        inst = e.matmul(psB[0:64, 0:128], vtok[:, kt, _rb:_rb + 64], aT[:, 128 * kt:128 * (kt + 1)],
                                        start=(i == 0), stop=(i == len(_kts) - 1))
                    return inst
                P.op("tensor", av_fn)
                P.op("vector", lambda e, _rb=rb, _it=it: e.tensor_copy(aosb[_rb:_rb + 64, 128 * _it:128 * (_it + 1)], psB[0:64, 0:128]))
        P.dma(ao_d.ap()[128 * hc:128 * (hc + 1), :], aosb[:, :], engine="gpsimd")

    P.dma(wobuf[:, :, :], Wo.ap().rearrange("(c p) o -> p c o", p=128))
    for t in range(NT):
        P.dma(aol[:, :, :], dview(ao_d, 128 * t, [[TOK, 128], [128 * TOK, KD], [1, 128]]))
        emit_layernorm(P, xsb[:, t, :], xr[:, :], sq, stats)
        for oh in range(2):
            emit_mm_group(P, psA[:, :], [aol[:, c, :] for c in range(KD)],
                          [wobuf[:, c, 512 * oh:512 * (oh + 1)] for c in range(KD)])
            P.op("vector", lambda e, _oh=oh: e.tensor_tensor(
                out=y1tmp[:, 512 * _oh:512 * (_oh + 1)], in0=xr[:, 512 * _oh:512 * (_oh + 1)],
                in1=psA[:, :], op=ALU.add))
        P.dma(dview(xrT_d, t * 128 * D, [[D, 128], [1, D]]), y1tmp[:, :], engine="gpsimd")

    for tcD in range(8):
        P.dma(y1sb[:, :, :], dview(xrT_d, tcD * 2 * 128 * D, [[D, 128], [128 * D, 2], [1, D]]))
        for t in range(2):
            emit_rmsnorm(P, y1sb[:, t, :], zsb[:, t, :], sq, stats, 1e-6, D)
        for t in range(2):
            for g in range(2):
                emit_transpose_batch(P, idsb, [zsb[:, t, 128 * (4 * g + i):128 * (4 * g + i + 1)] for i in range(4)], psT4,
                                     dst_ap_fn=lambda i, _t=t, _g=g: zT[:, 4 * _g + i, 128 * _t:128 * (_t + 1)])
        for c in range(KD):
            P.op("vector", lambda e, _c=c: e.tensor_scalar(out=zT[:, _c, :], in0=zT[:, _c, :], scalar1=lnwsb[:, 1, _c:_c + 1], scalar2=None, op0=ALU.mult))
        _m0 = P.mark(); mk3 = [_m0, _m0]
        for oc in range(32):
            b = oc % 2
            w1sel = w1l if b == 0 else w1lB
            P.dma(w1sel[:, :, :], W1.ap()[:, 128 * oc:128 * (oc + 1)].rearrange("(c p) m -> p c m", p=128), after=mk3[b], site="l3a", pf=b)
            P.pfuse("l3a", b)
            emit_mm_group(P, psA[:, 0:256], [w1sel[:, c, :] for c in range(KD)], [zT[:, c, :] for c in range(KD)])
            mk3[b] = P.mark()
            P.op("vector", lambda e, _oc=oc: e.tensor_scalar(out=hfm[:, _oc, :], in0=psA[:, 0:256], scalar1=0.0, scalar2=None, op0=ALU.max))
        for t in range(2):
            tt_ = 2 * tcD + t
            P.op("vector", lambda e, _t=t, _tt=tt_: e.tensor_tensor(out=y1sb[:, _t, :], in0=y1sb[:, _t, :], in1=xsb[:, _tt, :], op=ALU.add))
        _m0 = P.mark(); mk4 = [_m0, _m0]
        for oc2 in range(KD):
            b = oc2 % 2
            w2sel = w2sb if b == 0 else w2sbB
            P.dma(w2sel[:, :, :], W2.ap()[:, 128 * oc2:128 * (oc2 + 1)].rearrange("(c p) m -> p c m", p=128), after=mk4[b], site="l3a", pf=b)
            P.pfuse("l3a", b)
            emit_mm_group(P, psA[:, 0:256], [w2sel[:, c, :] for c in range(32)], [hfm[:, c, :] for c in range(32)])
            mk4[b] = P.mark()
            P.op("vector", lambda e: e.tensor_copy(dsb2[:, 0:256], psA[:, 0:256]))
            for t in range(2):
                P.op("tensor", lambda e, _t=t: e.transpose(psT4[:, 0:128], dsb2[:, 128 * _t:128 * (_t + 1)], idsb[:, :]))
                P.op("vector", lambda e, _t=t, _oc2=oc2: e.tensor_tensor(
                    out=zsb[:, _t, 128 * _oc2:128 * (_oc2 + 1)],
                    in0=y1sb[:, _t, 128 * _oc2:128 * (_oc2 + 1)], in1=psT4[:, 0:128], op=ALU.add))
        P.dma(xo.rearrange("(tc n p) d -> tc p n d", p=128, n=2)[tcD], zsb[:, :, :], engine="gpsimd")


def build_l3attn():
    P = Prog()
    IN = {}
    IN["x"] = P.din("x", [TOK, D]).ap()
    IN["Wq"] = P.din("Wq", [D, D])
    IN["Wk"] = P.din("Wk", [D, D])
    IN["Wv"] = P.din("Wv", [D, D])
    IN["Wo"] = P.din("Wo", [D, D])
    IN["W1"] = P.din("W1", [D, 4096])
    IN["W2"] = P.din("W2", [4096, D])
    IN["ln1w"] = P.din("ln1w", [1, D])
    IN["ln2w"] = P.din("ln2w", [1, D])
    IN["relb"] = P.din("relb", [32, 16]).ap()
    IN["scale_h"] = P.din("scale_h", [1, 16])
    IN["ident"] = P.din("ident", [128, 128]).ap()
    IN["ranti"] = P.din("ranti", [128, 128]).ap()
    IN["ET"] = P.din("ET", [32, S]).ap()
    IN["xo"] = P.dout("xo", [TOK, D]).ap()
    emit_l3attn(P, IN)
    return P.finalize()

def _bucket_rel(d):
    d = np.asarray(d)
    max_exact = 16
    is_small = d < max_exact
    safe = np.maximum(d, 1).astype(np.float32)
    large = max_exact + (np.log(safe / np.float32(max_exact)) / np.float32(math.log(128.0 / max_exact)) * 16).astype(np.int32)
    large = np.minimum(large, 31)
    return np.where(is_small, d, large)


def _build_ET():
    d = np.arange(S)
    b = _bucket_rel(d)
    ET = np.zeros((32, S), np.float32)
    ET[b, d] = 1.0
    return ET


_RANTI = np.eye(128, dtype=np.float32)[::-1].copy()


def run_l3attn(x, Wq, Wk, Wv, rel_bias, scale, Wo, ln1w, ln2w, W1, W2):
    nc = build_l3attn()
    shards = _shard_tok(x)
    ET = _build_ET()
    in_maps = [{"x": s, "Wq": Wq, "Wk": Wk, "Wv": Wv, "Wo": Wo, "W1": W1, "W2": W2,
                "ln1w": ln1w.reshape(1, D), "ln2w": ln2w.reshape(1, D),
                "relb": rel_bias, "scale_h": scale.reshape(1, 16),
                "ident": _IDENT, "ranti": _RANTI, "ET": ET} for s in shards]
    res = run_bass_kernel_spmd(nc, in_maps, core_ids=list(range(N_CORES)))
    return np.stack([r["xo"].reshape(S, D) for r in res.results])



_L0_OFF = [0, 2048, 3072, 3584, 3840, 3968]
_L0_N = [2048, 1024, 512, 256, 128, 64]


def emit_l0attn(P, IN):
    x = IN["x"]
    Wq = IN["Wq"]
    Wk = IN["Wk"]
    Wv = IN["Wv"]
    bq = IN["bq"]
    bk = IN["bk"]
    bv = IN["bv"]
    ident = IN["ident"]
    mdiag = IN["mdiag"]
    msib = IN["msib"]
    xo = IN["xo"]

    xnT_d = P.dtmp([D, TOK], name="xnTd")
    ao_d = P.dtmp([TOK, D], name="aod")

    KD = D // 128
    xsb = P.sb([128, NT, D], name="xsb")
    idsb = P.sb([128, 128], name="idsb")
    mdsb = P.sb([128, 2, 128], name="mdsb")
    mssb = P.sb([128, 2, 128], name="mssb")
    bsb = P.sb([1, 3, D], name="bsb")
    onesb = P.sb([1, 512], name="onesb")
    sq = P.sb([128, D], name="sq")
    stats = P.sb([128, 32], name="stats")
    xr = P.sb([128, D], name="xr")
    xnTl = P.sb([128, KD, 512], name="xnTl")
    wq_sb = P.sb([128, KD, 64], name="wq_sb")
    wk_sb = P.sb([128, KD, 64], name="wk_sb")
    wv_sb = P.sb([128, KD, 64], name="wv_sb")
    qfm = P.sb([64, 4096], name="qfm")
    kfm = P.sb([64, 4096], name="kfm")
    vfm = P.sb([64, 4096], name="vfm")
    vtok = P.sb([128, 32, 65], name="vtok")
    Ybuf = P.sb([65, 4096], name="Ybuf")
    arow = P.sb([128, 128], name="arow")
    aTs = P.sb([128, 128], name="aTs")
    ytk = P.sb([128, 65], name="ytk")
    psS = P.ps([128, 128], name="psS")
    psT = P.ps([128, 512], name="psT")
    psY = P.ps([128, 128], name="psY")
    psQ = P.ps([64, 512], name="psQ")
    psK = P.ps([64, 512], name="psK")

    P.dma(idsb[:, :], ident)
    P.dma(mdsb[:, :, :], mdiag)
    P.dma(mssb[:, :, :], msib)
    P.dma(bsb[:, 0, :], bq)
    P.dma(bsb[:, 1, :], bk)
    P.dma(bsb[:, 2, :], bv)
    P.op("vector", lambda e: e.memset(onesb[:, :], 1.0))
    P.op("vector", lambda e: e.memset(vtok[:, :, 64:65].rearrange("p a b -> p (a b)"), 1.0))
    P.dma(xsb[:, :, :], x.rearrange("(n p) d -> p n d", p=128))

    for t in range(NT):
        emit_layernorm(P, xsb[:, t, :], xr[:, :], sq, stats)
        for g in range(2):
            emit_transpose_batch(P, idsb, [xr[:, 128 * (4 * g + i):128 * (4 * g + i + 1)] for i in range(4)],
                                 psT, evac_batch=sq[:, 0:512])
            P.dma(dview(xnT_d, (4 * g * 128) * TOK + 128 * t, [[TOK, 128], [128 * TOK, 4], [1, 128]]),
                  sq[:, 0:512].rearrange("p (c m) -> p c m", m=128), engine="gpsimd")

    for h in range(16):
        hb = 64 * h
        P.dma(wq_sb[:, :, :], Wq.ap()[:, hb:hb + 64].rearrange("(c p) m -> p c m", p=128))
        P.dma(wk_sb[:, :, :], Wk.ap()[:, hb:hb + 64].rearrange("(c p) m -> p c m", p=128))
        P.dma(wv_sb[:, :, :], Wv.ap()[:, hb:hb + 64].rearrange("(c p) m -> p c m", p=128))
        for ch in range(4):
            P.dma(xnTl[:, :, :], dview(xnT_d, 512 * ch, [[TOK, 128], [128 * TOK, KD], [1, 512]]))

            def qk_fn(e, _ps, _w, _bcol, _hb=hb):
                for c in range(KD):
                    e.matmul(_ps[:, :], _w[:, c, :], xnTl[:, c, :], start=(c == 0), stop=False)
                return e.matmul(_ps[:, :], bsb[:, _bcol, _hb:_hb + 64], onesb[:, :], start=False, stop=True)
            P.op("tensor", lambda e, f=qk_fn: f(e, psQ, wq_sb, 0))
            P.op("vector", lambda e, _ch=ch: e.tensor_scalar(out=qfm[:, 512 * _ch:512 * (_ch + 1)], in0=psQ[:, :], scalar1=0.125, scalar2=None, op0=ALU.mult))
            P.op("tensor", lambda e, f=qk_fn: f(e, psK, wk_sb, 1))
            P.op("vector", lambda e, _ch=ch: e.tensor_copy(kfm[:, 512 * _ch:512 * (_ch + 1)], psK[:, :]))
            P.op("tensor", lambda e, f=qk_fn: f(e, psQ, wv_sb, 2))
            P.op("vector", lambda e, _ch=ch: e.tensor_copy(vfm[:, 512 * _ch:512 * (_ch + 1)], psQ[:, :]))
        for l in range(5):
            o0, n0, o1 = _L0_OFF[l], _L0_N[l], _L0_OFF[l + 1]
            n1 = n0 // 2
            for buf, avg in ((qfm, True), (kfm, True), (vfm, False)):
                P.op("vector", lambda e, _b=buf, _o0=o0, _o1=o1, _n1=n1: e.tensor_tensor(
                    out=_b[:, _o1:_o1 + _n1],
                    in0=_b[:, _o0:_o0 + 2 * _n1].rearrange("p (a two) -> p a two", two=2)[:, :, 0],
                    in1=_b[:, _o0:_o0 + 2 * _n1].rearrange("p (a two) -> p a two", two=2)[:, :, 1], op=ALU.add))
                if avg:
                    P.op("vector", lambda e, _b=buf, _o1=o1, _n1=n1: e.tensor_scalar(out=_b[:, _o1:_o1 + _n1], in0=_b[:, _o1:_o1 + _n1], scalar1=0.5, scalar2=None, op0=ALU.mult))
        cidx = 0
        chunk_list = []
        for l in range(6):
            o, n = _L0_OFF[l], _L0_N[l]
            for c0 in range(0, n, 128):
                npos = min(128, n - c0)
                chunk_list.append((l, c0, npos, cidx))
                P.op("tensor", lambda e, _o=o, _c0=c0, _n=npos: e.transpose(psY[0:_n, 0:64], vfm[:, _o + _c0:_o + _c0 + _n], idsb[0:64, 0:64]))
                P.op("vector", lambda e, _i=cidx, _n=npos: e.tensor_copy(vtok[0:_n, _i, 0:64], psY[0:_n, 0:64]))
                cidx += 1
        for (l, c0, npos, vi) in chunk_list:
            o = _L0_OFF[l]
            msk = mdsb if l == 0 else mssb
            P.op("tensor", lambda e, _o=o, _c0=c0, _n=npos: e.matmul(
                psS[0:_n, 0:_n], qfm[:, _o + _c0:_o + _c0 + _n], kfm[:, _o + _c0:_o + _c0 + _n], start=True, stop=True))
            P.op("vector", lambda e, _m=msk, _n=npos: e.tensor_tensor(out=arow[0:_n, 0:_n], in0=psS[0:_n, 0:_n], in1=_m[0:_n, 0, 0:_n], op=ALU.mult))
            P.op("vector", lambda e, _m=msk, _n=npos: e.tensor_tensor(out=arow[0:_n, 0:_n], in0=arow[0:_n, 0:_n], in1=_m[0:_n, 1, 0:_n], op=ALU.add))
            P.op("vector", lambda e, _n=npos: e.tensor_reduce(out=stats[0:_n, 12:13], in_=arow[0:_n, 0:_n], axis=AX.X, op=ALU.max))
            P.op("vector", lambda e, _n=npos: e.tensor_scalar(out=stats[0:_n, 13:14], in0=stats[0:_n, 12:13], scalar1=-1.0, scalar2=None, op0=ALU.mult))
            P.op("scalar", lambda e, _n=npos: e.activation(out=arow[0:_n, 0:_n], in_=arow[0:_n, 0:_n], func=AF.Exp, bias=stats[0:_n, 13:14]))
            P.op("tensor", lambda e, _n=npos: e.transpose(psY[0:_n, 0:_n], arow[0:_n, 0:_n], idsb[0:_n, 0:_n]))
            P.op("vector", lambda e, _n=npos: e.tensor_copy(aTs[0:_n, 0:_n], psY[0:_n, 0:_n]))
            P.op("tensor", lambda e, _vi=vi, _n=npos: e.matmul(psY[0:65, 0:_n], vtok[0:_n, _vi, 0:65], aTs[0:_n, 0:_n], start=True, stop=True))
            P.op("vector", lambda e, _o=o, _c0=c0, _n=npos: e.tensor_copy(Ybuf[:, _o + _c0:_o + _c0 + _n], psY[0:65, 0:_n]))
        for l in range(4, -1, -1):
            o, n = _L0_OFF[l], _L0_N[l]
            oc, ncn = _L0_OFF[l + 1], _L0_N[l + 1]
            P.op("vector", lambda e, _o=o, _n=n, _oc=oc, _nc=ncn: e.tensor_tensor(
                out=Ybuf[:, _o:_o + _n].rearrange("p (a two) -> p a two", two=2),
                in0=Ybuf[:, _o:_o + _n].rearrange("p (a two) -> p a two", two=2),
                in1=Ybuf[:, _oc:_oc + _nc].rearrange("p (a o) -> p a o", o=1).broadcast_to([65, _nc, 2]), op=ALU.add))
        for c0 in range(0, 2048, 128):
            P.op("tensor", lambda e, _c0=c0: e.transpose(psY[0:128, 0:65], Ybuf[:, _c0:_c0 + 128], idsb[0:65, 0:65]))
            P.op("vector", lambda e: e.tensor_copy(ytk[:, :], psY[0:128, 0:65]))
            P.op("vector", lambda e: e.tensor_scalar(out=stats[:, 16:17], in0=ytk[:, 64:65], scalar1=1e-8, scalar2=None, op0=ALU.add))
            P.op("vector", lambda e: e.reciprocal(out=stats[:, 16:17], in_=stats[:, 16:17]))
            P.op("vector", lambda e: e.tensor_scalar(out=ytk[:, 0:64], in0=ytk[:, 0:64], scalar1=stats[:, 16:17], scalar2=None, op0=ALU.mult))
            P.dma(ao_d.ap()[c0:c0 + 128, hb:hb + 64], ytk[:, 0:64], engine="gpsimd")

    for t in range(NT):
        P.dma(xr[:, :], ao_d.ap()[128 * t:128 * (t + 1), :])
        P.op("vector", lambda e, _t=t: e.tensor_tensor(out=xr[:, :], in0=xr[:, :], in1=xsb[:, _t, :], op=ALU.add))
        P.dma(xo[128 * t:128 * (t + 1), :], xr[:, :], engine="gpsimd")


def build_l0attn():
    P = Prog()
    IN = {}
    IN["x"] = P.din("x", [TOK, D]).ap()
    IN["Wq"] = P.din("Wq", [D, D])
    IN["Wk"] = P.din("Wk", [D, D])
    IN["Wv"] = P.din("Wv", [D, D])
    IN["bq"] = P.din("bq", [1, D]).ap()
    IN["bk"] = P.din("bk", [1, D]).ap()
    IN["bv"] = P.din("bv", [1, D]).ap()
    IN["ident"] = P.din("ident", [128, 128]).ap()
    IN["mdiag"] = P.din("mdiag", [128, 2, 128]).ap()
    IN["msib"] = P.din("msib", [128, 2, 128]).ap()
    IN["xo"] = P.dout("xo", [TOK, D]).ap()
    emit_l0attn(P, IN)
    return P.finalize()

def _l0_masks():
    i = np.arange(128)
    bi = i // 16
    Md = (bi[:, None] == bi[None, :]).astype(np.float32)
    Ms = (bi[:, None] ^ 1 == bi[None, :]).astype(np.float32)
    out = []
    for M in (Md, Ms):
        neg = (1.0 - M) * np.float32(-1e30)
        out.append(np.stack([M, neg], axis=1).astype(np.float32))
    return out


def run_l0attn(x, Wq, bq, Wk, bk, Wv, bv):
    nc = build_l0attn()
    shards = _shard_tok(x)
    Md, Ms = _l0_masks()
    in_maps = [{"x": s, "Wq": Wq, "Wk": Wk, "Wv": Wv,
                "bq": bq.reshape(1, D), "bk": bk.reshape(1, D), "bv": bv.reshape(1, D),
                "ident": _IDENT, "mdiag": Md, "msib": Ms} for s in shards]
    res = run_bass_kernel_spmd(nc, in_maps, core_ids=list(range(N_CORES)))
    return np.stack([r["xo"].reshape(S, D) for r in res.results])



def emit_l1attn(P, IN):
    DI = 2048
    NC = DI // 128
    x = IN["x"]
    WinT = IN["WinT"]
    convw = IN["convw"]
    WxT = IN["WxT"]
    WdtT = IN["WdtT"]
    bdt = IN["bdt"]
    Alog = IN["Alog"]
    Dp = IN["Dp"]
    WoutT = IN["WoutT"]
    normw = IN["normw"]
    ident = IN["ident"]
    xo = IN["xo"]

    u_d = P.dtmp([DI, TOK], name="ud")
    sz_d = P.dtmp([DI, TOK], name="szd")
    delta_d = P.dtmp([DI, TOK], name="deltad")
    du_d = P.dtmp([DI, TOK], name="dud")
    yg_d = P.dtmp([DI, TOK], name="ygd")
    BT_d = P.dtmp([64, TOK], name="BTd")
    CT_d = P.dtmp([64, TOK], name="CTd")

    KD = D // 128
    idsb = P.sb([128, 128], name="idsb")
    onesb = P.sb([1, 512], name="onesb")
    xtile = P.sb([128, D], name="xtile")
    sq = P.sb([128, D], name="sq")
    stats = P.sb([128, 32], name="stats")
    lnw = P.sb([128, KD, 1], name="lnw")
    wst = P.sb([128, KD, 128], name="wst")
    cwsb = P.sb([128, 8], name="cwsb")
    Btok = P.sb([128, NT, 64], name="Btok")
    Ctok = P.sb([128, NT, 64], name="Ctok")
    BT = P.sb([64, TOK], name="BT")
    CT = P.sb([64, TOK], name="CT")
    A4 = P.sb([128, 4, 64], name="A4")
    Bbc1_t = P.sb([128, 2048], name="Bbc1")
    Cbc1_t = P.sb([128, 2048], name="Cbc1")
    dpv = P.sb([128, NC], name="dpv")
    bdv = P.sb([128, NC], name="bdv")
    U1 = P.sb([128, 16384], name="U1")
    U2 = P.sb([128, 18432], name="U2")
    xrT = U1[:, 0:16384].rearrange("p (c m) -> p c m", m=2048)
    delta4 = U1[:, 0:8192].rearrange("p (c m) -> p c m", m=2048)
    du4 = U1[:, 8192:16384].rearrange("p (c m) -> p c m", m=2048)
    wout = U1[:, 0:16384].rearrange("p (c m) -> p c m", m=1024)
    xmp = U2[:, 0:2052]
    zst = U2[:, 2052:4100]
    WxS = U2[:, 0:3072].rearrange("p (c m) -> p c m", m=192)
    uload = U2[:, 3072:5120].rearrange("p (c m) -> p c m", m=128)
    xdb = U2[:, 5120:5312]
    WdtS = U2[0:64, 5312:7360].rearrange("p (c m) -> p c m", m=128)
    dtT = U2[0:64, 7360:9408]
    dsb = U2[:, 9408:11456]
    sp1 = U2[:, 11456:13504]
    sp2 = U2[:, 13504:15552]
    uch = U2[:, 15552:17600]
    Bbc2 = [U2[:, 0:2048], Bbc1_t[:, :]]
    Cbc2 = [U2[:, 2048:4096], Cbc1_t[:, :]]
    dA2b_t = P.sb([128, 2048], name="dA2b")
    dA2 = [U2[:, 4096:6144], dA2b_t[:, :]]
    tmpb = U2[:, 6144:8192]
    hb = U2[:, 8192:10240]
    y4 = U2[:, 10240:18432].rearrange("p (c m) -> p c m", m=2048)
    psM = P.ps([128, 512], name="psM")
    psT = P.ps([128, 512], name="psT")
    psB2 = P.ps([128, 512], name="psB2")
    psB4 = P.ps([128, 2048], name="psB4")

    P.dma(idsb[:, :], ident)
    P.op("vector", lambda e: e.memset(onesb[:, :], 1.0))
    P.dma(lnw[:, :, 0:1], dview(normw, 0, [[1, 128], [128, KD]]), allow_slow=True)
    P.dma(dpv[:, :], dview(Dp, 0, [[1, 128], [128, NC]]), allow_slow=True)
    P.dma(bdv[:, :], dview(bdt, 0, [[1, 128], [128, NC]]), allow_slow=True)

    for t in range(NT):
        P.dma(xtile[:, :], x[128 * t:128 * (t + 1), :])
        emit_layernorm(P, xtile[:, :], xtile[:, :], sq, stats)
        emit_rmsnorm(P, xtile[:, :], xtile[:, :], sq, stats, 1.1920928955078125e-07, D)
        for g in range(2):
            emit_transpose_batch(P, idsb, [xtile[:, 128 * (4 * g + i):128 * (4 * g + i + 1)] for i in range(4)],
                                 psT, dst_ap_fn=lambda i, _g=g, _t=t: xrT[:, 4 * _g + i, 128 * _t:128 * (_t + 1)])

    for oc in range(2 * NC):
        P.dma(wst[:, :, :], WinT.ap()[:, 128 * oc:128 * (oc + 1)].rearrange("(c p) m -> p c m", p=128))
        for c in range(KD):
            P.op("vector", lambda e, _c=c: e.tensor_scalar(out=wst[:, _c, :], in0=wst[:, _c, :], scalar1=lnw[:, _c, 0:1], scalar2=None, op0=ALU.mult))
        if oc < NC:
            P.op("vector", lambda e: e.memset(xmp[:, 0:3], 0.0))
            for tc in range(4):
                emit_mm_group(P, psM[:, :], [wst[:, c, :] for c in range(KD)], [xrT[:, c, 512 * tc:512 * (tc + 1)] for c in range(KD)])
                P.op("vector", lambda e, _tc=tc: e.tensor_copy(xmp[:, 3 + 512 * _tc:3 + 512 * (_tc + 1)], psM[:, :]))
            P.dma(cwsb[:, 0:5], convw.ap()[128 * oc:128 * (oc + 1), :])
            P.op("vector", lambda e: e.tensor_scalar(out=zst[:, 0:2048], in0=xmp[:, 0:2048], scalar1=cwsb[:, 0:1], scalar2=None, op0=ALU.mult))
            for r in range(1, 4):
                P.op("vector", lambda e, _r=r: e.scalar_tensor_tensor(out=zst[:, 0:2048], in0=xmp[:, _r:_r + 2048], scalar=cwsb[:, _r:_r + 1], in1=zst[:, 0:2048], op0=ALU.mult, op1=ALU.add))
            P.op("vector", lambda e: e.tensor_scalar(out=zst[:, 0:2048], in0=zst[:, 0:2048], scalar1=cwsb[:, 4:5], scalar2=None, op0=ALU.add))
            P.op("scalar", lambda e: e.activation(out=zst[:, 0:2048], in_=zst[:, 0:2048], func=AF.Silu))
            P.dma(u_d.ap()[128 * oc:128 * (oc + 1), :], zst[:, 0:2048], engine="gpsimd")
        else:
            for tc in range(4):
                emit_mm_group(P, psM[:, :], [wst[:, c, :] for c in range(KD)], [xrT[:, c, 512 * tc:512 * (tc + 1)] for c in range(KD)])
                P.op("scalar", lambda e, _tc=tc: e.activation(out=zst[:, 512 * _tc:512 * (_tc + 1)], in_=psM[:, :], func=AF.Silu))
            P.dma(sz_d.ap()[128 * (oc - NC):128 * (oc - NC + 1), :], zst[:, 0:2048], engine="gpsimd")

    P.dma(WxS[:, :, :], WxT.ap().rearrange("(c p) m -> p c m", p=128))
    for t in range(NT):
        P.dma(uload[:, :, :], dview(u_d, 128 * t, [[TOK, 128], [128 * TOK, NC], [1, 128]]))
        emit_mm_group(P, psM[:, 0:192], [uload[:, c, :] for c in range(NC)], [WxS[:, c, :] for c in range(NC)])
        P.op("vector", lambda e: e.tensor_copy(xdb[:, :], psM[:, 0:192]))
        P.op("tensor", lambda e: e.transpose(psT[0:64, 0:128], xdb[:, 0:64], idsb[:, :]))
        P.op("vector", lambda e, _t=t: e.tensor_copy(dtT[:, 128 * _t:128 * (_t + 1)], psT[0:64, 0:128]))
        P.op("vector", lambda e, _t=t: e.tensor_copy(Btok[:, _t, :], xdb[:, 64:128]))
        P.op("vector", lambda e, _t=t: e.tensor_copy(Ctok[:, _t, :], xdb[:, 128:192]))
    for t in range(NT):
        P.op("tensor", lambda e, _t=t: e.transpose(psT[0:64, 0:128], Btok[:, _t, :], idsb[:, :]))
        P.op("vector", lambda e, _t=t: e.tensor_copy(BT[:, 128 * _t:128 * (_t + 1)], psT[0:64, 0:128]))
        P.op("tensor", lambda e, _t=t: e.transpose(psT[0:64, 0:128], Ctok[:, _t, :], idsb[:, :]))
        P.op("vector", lambda e, _t=t: e.tensor_copy(CT[:, 128 * _t:128 * (_t + 1)], psT[0:64, 0:128]))
    P.dma(BT_d.ap()[:, :], BT[:, :], engine="gpsimd")
    P.dma(CT_d.ap()[:, :], CT[:, :], engine="gpsimd")

    P.dma(WdtS[:, :, :], WdtT.ap().rearrange("o (c m) -> o c m", m=128))
    for cc in range(NC):
        for tc in range(4):
            P.op("tensor", lambda e, _cc=cc, _tc=tc: e.matmul(psM[:, :], WdtS[:, _cc, :], dtT[:, 512 * _tc:512 * (_tc + 1)], start=True, stop=True))
            P.op("vector", lambda e, _tc=tc: e.tensor_copy(dsb[:, 512 * _tc:512 * (_tc + 1)], psM[:, :]))
        P.op("vector", lambda e, _cc=cc: e.tensor_scalar(out=dsb[:, :], in0=dsb[:, :], scalar1=bdv[:, _cc:_cc + 1], scalar2=None, op0=ALU.add))
        P.op("scalar", lambda e: e.activation(out=sp1[:, :], in_=dsb[:, :], func=AF.Abs))
        P.op("scalar", lambda e: e.activation(out=sp1[:, :], in_=sp1[:, :], func=AF.Exp, scale=-1.0))
        P.op("vector", lambda e: e.tensor_scalar(out=sp1[:, :], in0=sp1[:, :], scalar1=1.0, scalar2=None, op0=ALU.add))
        P.op("scalar", lambda e: e.activation(out=sp1[:, :], in_=sp1[:, :], func=AF.Ln))
        P.op("vector", lambda e: e.tensor_scalar(out=sp2[:, :], in0=dsb[:, :], scalar1=0.0, scalar2=None, op0=ALU.max))
        P.op("vector", lambda e: e.tensor_tensor(out=dsb[:, :], in0=sp1[:, :], in1=sp2[:, :], op=ALU.add))
        P.dma(delta_d.ap()[128 * cc:128 * (cc + 1), :], dsb[:, :], engine="gpsimd")
        P.dma(uch[:, :], u_d.ap()[128 * cc:128 * (cc + 1), :])
        P.op("vector", lambda e: e.tensor_tensor(out=dsb[:, :], in0=dsb[:, :], in1=uch[:, :], op=ALU.mult))
        P.dma(du_d.ap()[128 * cc:128 * (cc + 1), :], dsb[:, :], engine="gpsimd")

    P.settag("zScan")
    for g4 in range(NC // 4):
        for j in range(4):
            cc = 4 * g4 + j
            P.dma(delta4[:, j, :], delta_d.ap()[128 * cc:128 * (cc + 1), :])
            P.dma(du4[:, j, :], du_d.ap()[128 * cc:128 * (cc + 1), :])
            P.dma(A4[:, j, :], Alog.ap()[128 * cc:128 * (cc + 1), :])
        mbc_rows = [P.mark(), P.mark()]
        P.op("scalar", lambda e: e.activation(out=A4[:, :, :], in_=A4[:, :, :], func=AF.Exp))
        P.op("vector", lambda e: e.tensor_scalar(out=A4[:, :, :], in0=A4[:, :, :], scalar1=-1.0, scalar2=None, op0=ALU.mult))
        _mm = P.mark(); mexp = [_mm, _mm]
        for n in range(64):
            b2 = n % 2
            P.dma(Bbc2[b2], dview(BT_d, n * TOK, [[0, 128], [1, TOK]]), after=mbc_rows[b2], site="mam", pf=4 + b2)
            P.dma(Cbc2[b2], dview(CT_d, n * TOK, [[0, 128], [1, TOK]]), after=mbc_rows[b2], site="mam", pf=4 + b2)
            first = True
            for j in range(4):
                jb = j % 2
                P.op("scalar", lambda e, _j=j, _n=n, _b=jb: e.activation(out=dA2[_b][:, :], in_=delta4[:, _j, :], func=AF.Exp, scale=A4[:, _j, _n:_n + 1]),
                     after=mexp[jb], site="mam", pf=jb)
                if first:
                    P.pfuse("mam", 4 + b2)
                    first = False
                P.op("vector", lambda e, _j=j, _b2=b2: e.tensor_tensor(out=tmpb[:, :], in0=du4[:, _j, :], in1=Bbc2[_b2], op=ALU.mult))
                P.pfuse("mam", jb, unit=1)
                P.op("vector", lambda e, _b=jb: e.tensor_tensor_scan(out=hb[:, :], data0=dA2[_b][:, :], data1=tmpb[:, :], initial=0.0, op0=ALU.mult, op1=ALU.add))
                mexp[jb] = P.mark()
                P.op("vector", lambda e, _b2=b2: e.tensor_tensor(out=hb[:, :], in0=hb[:, :], in1=Cbc2[_b2], op=ALU.mult))
                if n == 0:
                    P.op("vector", lambda e, _j=j: e.tensor_copy(y4[:, _j, :], hb[:, :]))
                else:
                    P.op("vector", lambda e, _j=j: e.tensor_tensor(out=y4[:, _j, :], in0=y4[:, _j, :], in1=hb[:, :], op=ALU.add))
            mbc_rows[b2] = P.mark()
        for j in range(4):
            cc = 4 * g4 + j
            P.dma(dA2[0][:, :], u_d.ap()[128 * cc:128 * (cc + 1), :])
            P.op("vector", lambda e, _j=j, _cc=cc: e.scalar_tensor_tensor(out=y4[:, _j, :], in0=dA2[0][:, :], scalar=dpv[:, _cc:_cc + 1], in1=y4[:, _j, :], op0=ALU.mult, op1=ALU.add))
            P.dma(dA2[0][:, :], sz_d.ap()[128 * cc:128 * (cc + 1), :])
            P.op("vector", lambda e, _j=j: e.tensor_tensor(out=y4[:, _j, :], in0=y4[:, _j, :], in1=dA2[0][:, :], op=ALU.mult))
            P.dma(yg_d.ap()[128 * cc:128 * (cc + 1), :], y4[:, j, :], engine="gpsimd")

    P.settag("zOut")
    P.dma(wout[:, :, :], WoutT.ap().rearrange("(c p) m -> p c m", p=128))
    for t in range(NT):
        P.dma(uload[:, :, :], dview(yg_d, 128 * t, [[TOK, 128], [128 * TOK, NC], [1, 128]]))
        P.dma(xtile[:, :], x[128 * t:128 * (t + 1), :])
        emit_layernorm(P, xtile[:, :], Bbc2[0][:, 0:1024], sq, stats)
        P.op("vector", lambda e: e.tensor_tensor(out=xtile[:, :], in0=xtile[:, :], in1=Bbc2[0][:, 0:1024], op=ALU.add))
        for oh in range(2):
            emit_mm_group(P, psM[:, :], [uload[:, c, :] for c in range(NC)], [wout[:, c, 512 * oh:512 * (oh + 1)] for c in range(NC)])
            P.op("vector", lambda e, _oh=oh: e.tensor_tensor(out=xtile[:, 512 * _oh:512 * (_oh + 1)], in0=xtile[:, 512 * _oh:512 * (_oh + 1)], in1=psM[:, :], op=ALU.add))
        P.dma(xo[128 * t:128 * (t + 1), :], xtile[:, :], engine="gpsimd")


def build_l1attn():
    P = Prog()
    DI = 2048
    IN = {}
    IN["x"] = P.din("x", [TOK, D]).ap()
    IN["WxT"] = P.din("WxT", [DI, 192])
    IN["WdtT"] = P.din("WdtT", [64, DI])
    IN["bdt"] = P.din("bdt", [DI, 1])
    IN["Alog"] = P.din("Alog", [DI, 64])
    IN["Dp"] = P.din("Dp", [DI, 1])
    IN["WoutT"] = P.din("WoutT", [DI, D])
    IN["normw"] = P.din("normw", [1, D])
    IN["ident"] = P.din("ident", [128, 128]).ap()
    IN["xo"] = P.dout("xo", [TOK, D]).ap()
    IN["WinT"] = P.din("WinT", [D, 2 * DI])
    IN["convw"] = P.din("convw", [DI, 5])
    emit_l1attn(P, IN)
    return P.finalize()

def run_l1attn(x, norm_w, W_in, conv_w, conv_b, W_x, W_dt, b_dt, A_log, Dp_, W_out):
    nc = build_l1attn()
    shards = _shard_tok(x)
    WinT = np.ascontiguousarray(W_in.T)
    cw = np.concatenate([conv_w.reshape(2048, 4), conv_b.reshape(2048, 1)], axis=1).astype(np.float32)
    WxT = np.ascontiguousarray(W_x.T)
    WdtT = np.ascontiguousarray(W_dt.T)
    WoutT = np.ascontiguousarray(W_out.T)
    in_maps = [{"x": s, "WinT": WinT, "convw": cw, "WxT": WxT, "WdtT": WdtT,
                "bdt": b_dt.reshape(2048, 1), "Alog": A_log, "Dp": Dp_.reshape(2048, 1),
                "WoutT": WoutT, "normw": norm_w.reshape(1, D), "ident": _IDENT} for s in shards]
    res = run_bass_kernel_spmd(nc, in_maps, core_ids=list(range(N_CORES)))
    return np.stack([r["xo"].reshape(S, D) for r in res.results])



def emit_l2attn(P, IN):
    H = 16
    x = IN["x"]
    wqkvT = IN["wqkvT"]
    wpT = IN["wpT"]
    bproj = IN["bproj"]
    rkeT = IN["rkeT"]
    rqeT = IN["rqeT"]
    rk02 = IN["rk02"]
    rq02 = IN["rq02"]
    rv0in = IN["rv0in"]
    rvmid = IN["rvmid"]
    G1 = IN["G1"].ap()
    G2 = IN["G2"].ap()
    EXP3 = IN["EXP3"].ap()
    ident = IN["ident"]
    xo = IN["xo"]

    xnT_d = P.dtmp([D, TOK], name="xnTd")
    ao_d = P.dtmp([D, TOK], name="aod")
    qst_d = P.dtmp([TOK, 512], name="qstd")
    kst_d = P.dtmp([TOK, 512], name="kstd")
    wnd_d = P.dtmp([128, 512], name="wndd")

    KD = D // 128
    xsb = P.sb([128, NT, D], name="xsb")
    idsb = P.sb([128, 128], name="idsb")
    sq = P.sb([128, D], name="sq")
    stats = P.sb([128, 32], name="stats")
    xr = P.sb([128, D], name="xr")
    xnTl = P.sb([128, KD, 512], name="xnTl")
    wsl = P.sb([128, KD, 192], name="wsl")
    G1sb = P.sb([63, 512], name="G1sb")
    G2sb = P.sb([63, 512], name="G2sb")
    EXP3sb = P.sb([63, 384], name="EXP3sb")
    rtab = P.sb([63, 64], name="rtab")
    r02 = P.sb([64, 4], name="r02")
    rv2 = P.sb([1, 3, 64], name="rv2")
    RKE = P.sb([64, 512], name="RKE")
    RQE = P.sb([64, 512], name="RQE")
    RVx = P.sb([128, 3, 64], name="RVx")
    qaug = P.sb([67, TOK], name="qaug")
    kaug = P.sb([67, TOK], name="kaug")
    vtok = P.sb([128, NT, 64], name="vtok")
    qlc62 = P.sb([128, NT, 1], name="qlc62")
    stg = P.sb([2, TOK], name="stg")
    tm2 = P.sb([128, 2], name="tm2")
    onesr = P.sb([1, 128], name="onesr")
    ones128 = P.sb([128, 1], name="ones128")
    zrow = P.sb([128, 256], name="zrow")
    arow = P.sb([128, TOK], name="arow")
    aT = P.sb([128, TOK], name="aT")
    Pext = P.sb([128, 384], name="Pext")
    gq = P.sb([128, 384], name="gq")
    gqQ = P.sb([128, 2, 384], name="gqQ")
    gqK = P.sb([128, 2, 3, 128], name="gqK")
    s62a = P.sb([1, 128], name="s62a")
    wsuf = P.sb([128, 1], name="wsuf")
    aosb = P.sb([128, 64], name="aosb")
    aosb2 = P.sb([64, 128], name="aosb2")
    wpsb = P.sb([128, KD, D], name="wpsb")
    bpsb = P.sb([1, D], name="bpsb")
    aol = P.sb([128, KD, 128], name="aol")
    psA = P.ps([128, 512], name="psA")
    psS = P.ps([128, 2048], name="psS")
    psT4 = P.ps([128, 512], name="psT4")
    psV = P.ps([128, 64], name="psV")
    psR = P.ps([1, 512], name="psR")

    P.dma(idsb[:, :], ident)
    P.dma(G1sb[:, :], G1)
    P.dma(G2sb[:, :], G2)
    P.dma(EXP3sb[:, :], EXP3)
    P.dma(xsb[:, :, :], x.rearrange("(n p) d -> p n d", p=128))
    P.op("vector", lambda e: e.memset(onesr[:, :], 1.0))
    P.op("vector", lambda e: e.memset(ones128[:, :], 1.0))
    P.op("vector", lambda e: e.memset(zrow[:, :], 0.0))

    for t in range(NT):
        emit_layernorm(P, xsb[:, t, :], xr[:, :], sq, stats)
        for g in range(2):
            emit_transpose_batch(P, idsb, [xr[:, 128 * (4 * g + i):128 * (4 * g + i + 1)] for i in range(4)],
                                 psT4, evac_batch=sq[:, 0:512])
            P.dma(dview(xnT_d, (4 * g * 128) * TOK + 128 * t, [[TOK, 128], [128 * TOK, 4], [1, 128]]),
                  sq[:, 0:512].rearrange("p (c m) -> p c m", m=128), engine="gpsimd")

    for h in range(H):
        P.dma(rtab[:, :], rkeT.ap()[h])
        P.op("tensor", lambda e: e.matmul(psA[0:64, :], rtab[:, :], G1sb[:, :], start=True, stop=True))
        P.op("vector", lambda e: e.tensor_copy(RKE[:, :], psA[0:64, :]))
        P.dma(rtab[:, :], rqeT.ap()[h])
        P.op("tensor", lambda e: e.matmul(psA[0:64, :], rtab[:, :], G2sb[:, :], start=True, stop=True))
        P.op("vector", lambda e: e.tensor_scalar(out=RQE[:, :], in0=psA[0:64, :], scalar1=0.125, scalar2=None, op0=ALU.mult))
        P.dma(r02[:, 0:2], rk02.ap()[h])
        P.dma(r02[:, 2:4], rq02.ap()[h])
        P.op("vector", lambda e: e.tensor_scalar(out=r02[:, 2:4], in0=r02[:, 2:4], scalar1=0.125, scalar2=None, op0=ALU.mult))
        P.dma(rv2[:, 0:2, :].rearrange("a b c -> a (b c)"), rv0in.ap()[h].rearrange("b c -> (b c)").unsqueeze(0))
        P.op("vector", lambda e: e.tensor_tensor(out=rv2[:, 2, :], in0=rv2[:, 1, :], in1=rv2[:, 0, :], op=ALU.subtract))
        P.dma(rtab[:, :], rvmid.ap()[h])
        for uc in range(3):
            P.op("tensor", lambda e, _uc=uc: e.matmul(psA[0:128, 0:64], EXP3sb[:, 128 * _uc:128 * (_uc + 1)], rtab[:, :], start=True, stop=True))
            P.op("vector", lambda e, _uc=uc: e.tensor_copy(RVx[:, _uc, :], psA[0:128, 0:64]))
        P.op("vector", lambda e: e.memset(qaug[64:67, :], 1.0))
        P.op("vector", lambda e: e.memset(kaug[64:67, :], 1.0))
        for t in range(4):
            P.dma(xnTl[:, :, :], dview(xnT_d, 512 * t, [[TOK, 128], [128 * TOK, KD], [1, 512]]))
            if t == 0:
                P.dma(wsl[:, :, 0:64], wqkvT.ap()[:, 64 * h:64 * h + 64].rearrange("(c p) m -> p c m", p=128))
                P.dma(wsl[:, :, 64:128], wqkvT.ap()[:, D + 64 * h:D + 64 * h + 64].rearrange("(c p) m -> p c m", p=128))
                P.dma(wsl[:, :, 128:192], wqkvT.ap()[:, 2 * D + 64 * h:2 * D + 64 * h + 64].rearrange("(c p) m -> p c m", p=128))
            emit_mm_group(P, psA[0:64, :], [wsl[:, c, 0:64] for c in range(KD)], [xnTl[:, c, :] for c in range(KD)])
            P.op("vector", lambda e, _t=t: e.tensor_scalar(out=qaug[0:64, 512 * _t:512 * (_t + 1)], in0=psA[0:64, :], scalar1=0.125, scalar2=None, op0=ALU.mult))
            emit_mm_group(P, psA[0:64, :], [wsl[:, c, 64:128] for c in range(KD)], [xnTl[:, c, :] for c in range(KD)])
            P.op("vector", lambda e, _t=t: e.tensor_copy(kaug[0:64, 512 * _t:512 * (_t + 1)], psA[0:64, :]))
            for tt_ in range(4):
                emit_mm_group(P, psA[:, 0:64], [xnTl[:, c, 128 * tt_:128 * (tt_ + 1)] for c in range(KD)], [wsl[:, c, 128:192] for c in range(KD)])
                P.op("vector", lambda e, _t=t, _tt=tt_: e.tensor_copy(vtok[:, 4 * _t + _tt, :], psA[:, 0:64]))
        for t in range(NT):
            P.op("tensor", lambda e, _t=t: e.matmul(psA[:, 0:2], qaug[0:64, 128 * _t:128 * (_t + 1)], r02[:, 0:2], start=True, stop=True))
            P.op("vector", lambda e: e.tensor_copy(tm2[:, :], psA[:, 0:2]))
            P.op("vector", lambda e: e.tensor_tensor(out=tm2[:, 1:2], in0=tm2[:, 1:2], in1=tm2[:, 0:1], op=ALU.subtract))
            P.op("vector", lambda e, _t=t: e.tensor_copy(qlc62[:, _t, :], tm2[:, 1:2]))
            P.op("tensor", lambda e: e.transpose(psT4[0:2, 0:128], tm2[:, :], idsb[:, :]))
            P.op("vector", lambda e, _t=t: e.tensor_copy(stg[:, 128 * _t:128 * (_t + 1)], psT4[0:2, 0:128]))
        P.dma(qaug[64:65, :], stg[0:1, :], engine="gpsimd")
        for t in range(NT):
            P.op("tensor", lambda e, _t=t: e.matmul(psA[:, 0:2], kaug[0:64, 128 * _t:128 * (_t + 1)], r02[:, 2:4], start=True, stop=True))
            P.op("vector", lambda e: e.tensor_copy(tm2[:, :], psA[:, 0:2]))
            P.op("vector", lambda e: e.tensor_tensor(out=tm2[:, 1:2], in0=tm2[:, 1:2], in1=tm2[:, 0:1], op=ALU.subtract))
            P.op("tensor", lambda e: e.transpose(psT4[0:2, 0:128], tm2[:, :], idsb[:, :]))
            P.op("vector", lambda e, _t=t: e.tensor_copy(stg[:, 128 * _t:128 * (_t + 1)], psT4[0:2, 0:128]))
        P.dma(kaug[65:67, :], stg[:, :], engine="gpsimd")
        for t in range(NT):
            P.op("tensor", lambda e, _t=t: e.matmul(psA[:, :], qaug[0:64, 128 * _t:128 * (_t + 1)], RKE[:, :], start=True, stop=True))
            P.op("vector", lambda e: e.tensor_copy(sq[:, 0:512], psA[:, :]))
            P.dma(qst_d.ap()[128 * t:128 * (t + 1), :], sq[:, 0:512], engine="gpsimd")
        for t in range(NT):
            P.op("tensor", lambda e, _t=t: e.matmul(psA[:, :], kaug[0:64, 128 * _t:128 * (_t + 1)], RQE[:, :], start=True, stop=True))
            P.op("vector", lambda e: e.tensor_copy(sq[:, 0:512], psA[:, :]))
            P.dma(kst_d.ap()[128 * t:128 * (t + 1), :], sq[:, 0:512], engine="gpsimd")

        def _near(itf):
            return [jt for jt in (itf - 1, itf, itf + 1) if 0 <= jt < NT]

        def pf_issue(itf, aft):
            nearf = _near(itf)
            w0f = 128 * nearf[0]
            widthf = 128 * len(nearf)
            nnf = len(nearf)
            P.dma(gqQ[:, itf % 2, 0:widthf],
                  dview(qst_d, (128 * itf) * 512 + w0f - 128 * itf + 256, [[511, 128], [1, widthf]]),
                  after=aft, site="l2a", pf=itf % 2)
            P.dma(gqK[:, itf % 2, 0:nnf, :],
                  dview(kst_d, (128 * nearf[0]) * 511 + 255 + 128 * itf, [[511, 128], [128 * 511, nnf], [1, 128]]),
                  after=aft, site="l2a", pf=itf % 2)

        mqk = P.mark()
        pf_issue(0, mqk)
        pf_issue(1, mqk)
        for it in range(NT):
            near = _near(it)
            slab = list(range(it + 2, NT))

            def qk_fn(e, _it=it, _slab=tuple(slab)):
                inst = None
                for jt in range(NT):
                    kk = 67 if jt in _slab else 66
                    inst = e.matmul(psS[:, 128 * jt:128 * (jt + 1)],
                                    qaug[0:kk, 128 * _it:128 * (_it + 1)],
                                    kaug[0:kk, 128 * jt:128 * (jt + 1)],
                                    start=True, stop=True)
                return inst
            P.op("tensor", qk_fn)
            P.op("vector", lambda e: e.tensor_copy(arow[:, :], psS[:, :]))
            if slab:
                P.op("vector", lambda e, _it=it, _s0=128 * (it + 2): e.tensor_scalar(
                    out=arow[:, _s0:2048], in0=arow[:, _s0:2048], scalar1=qlc62[:, _it, 0:1], scalar2=None, op0=ALU.add))
            w0 = 128 * (near[0])
            wn = 128 * len(near) + 256 if False else 128 * (len(near) + 2)
            width = 128 * len(near) + 0
            P.pfuse("l2a", it % 2)
            P.op("vector", lambda e, _w0=w0, _w=width, _p=it % 2: e.tensor_tensor(out=arow[:, _w0:_w0 + _w], in0=arow[:, _w0:_w0 + _w], in1=gqQ[:, _p, 0:_w], op=ALU.add))
            nn = len(near)

            def ktr(e, _nn=nn, _p=it % 2):
                inst = None
                for ji in range(_nn):
                    inst = e.transpose(psT4[:, 128 * ji:128 * (ji + 1)], gqK[:, _p, ji, :], idsb[:, :])
                return inst
            P.op("tensor", ktr)
            P.op("vector", lambda e, _n0=near[0], _nn=nn: e.tensor_tensor(
                out=arow[:, 128 * _n0:128 * (_n0 + _nn)], in0=arow[:, 128 * _n0:128 * (_n0 + _nn)],
                in1=psT4[:, 0:128 * _nn], op=ALU.add))
            if it + 2 < NT:
                pf_issue(it + 2, P.mark())
            P.op("vector", lambda e: e.tensor_reduce(out=stats[:, 12:13], in_=arow[:, :], axis=AX.X, op=ALU.max))
            P.op("vector", lambda e: e.tensor_scalar(out=stats[:, 13:14], in0=stats[:, 12:13], scalar1=-1.0, scalar2=None, op0=ALU.mult))
            P.op("scalar", lambda e: e.activation(out=arow[:, :], in_=arow[:, :], func=AF.Exp, bias=stats[:, 13:14], accum_out=stats[:, 14:15]))
            P.op("vector", lambda e: e.reciprocal(out=stats[:, 14:15], in_=stats[:, 14:15]))
            P.op("vector", lambda e: e.tensor_scalar(out=arow[:, :], in0=arow[:, :], scalar1=stats[:, 14:15], scalar2=None, op0=ALU.mult))
            cw0 = 0 if it > 0 else 128
            if cw0 > 0:
                P.dma(wnd_d.ap()[:, 0:cw0], zrow[:, 0:cw0], engine="gpsimd")
            P.dma(wnd_d.ap()[:, cw0:cw0 + 128 * len(near)], arow[:, 128 * near[0]:128 * (near[0] + len(near))], engine="gpsimd")
            cend = cw0 + 128 * len(near)
            zhi = 512 if (h == 0 and it == 0) else (384 if it == NT - 1 else cend)
            if cend < zhi:
                P.dma(wnd_d.ap()[:, cend:zhi], zrow[:, 0:zhi - cend], engine="gpsimd")
            P.dma(Pext[:, :], dview(wnd_d, 0, [[513, 128], [1, 384]]), after=P.mark(), site="l2a", pf=2)
            for b0 in range(0, NT, 4):
                emit_transpose_batch(P, idsb, [arow[:, 128 * (b0 + i):128 * (b0 + i + 1)] for i in range(4)], psT4,
                                     evac_batch=aT[:, 128 * b0:128 * (b0 + 4)])
            if slab:
                def sl_fn(e, _slab=tuple(slab)):
                    inst = None
                    for i_, jt in enumerate(_slab):
                        inst = e.matmul(psR[0:1, 0:128], ones128[:, :], aT[:, 128 * jt:128 * (jt + 1)],
                                        start=(i_ == 0), stop=(i_ == len(_slab) - 1))
                    return inst
                P.op("tensor", sl_fn)
                P.op("vector", lambda e: e.tensor_copy(s62a[:, :], psR[0:1, 0:128]))
            else:
                P.op("vector", lambda e: e.memset(s62a[:, :], 0.0))
            P.pfuse("l2a", 2)
            P.op("vector", lambda e: e.tensor_reduce(out=wsuf[:, :], in_=Pext[:, 159:384], axis=AX.X, op=ALU.add))
            P.op("tensor", lambda e: e.transpose(psR[0:1, 0:128], wsuf[:, :], idsb[:, :]))
            P.op("vector", lambda e: e.tensor_tensor(out=s62a[:, :], in0=s62a[:, :], in1=psR[0:1, 0:128], op=ALU.add))
            emit_transpose_batch(P, idsb, [Pext[:, 128 * i:128 * (i + 1)] for i in range(3)], psT4,
                                 evac_batch=gq[:, 0:384])

            def v_fn(e):
                for jt in range(NT):
                    e.matmul(psV[:, :], aT[:, 128 * jt:128 * (jt + 1)], vtok[:, jt, :], start=(jt == 0), stop=False)
                for uc in range(3):
                    e.matmul(psV[:, :], gq[:, 0:384].rearrange("p (c m) -> p c m", m=128)[:, uc, :] if False else gq[:, 128 * uc:128 * (uc + 1)], RVx[:, uc, :], start=False, stop=False)
                e.matmul(psV[:, :], onesr[:, :], rv2[:, 0, :], start=False, stop=False)
                return e.matmul(psV[:, :], s62a[:, :], rv2[:, 2, :], start=False, stop=True)
            P.op("tensor", v_fn)
            P.op("vector", lambda e: e.tensor_copy(aosb[:, :], psV[:, :]))
            P.op("tensor", lambda e: e.transpose(psT4[0:64, 0:128], aosb[:, :], idsb[:, :]))
            P.op("vector", lambda e: e.tensor_copy(aosb2[0:64, :], psT4[0:64, 0:128]))
            P.dma(ao_d.ap()[64 * h:64 * h + 64, 128 * it:128 * (it + 1)], aosb2[0:64, :], engine="gpsimd")

    P.dma(wpsb[:, :, :], wpT.ap().rearrange("(c p) o -> p c o", p=128))
    P.dma(bpsb[:, :], bproj.ap())
    for t in range(NT):
        P.dma(aol[:, :, :], dview(ao_d, 128 * t, [[TOK, 128], [128 * TOK, KD], [1, 128]]))
        for oh in range(2):
            def pj_fn(e, _oh=oh):
                for c in range(KD):
                    e.matmul(psA[:, :], aol[:, c, :], wpsb[:, c, 512 * _oh:512 * (_oh + 1)], start=(c == 0), stop=False)
                return e.matmul(psA[:, :], onesr[:, :], bpsb[:, 512 * _oh:512 * (_oh + 1)], start=False, stop=True)
            P.op("tensor", pj_fn)
            P.op("vector", lambda e, _t=t, _oh=oh: e.tensor_tensor(out=xsb[:, _t, 512 * _oh:512 * (_oh + 1)], in0=xsb[:, _t, 512 * _oh:512 * (_oh + 1)], in1=psA[:, :], op=ALU.add))
        P.dma(xo[128 * t:128 * (t + 1), :], xsb[:, t, :], engine="gpsimd")


def build_l2attn():
    P = Prog()
    H = 16
    IN = {}
    IN["x"] = P.din("x", [TOK, D]).ap()
    IN["bproj"] = P.din("bproj", [1, D])
    IN["rqeT"] = P.din("rqeT", [H, 63, 64])
    IN["rq02"] = P.din("rq02", [H, 64, 2])
    IN["ident"] = P.din("ident", [128, 128]).ap()
    IN["xo"] = P.dout("xo", [TOK, D]).ap()
    IN["wqkvT"] = P.din("wqkvT", [D, 3 * D])
    IN["wpT"] = P.din("wpT", [D, D])
    IN["rkeT"] = P.din("rkeT", [H, 63, 64])
    IN["rk02"] = P.din("rk02", [H, 64, 2])
    IN["rv0in"] = P.din("rv0in", [H, 2, 64])
    IN["rvmid"] = P.din("rvmid", [H, 63, 64])
    IN["G1"] = P.din("G1", [63, 512])
    IN["G2"] = P.din("G2", [63, 512])
    IN["EXP3"] = P.din("EXP3", [63, 384])
    emit_l2attn(P, IN)
    return P.finalize()

def _l2_consts():
    def g(d):
        return np.clip(d, -31, 31) + 31
    u = np.arange(512)
    G1 = np.zeros((63, 512), np.float32)
    b1 = g(u - 256)
    G1[b1, u] = 1.0
    G1[0, :] -= 1.0
    G2 = np.zeros((63, 512), np.float32)
    b2 = g(u - 255)
    G2[b2, u] = 1.0
    G2[62, :] -= 1.0
    u3 = np.arange(384)
    E3 = np.zeros((63, 384), np.float32)
    d3 = u3 - 128
    b3 = g(d3)
    band = (np.abs(d3) <= 30)
    E3[b3[band], u3[band]] = 1.0
    E3[0, u3[band]] -= 1.0
    return G1, G2, E3


def run_l2attn(x, w_qkv, w_proj, b_proj, rpe_q, rpe_k, rpe_v):
    nc = build_l2attn()
    shards = _shard_tok(x)
    G1, G2, E3 = _l2_consts()
    in_maps = [{"x": s,
                "wqkvT": np.ascontiguousarray(w_qkv.T),
                "wpT": np.ascontiguousarray(w_proj.T),
                "bproj": b_proj.reshape(1, D),
                "rkeT": np.ascontiguousarray(np.swapaxes(rpe_k, 1, 2)),
                "rqeT": np.ascontiguousarray(np.swapaxes(rpe_q, 1, 2)),
                "rk02": np.ascontiguousarray(rpe_k[:, :, [0, 62]]),
                "rq02": np.ascontiguousarray(rpe_q[:, :, [62, 0]]),
                "rv0in": np.ascontiguousarray(rpe_v[:, [0, 62], :]),
                "rvmid": np.ascontiguousarray(rpe_v),
                "G1": G1, "G2": G2, "EXP3": E3, "ident": _IDENT} for s in shards]
    res = run_bass_kernel_spmd(nc, in_maps, core_ids=list(range(N_CORES)))
    return np.stack([r["xo"].reshape(S, D) for r in res.results])


def emit_l0ffn_f1(P, IN):
    R = 2048
    x = IN["x"]
    wqkvT = IN["wqkvT"]
    wprojT = IN["wprojT"]
    routW = IN["routW"]
    g1 = IN["g1"]
    b1 = IN["b1"]
    g2 = IN["g2"]
    b2 = IN["b2"]
    m8 = IN["m8"]
    ident = IN["ident"]
    h1o = IN["h1o"]
    ln2o = IN["ln2o"]
    oh1o = IN["oh1o"]
    oh2o = IN["oh2o"]
    p12o = IN["p12o"]

    KD = D // 128
    NTl = R // 128
    idsb = P.sb([128, 128], name="idsb")
    msb = P.sb([128, 2, 128], name="msb")
    xsb = P.sb([128, NTl, D], name="xsb")
    gbc = P.sb([128, 4, D], name="gbc")
    grow = P.sb([1, 4, D], name="grow")
    sq = P.sb([128, D], name="sq")
    stats = P.sb([128, 32], name="stats")
    ln1 = P.sb([128, D], name="ln1")
    lnT = P.sb([128, KD, 128], name="lnT")
    wsb = P.sb([128, KD, 512], name="wsb")
    qkv = P.sb([128, 1536], name="qkv")
    qfm = P.sb([64, 128], name="qfm")
    kfm = P.sb([64, 128], name="kfm")
    arow2 = P.sb([128, 128], name="arow2")
    aT2 = P.sb([128, 128], name="aT2")
    ao = P.sb([128, NTl, D], name="ao")
    onesr = P.sb([1, 512], name="onesr")
    rw = P.sb([128, KD, 8], name="rw")
    lg = P.sb([128, 8], name="lg")
    psA = P.ps([128, 512], name="psA")
    psT = P.ps([128, 512], name="psT")
    psS2 = P.ps([128, 128], name="psS2")

    P.dma(idsb[:, :], ident)
    P.dma(msb[:, :, :], m8)
    P.op("vector", lambda e: e.memset(onesr[:, :], 1.0))
    for i, gb in enumerate((g1, b1, g2, b2)):
        P.dma(grow[:, i, :], gb.ap())
    for i in range(4):
        for c in range(2):
            P.op("tensor", lambda e, _i=i, _c=c: e.matmul(psA[:, :], onesr[0:1, 0:128], grow[:, _i, 512 * _c:512 * (_c + 1)], start=True, stop=True))
            P.op("vector", lambda e, _i=i, _c=c: e.tensor_copy(gbc[:, _i, 512 * _c:512 * (_c + 1)], psA[:, :]))

    for t in range(NTl):
        P.dma(ln1[:, :], x[128 * t:128 * (t + 1), :])
        emit_layernorm(P, ln1[:, :], xsb[:, t, :], sq, stats)

    for t in range(NTl):
        emit_layernorm(P, xsb[:, t, :], ln1[:, :], sq, stats)
        P.op("vector", lambda e: e.tensor_tensor(out=ln1[:, :], in0=ln1[:, :], in1=gbc[:, 0, :], op=ALU.mult))
        P.op("vector", lambda e: e.tensor_tensor(out=ln1[:, :], in0=ln1[:, :], in1=gbc[:, 1, :], op=ALU.add))
        for g in range(2):
            emit_transpose_batch(P, idsb, [ln1[:, 128 * (4 * g + i):128 * (4 * g + i + 1)] for i in range(4)], psT,
                                 dst_ap_fn=lambda i, _g=g: lnT[:, 4 * _g + i, :])
        for oc in range(3):
            P.dma(wsb[:, :, :], wqkvT.ap()[:, 512 * oc:512 * (oc + 1)].rearrange("(c p) m -> p c m", p=128))
            emit_mm_group(P, psA[:, :], [lnT[:, c, :] for c in range(KD)], [wsb[:, c, :] for c in range(KD)])
            P.op("vector", lambda e, _oc=oc: e.tensor_copy(qkv[:, 512 * _oc:512 * (_oc + 1)], psA[:, :]))
        for g4 in range(4):
            base = 384 * g4
            P.op("tensor", lambda e, _b=base: e.transpose(psT[0:64, 0:128], qkv[:, _b + 256:_b + 320], idsb[:, :]))
            P.op("vector", lambda e: e.tensor_copy(kfm[:, :], psT[0:64, 0:128]))
            for hq in range(4):
                qb = base + 64 * hq
                P.op("tensor", lambda e, _qb=qb: e.transpose(psT[0:64, 0:128], qkv[:, _qb:_qb + 64], idsb[:, :]))
                P.op("vector", lambda e: e.tensor_copy(qfm[:, :], psT[0:64, 0:128]))
                P.op("tensor", lambda e: e.matmul(psS2[:, :], qfm[:, :], kfm[:, :], start=True, stop=True))
                P.op("vector", lambda e: e.tensor_scalar(out=arow2[:, :], in0=psS2[:, :], scalar1=0.125, scalar2=None, op0=ALU.mult))
                P.op("vector", lambda e: e.tensor_tensor(out=arow2[:, :], in0=arow2[:, :], in1=msb[:, 0, :], op=ALU.mult))
                P.op("vector", lambda e: e.tensor_tensor(out=arow2[:, :], in0=arow2[:, :], in1=msb[:, 1, :], op=ALU.add))
                P.op("vector", lambda e: e.tensor_reduce(out=stats[:, 12:13], in_=arow2[:, :], axis=AX.X, op=ALU.max))
                P.op("vector", lambda e: e.tensor_scalar(out=stats[:, 13:14], in0=stats[:, 12:13], scalar1=-1.0, scalar2=None, op0=ALU.mult))
                P.op("scalar", lambda e: e.activation(out=arow2[:, :], in_=arow2[:, :], func=AF.Exp, bias=stats[:, 13:14], accum_out=stats[:, 14:15]))
                P.op("vector", lambda e: e.reciprocal(out=stats[:, 14:15], in_=stats[:, 14:15]))
                P.op("vector", lambda e: e.tensor_scalar(out=arow2[:, :], in0=arow2[:, :], scalar1=stats[:, 14:15], scalar2=None, op0=ALU.mult))
                P.op("tensor", lambda e: e.transpose(psS2[:, :], arow2[:, :], idsb[:, :]))
                P.op("vector", lambda e: e.tensor_copy(aT2[:, :], psS2[:, :]))
                P.op("tensor", lambda e, _b=base: e.matmul(psS2[:, 0:64], aT2[:, :], qkv[:, _b + 320:_b + 384], start=True, stop=True))
                P.op("vector", lambda e, _t=t, _g=g4, _h=hq: e.tensor_copy(ao[:, _t, 256 * _g + 64 * _h:256 * _g + 64 * _h + 64], psS2[:, 0:64]))

    for t in range(NTl):
        for g in range(2):
            emit_transpose_batch(P, idsb, [ao[:, t, 128 * (4 * g + i):128 * (4 * g + i + 1)] for i in range(4)], psT,
                                 dst_ap_fn=lambda i, _g=g: lnT[:, 4 * _g + i, :])
        for oc in range(2):
            P.dma(wsb[:, :, :], wprojT.ap()[:, 512 * oc:512 * (oc + 1)].rearrange("(c p) m -> p c m", p=128))
            emit_mm_group(P, psA[:, :], [lnT[:, c, :] for c in range(KD)], [wsb[:, c, :] for c in range(KD)])
            P.op("vector", lambda e, _t=t, _oc=oc: e.tensor_tensor(out=xsb[:, _t, 512 * _oc:512 * (_oc + 1)], in0=xsb[:, _t, 512 * _oc:512 * (_oc + 1)], in1=psA[:, :], op=ALU.add))
        P.dma(h1o[128 * t:128 * (t + 1), :], xsb[:, t, :], engine="gpsimd")
        emit_layernorm(P, xsb[:, t, :], ln1[:, :], sq, stats)
        P.op("vector", lambda e: e.tensor_tensor(out=ln1[:, :], in0=ln1[:, :], in1=gbc[:, 2, :], op=ALU.mult))
        P.op("vector", lambda e: e.tensor_tensor(out=ln1[:, :], in0=ln1[:, :], in1=gbc[:, 3, :], op=ALU.add))
        P.dma(ln2o[128 * t:128 * (t + 1), :], ln1[:, :], engine="gpsimd")
        for g in range(2):
            emit_transpose_batch(P, idsb, [ln1[:, 128 * (4 * g + i):128 * (4 * g + i + 1)] for i in range(4)], psT,
                                 dst_ap_fn=lambda i, _g=g: lnT[:, 4 * _g + i, :])
        if t == 0:
            P.dma(rw[:, :, :], routW.ap().rearrange("(c p) m -> p c m", p=128))
        emit_mm_group(P, psA[:, 0:8], [lnT[:, c, :] for c in range(KD)], [rw[:, c, :] for c in range(KD)])
        P.op("vector", lambda e: e.tensor_copy(lg[:, :], psA[:, 0:8]))
        P.op("vector", lambda e: e.tensor_reduce(out=stats[:, 16:17], in_=lg[:, :], axis=AX.X, op=ALU.max))
        P.op("vector", lambda e: e.tensor_scalar(out=sq[:, 0:8], in0=lg[:, :], scalar1=stats[:, 16:17], scalar2=None, op0=ALU.is_equal))
        P.dma(oh1o[128 * t:128 * (t + 1), :], sq[:, 0:8], engine="gpsimd")
        P.op("vector", lambda e: e.scalar_tensor_tensor(out=sq[:, 8:16], in0=sq[:, 0:8], scalar=-1e30, in1=lg[:, :], op0=ALU.mult, op1=ALU.add))
        P.op("vector", lambda e: e.tensor_reduce(out=stats[:, 17:18], in_=sq[:, 8:16], axis=AX.X, op=ALU.max))
        P.op("vector", lambda e: e.tensor_scalar(out=sq[:, 16:24], in0=sq[:, 8:16], scalar1=stats[:, 17:18], scalar2=None, op0=ALU.is_equal))
        P.dma(oh2o[128 * t:128 * (t + 1), :], sq[:, 16:24], engine="gpsimd")
        P.op("vector", lambda e: e.tensor_tensor(out=stats[:, 18:19], in0=stats[:, 16:17], in1=stats[:, 17:18], op=ALU.subtract))
        P.op("scalar", lambda e: e.activation(out=stats[:, 20:21], in_=stats[:, 18:19], func=AF.Sigmoid))
        P.op("vector", lambda e: e.tensor_scalar(out=stats[:, 19:20], in0=stats[:, 18:19], scalar1=-1.0, scalar2=None, op0=ALU.mult))
        P.op("scalar", lambda e: e.activation(out=stats[:, 21:22], in_=stats[:, 19:20], func=AF.Sigmoid))
        P.dma(p12o[128 * t:128 * (t + 1), :], stats[:, 20:22], engine="gpsimd")


def build_l0ffn_f1():
    P = Prog()
    R = 2048
    IN = {}
    IN["x"] = P.din("x", [R, D]).ap()
    IN["wqkvT"] = P.din("wqkvT", [D, 1536])
    IN["wprojT"] = P.din("wprojT", [D, D])
    IN["routW"] = P.din("routW", [D, 8])
    IN["g1"] = P.din("g1", [1, D])
    IN["b1"] = P.din("b1", [1, D])
    IN["g2"] = P.din("g2", [1, D])
    IN["b2"] = P.din("b2", [1, D])
    IN["m8"] = P.din("m8", [128, 2, 128]).ap()
    IN["ident"] = P.din("ident", [128, 128]).ap()
    IN["h1o"] = P.dout("h1o", [R, D]).ap()
    IN["ln2o"] = P.dout("ln2o", [R, D]).ap()
    IN["oh1o"] = P.dout("oh1o", [R, 8]).ap()
    IN["oh2o"] = P.dout("oh2o", [R, 8]).ap()
    IN["p12o"] = P.dout("p12o", [R, 2]).ap()
    emit_l0ffn_f1(P, IN)
    return P.finalize()

def emit_l0ffn_f3(P, IN):
    NA = 32768
    CAP = 5120
    KD = D // 128
    ln2 = IN["ln2"]
    ass = IN["ass"]
    ohsel = IN["ohsel"]
    pa = IN["pa"]
    tids = IN["tids"]
    esel = IN["esel"]
    w1 = IN["w1"]
    w2 = IN["w2"]
    tris = IN["tris"]
    ident = IN["ident"]
    moeo = IN["moeo"]
    slot_d = P.dtmp([CAP + 128, 2], name="slotd")

    idsb = P.sb([128, 128], name="idsb")
    trisb = P.sb([128, 128], name="trisb")
    ones1x = P.sb([1, 128], name="ones1x")
    ones128c = P.sb([128, 1], name="ones128c")
    eselbc = P.sb([128, 8], name="eselbc")
    eselr = P.sb([1, 8], name="eselr")
    runr = P.sb([1, 8], name="runr")
    assl = P.sb([128, 16, 8], name="assl")
    ohl = P.sb([128, 16], name="ohl")
    pal = P.sb([128, 16], name="pal")
    tidl = P.sb([128, 16], name="tidl")
    tmp8 = P.sb([128, 8], name="tmp8")
    wtile = P.sb([128, 16], name="wtile")
    kvt = P.sb([128, 16], name="kvt")
    sc2 = P.sb([128, 2], name="sc2")
    sci = P.sb([128, 1], I32, name="sci")
    sci16 = P.sb([128, 16], I32, name="sci16")
    w1sb = P.sb([128, KD, 2048], name="w1sb")
    w2sb = P.sb([128, 16, 1024], name="w2sb")
    gath = P.sb([128, D], name="gath")
    gT = P.sb([128, 8, 128], name="gT")
    hfm = P.sb([128, 16, 128], name="hfm")
    outr = P.sb([128, D], name="outr")
    zrow2 = P.sb([128, D], name="zrow2")
    psA = P.ps([128, 512], name="psA")
    psW = P.ps([128, 8], name="psW")
    psR = P.ps([1, 8], name="psR")
    psT = P.ps([128, 512], name="psT")

    P.dma(idsb[:, :], ident)
    P.dma(trisb[:, :], tris)
    P.dma(eselr[:, :], esel)
    P.op("vector", lambda e: e.memset(ones1x[:, :], 1.0))
    P.op("vector", lambda e: e.memset(ones128c[:, :], 1.0))
    P.op("vector", lambda e: e.memset(runr[:, :], 0.0))
    P.op("vector", lambda e: e.memset(zrow2[:, :], 0.0))
    P.op("tensor", lambda e: e.matmul(psW[:, :], ones1x[:, :], eselr[:, :], start=True, stop=True))
    P.op("vector", lambda e: e.tensor_copy(eselbc[:, :], psW[:, :]))

    for t in range(B * S // 128):
        P.dma(moeo[128 * t:128 * (t + 1), :], zrow2[:, :], engine="gpsimd")
    for b0 in range(0, CAP + 128, 128):
        n = min(128, CAP + 128 - b0)
        P.dma(slot_d.ap()[b0:b0 + n, :], zrow2[0:n, 0:2], engine="gpsimd")
    pcol = P.sb([128, 1], name="pcol")
    P.op("gpsimd", lambda e: e.iota(out=sci[:, :], pattern=[[0, 1]], base=CAP, channel_multiplier=1))
    P.op("vector", lambda e: e.tensor_copy(pcol[:, :], sci[:, :]))

    for big in range(16):
        P.dma(assl[:, :, :], ass[2048 * big:2048 * (big + 1), :].rearrange("(c p) e -> p c e", p=128))
        P.dma(ohl[:, :], ohsel[big])
        P.dma(pal[:, :], pa[big])
        P.dma(tidl[:, :], tids[big])
        for j in range(16):
            def pf(e, _j=j):
                e.matmul(psW[:, :], trisb[:, :], assl[:, _j, :], start=True, stop=False)
                return e.matmul(psW[:, :], ones1x[:, :], runr[:, :], start=False, stop=True)
            P.op("tensor", pf)
            P.op("vector", lambda e: e.tensor_tensor(out=tmp8[:, :], in0=psW[:, :], in1=eselbc[:, :], op=ALU.mult))
            P.op("vector", lambda e, _j=j: e.tensor_reduce(out=wtile[:, _j:_j + 1], in_=tmp8[:, :], axis=AX.X, op=ALU.add))
            P.op("tensor", lambda e, _j=j: e.matmul(psR[:, :], ones128c[:, :], assl[:, _j, :], start=True, stop=True))
            P.op("vector", lambda e: e.tensor_tensor(out=runr[:, :], in0=runr[:, :], in1=psR[:, :], op=ALU.add))
        P.op("vector", lambda e: e.tensor_scalar(out=kvt[:, :], in0=wtile[:, :], scalar1=float(CAP), scalar2=None, op0=ALU.is_lt))
        P.op("vector", lambda e: e.tensor_tensor(out=kvt[:, :], in0=kvt[:, :], in1=ohl[:, :], op=ALU.mult))
        P.op("vector", lambda e: e.scalar_tensor_tensor(out=wtile[:, :], in0=wtile[:, :], scalar=pcol[:, :], in1=kvt[:, :], op0=ALU.subtract, op1=ALU.mult))
        P.op("vector", lambda e: e.tensor_scalar(out=wtile[:, :], in0=wtile[:, :], scalar1=pcol[:, :], scalar2=None, op0=ALU.add))
        P.op("vector", lambda e: e.tensor_copy(sci16[:, :], wtile[:, :]))
        for j in range(16):
            P.op("gpsimd", lambda e, _j=j: e.indirect_dma_start(
                out=slot_d.ap()[:, :], out_offset=bass.IndirectOffsetOnAxis(ap=sci16[0:128, _j:_j + 1], axis=0),
                in_=metl[:, _j, 8:10], in_offset=None), weight=16)

    P.dma(w1sb[:, :, :], w1.ap().rearrange("(c p) m -> p c m", p=128))
    P.dma(w2sb[:, :, :], w2.ap().rearrange("(c p) m -> p c m", p=128))
    GC = 0.7978845608028654
    for ch in range(CAP // 128):
        P.dma(sc2[:, :], slot_d.ap()[128 * ch:128 * (ch + 1), :])
        P.op("vector", lambda e: e.tensor_copy(sci[:, :], sc2[:, 0:1]))
        P.op("gpsimd", lambda e: e.indirect_dma_start(
            out=gath[:, :], out_offset=None, in_=ln2.ap(),
            in_offset=bass.IndirectOffsetOnAxis(ap=sci[0:128, 0:1], axis=0)), weight=16)
        for g in range(2):
            emit_transpose_batch(P, idsb, [gath[:, 128 * (4 * g + i):128 * (4 * g + i + 1)] for i in range(4)], psT,
                                 dst_ap_fn=lambda i, _g=g: gT[:, 4 * _g + i, :])
        for fc in range(16):
            emit_mm_group(P, psA[:, 0:128], [w1sb[:, c, 128 * fc:128 * (fc + 1)] for c in range(KD)], [gT[:, c, :] for c in range(KD)])
            P.op("scalar", lambda e: e.activation(out=gsc[:, 0, :], in_=psA[:, 0:128], func=AF.Square))
            P.op("vector", lambda e: e.tensor_scalar(out=gsc[:, 0, :], in0=gsc[:, 0, :], scalar1=0.044715, scalar2=1.0, op0=ALU.mult, op1=ALU.add))
            P.op("vector", lambda e: e.tensor_tensor(out=gsc[:, 0, :], in0=gsc[:, 0, :], in1=psA[:, 0:128], op=ALU.mult))
            P.op("scalar", lambda e: e.activation(out=gsc[:, 1, :], in_=gsc[:, 0, :], func=AF.Tanh, scale=GC))
            P.op("vector", lambda e: e.tensor_scalar(out=gsc[:, 2, :], in0=psA[:, 0:128], scalar1=0.5, scalar2=None, op0=ALU.mult))
            P.op("vector", lambda e, _fc=fc: e.scalar_tensor_tensor(out=hfm[:, _fc, :], in0=gsc[:, 1, :], scalar=1.0, in1=gsc[:, 2, :], op0=ALU.add, op1=ALU.mult))
        for oh in range(2):
            emit_mm_group(P, psA[:, :], [hfm[:, c, :] for c in range(16)], [w2sb[:, c, 512 * oh:512 * (oh + 1)] for c in range(16)])
            P.op("vector", lambda e, _oh=oh: e.tensor_scalar(out=outr[:, 512 * _oh:512 * (_oh + 1)], in0=psA[:, :], scalar1=sc2[:, 1:2], scalar2=None, op0=ALU.mult))
        P.op("gpsimd", lambda e: e.indirect_dma_start(
            out=moeo, out_offset=bass.IndirectOffsetOnAxis(ap=sci[0:128, 0:1], axis=0),
            in_=outr[:, :], in_offset=None, compute_op=ALU.add), weight=16)


def build_l0ffn_f3():
    P = Prog()
    NA = 32768
    IN = {}
    IN["ln2"] = P.din("ln2", [B * S, D])
    IN["ass"] = P.din("ass", [NA, 8]).ap()
    IN["ohsel"] = P.din("ohsel", [16, 128, 16]).ap()
    IN["pa"] = P.din("pa", [16, 128, 16]).ap()
    IN["tids"] = P.din("tids", [16, 128, 16]).ap()
    IN["esel"] = P.din("esel", [1, 8]).ap()
    IN["w1"] = P.din("w1", [D, 2048])
    IN["w2"] = P.din("w2", [2048, D])
    IN["tris"] = P.din("tris", [128, 128]).ap()
    IN["ident"] = P.din("ident", [128, 128]).ap()
    IN["moeo"] = P.dout("moeo", [B * S, D]).ap()
    emit_l0ffn_f3(P, IN)
    return P.finalize()

def emit_l0ffn_f4(P, IN):
    x = IN["x"]
    h1 = IN["h1"]
    moes = [P.din(f"moe{e}", [TOK, D]).ap() for e in range(8)]
    xo = IN["xo"]
    a = P.sb([128, D], name="a")
    bsb = P.sb([128, D], name="bsb")
    for t in range(NT):
        P.dma(a[:, :], h1[128 * t:128 * (t + 1), :])
        for e8 in range(8):
            P.dma(bsb[:, :], moes[e8][128 * t:128 * (t + 1), :])
            P.op("vector", lambda e: e.tensor_tensor(out=a[:, :], in0=a[:, :], in1=bsb[:, :], op=ALU.add))
        P.dma(bsb[:, :], x[128 * t:128 * (t + 1), :])
        P.op("vector", lambda e: e.scalar_tensor_tensor(out=a[:, :], in0=a[:, :], scalar=0.5, in1=bsb[:, :], op0=ALU.mult, op1=ALU.add))
        P.dma(xo[128 * t:128 * (t + 1), :], a[:, :], engine="gpsimd")


def build_l0ffn_f4():
    P = Prog()
    IN = {}
    IN["x"] = P.din("x", [TOK, D]).ap()
    IN["h1"] = P.din("h1", [TOK, D]).ap()
    IN["xo"] = P.dout("xo", [TOK, D]).ap()
    emit_l0ffn_f4(P, IN)
    return P.finalize()

def _l0ffn_consts():
    r = np.arange(128)
    blk = r // 8
    s_ = r % 8
    keep = ((blk[:, None] == blk[None, :]) & (s_[None, :] <= s_[:, None])).astype(np.float32)
    neg = (1.0 - keep) * np.float32(-1e30)
    m8 = np.stack([keep, neg], axis=1).astype(np.float32)
    tris = (np.arange(128)[:, None] < np.arange(128)[None, :]).astype(np.float32)
    return m8, tris


def run_l0ffn(x, ln1w, ln1b, ln2w, ln2b, qkv_w, proj_w, rout_w, moe_w1, moe_w2):
    m8, tris = _l0ffn_consts()
    nc1 = build_l0ffn_f1()
    xs = []
    for c in range(N_CORES):
        sl = x[:, 256 * c:256 * (c + 1), :]
        xs.append(np.ascontiguousarray(np.transpose(sl, (1, 0, 2)).reshape(2048, D)))
    common1 = {"wqkvT": np.ascontiguousarray(qkv_w.T), "wprojT": np.ascontiguousarray(proj_w.T),
               "routW": np.ascontiguousarray(rout_w), "g1": ln1w.reshape(1, D), "b1": ln1b.reshape(1, D),
               "g2": ln2w.reshape(1, D), "b2": ln2b.reshape(1, D), "m8": m8, "ident": _IDENT}
    in1 = [{**common1, "x": s} for s in xs]
    r1 = run_bass_kernel_spmd(nc1, in1, core_ids=list(range(N_CORES)))

    def glob(key, width):
        parts = []
        for c in range(N_CORES):
            a = r1.results[c][key].reshape(256, 8, width)
            parts.append(np.transpose(a, (1, 0, 2)))
        return np.ascontiguousarray(np.concatenate(parts, axis=1).reshape(B * S, width))

    h1g = glob("h1o", D)
    ln2g = glob("ln2o", D)
    oh1 = glob("oh1o", 8)
    oh2 = glob("oh2o", 8)
    p12 = glob("p12o", 2)
    ass = np.ascontiguousarray(np.stack([oh1, oh2], axis=1).reshape(2 * B * S, 8))
    pa = np.ascontiguousarray(p12.reshape(2 * B * S, 1))
    tids = np.ascontiguousarray((np.arange(2 * B * S) // 2).astype(np.float32).reshape(-1, 1))
    nc3 = build_l0ffn_f3()
    in3 = []
    for e in range(N_CORES):
        esel = np.zeros((1, 8), np.float32); esel[0, e] = 1.0
        def _blk(v):
            return np.ascontiguousarray(v.reshape(16, 16, 128).transpose(0, 2, 1))
        in3.append({"ln2": ln2g, "ass": ass, "ohsel": _blk(ass[:, e]),
                    "pa": _blk(pa[:, 0]), "tids": _blk(tids[:, 0]), "esel": esel,
                    "w1": np.ascontiguousarray(moe_w1[e]), "w2": np.ascontiguousarray(moe_w2[e]),
                    "tris": tris, "ident": _IDENT})
    r3 = run_bass_kernel_spmd(nc3, in3, core_ids=list(range(N_CORES)))
    moes = [r3.results[e]["moeo"] for e in range(N_CORES)]
    nc4 = build_l0ffn_f4()
    in4 = []
    for c in range(N_CORES):
        d = {"x": np.ascontiguousarray(x[c].reshape(TOK, D)),
             "h1": np.ascontiguousarray(h1g.reshape(B, S, D)[c])}
        for e in range(8):
            d[f"moe{e}"] = np.ascontiguousarray(moes[e].reshape(B, S, D)[c])
        in4.append(d)
    r4 = run_bass_kernel_spmd(nc4, in4, core_ids=list(range(N_CORES)))
    return np.stack([r["xo"].reshape(S, D) for r in r4.results])




def _npdt(arr):
    if arr.dtype == np.float32:
        return F32
    if arr.dtype == np.int32:
        return I32
    if arr.dtype == np.uint8:
        return U8
    raise ValueError(arr.dtype)


def run_l123_fused(x, I):
    P = Prog()
    host = {}

    def din(name, arr):
        h = P.din(name, list(arr.shape), _npdt(arr))
        host[name] = arr
        return h

    identh = din("c_ident", _IDENT)
    l1f = {
        "k0": din("l1f_k0", np.ascontiguousarray(I["L1_ffn_k0"]).view(np.uint8)),
        "k1": din("l1f_k1", np.ascontiguousarray(I["L1_ffn_k1"]).view(np.uint8)),
        "k2": din("l1f_k2", np.ascontiguousarray(I["L1_ffn_k2"]).view(np.uint8)),
        "s0": din("l1f_s0", np.float32(I["L1_ffn_s0"]).reshape(1, 1)),
        "s1": din("l1f_s1", np.float32(I["L1_ffn_s1"]).reshape(1, 1)),
        "s2": din("l1f_s2", np.float32(I["L1_ffn_s2"]).reshape(1, 1)),
        "ident": identh,
    }
    cw = np.concatenate([I["L1_attn_conv_w"].reshape(2048, 4), I["L1_attn_conv_b"].reshape(2048, 1)], axis=1).astype(np.float32)
    l1a = {
        "WinT": din("l1a_WinT", np.ascontiguousarray(I["L1_attn_W_in"].T)),
        "convw": din("l1a_convw", cw),
        "WxT": din("l1a_WxT", np.ascontiguousarray(I["L1_attn_W_x"].T)),
        "WdtT": din("l1a_WdtT", np.ascontiguousarray(I["L1_attn_W_dt"].T)),
        "bdt": din("l1a_bdt", np.ascontiguousarray(I["L1_attn_b_dt"].reshape(2048, 1))),
        "Alog": din("l1a_Alog", np.ascontiguousarray(I["L1_attn_A_log"])),
        "Dp": din("l1a_Dp", np.ascontiguousarray(I["L1_attn_Dp"].reshape(2048, 1))),
        "WoutT": din("l1a_WoutT", np.ascontiguousarray(I["L1_attn_W_out"].T)),
        "normw": din("l1a_normw", np.ascontiguousarray(I["L1_attn_norm_w"].reshape(1, D))),
        "ident": identh,
    }
    l2f = {
        "wg": din("l2f_wg", np.ascontiguousarray(I["L2_ffn_w_gate"])),
        "wu": din("l2f_wu", np.ascontiguousarray(I["L2_ffn_w_up"])),
        "wd": din("l2f_wd", np.ascontiguousarray(I["L2_ffn_w_down"])),
        "ident": identh,
    }
    G1, G2, E3 = _l2_consts()
    l2a = {
        "wqkvT": din("l2a_wqkvT", np.ascontiguousarray(I["L2_attn_w_qkv"].T)),
        "wpT": din("l2a_wpT", np.ascontiguousarray(I["L2_attn_w_proj"].T)),
        "bproj": din("l2a_bproj", np.ascontiguousarray(I["L2_attn_b_proj"].reshape(1, D))),
        "rkeT": din("l2a_rkeT", np.ascontiguousarray(np.swapaxes(I["L2_attn_rpe_k_table"], 1, 2))),
        "rqeT": din("l2a_rqeT", np.ascontiguousarray(np.swapaxes(I["L2_attn_rpe_q_table"], 1, 2))),
        "rk02": din("l2a_rk02", np.ascontiguousarray(I["L2_attn_rpe_k_table"][:, :, [0, 62]])),
        "rq02": din("l2a_rq02", np.ascontiguousarray(I["L2_attn_rpe_q_table"][:, :, [62, 0]])),
        "rv0in": din("l2a_rv0in", np.ascontiguousarray(I["L2_attn_rpe_v_table"][:, [0, 62], :])),
        "rvmid": din("l2a_rvmid", np.ascontiguousarray(I["L2_attn_rpe_v_table"])),
        "G1": din("l2a_G1", G1), "G2": din("l2a_G2", G2), "EXP3": din("l2a_E3", E3),
        "ident": identh,
    }
    l3f = {
        "wgT": din("l3f_wgT", np.ascontiguousarray(I["L3_ffn_w_gate"].T)),
        "wuT": din("l3f_wuT", np.ascontiguousarray(I["L3_ffn_w_up"].T)),
        "wdT": din("l3f_wdT", np.ascontiguousarray(I["L3_ffn_w_down"].T)),
        "ident": identh,
    }
    l3a = {
        "Wq": din("l3a_Wq", np.ascontiguousarray(I["L3_attn_Wq"])),
        "Wk": din("l3a_Wk", np.ascontiguousarray(I["L3_attn_Wk"])),
        "Wv": din("l3a_Wv", np.ascontiguousarray(I["L3_attn_Wv"])),
        "Wo": din("l3a_Wo", np.ascontiguousarray(I["L3_attn_Wo"])),
        "W1": din("l3a_W1", np.ascontiguousarray(I["L3_attn_W1"])),
        "W2": din("l3a_W2", np.ascontiguousarray(I["L3_attn_W2"])),
        "ln1w": din("l3a_ln1w", np.ascontiguousarray(I["L3_attn_ln1_w"].reshape(1, D))),
        "ln2w": din("l3a_ln2w", np.ascontiguousarray(I["L3_attn_ln2_w"].reshape(1, D))),
        "relb": din("l3a_relb", np.ascontiguousarray(I["L3_attn_rel_bias"])),
        "scale_h": din("l3a_scale", np.ascontiguousarray(I["L3_attn_scale"].reshape(1, 16))),
        "ET": din("l3a_ET", _build_ET()),
        "ranti": din("l3a_ranti", _RANTI),
        "ident": identh,
    }
    l3a["relb"] = l3a["relb"].ap()
    l3a["ET"] = l3a["ET"].ap()
    l3a["ranti"] = l3a["ranti"].ap()

    x_in = P.din("xin", [TOK, D])
    xo = P.dout("xof", [TOK, D])
    chain = [x_in] + [P.dtmp([TOK, D], name=f"chain{i}") for i in range(8)] + [xo]
    seq = [
        (emit_l1ffn, l1f), (emit_l1attn, l1a), (emit_l1ffn, l1f),
        (emit_l2ffn, l2f), (emit_l2attn, l2a), (emit_l2ffn, l2f),
        (emit_l3ffn, l3f), (emit_l3attn, l3a), (emit_l3ffn, l3f),
    ]
    for i, (fn, ins) in enumerate(seq):
        P.scope_open(f"s{i}_")
        IN = dict(ins)
        IN["ident"] = identh.ap()
        IN["x"] = chain[i].ap()
        IN["xo"] = chain[i + 1].ap()
        fn(P, IN)
        P.scope_close()
    nc = P.finalize()
    shards = _shard_tok(x)
    in_maps = [{**host, "xin": s} for s in shards]
    res = run_bass_kernel_spmd(nc, in_maps, core_ids=list(range(N_CORES)))
    return np.stack([r["xof"].reshape(S, D) for r in res.results])



def emit_l0ffn_fused(P, IN):
    R = TOK
    KD = D // 128
    NTl = NT
    CAP = 5120
    x = IN["x"]
    xo = IN["xo"]
    ident = IN["ident"]
    wqkvT, wprojT, routW = IN["wqkvT"], IN["wprojT"], IN["routW"]
    g1, b1, g2, b2 = IN["g1"], IN["b1"], IN["g2"], IN["b2"]
    tris, esel, mrow, offs = IN["tris"], IN["esel"], IN["mrow"], IN["offs"]
    w1, w2 = IN["w1"], IN["w2"]
    ccin, ccout = IN["ccin"], IN["ccout"]

    qkv_d = P.dtmp([R, 1536], name="qkvd")
    ao_d = P.dtmp([R, D], name="aod")
    h1_d = P.dtmp([R, D], name="h1d")
    meta_d = P.dtmp([4096, 12], name="metad")
    slot_d = P.dtmp([CAP + 128, 2], name="slotd")

    idsb = P.sb([128, 128], name="idsb")
    trisb = P.sb([128, 128], name="trisb")
    onesr = P.sb([1, 512], name="onesr")
    ones128c = P.sb([128, 1], name="ones128c")
    gbc = P.sb([128, 4, D], name="gbc")
    grow = P.sb([1, 4, D], name="grow")
    sq = P.sb([128, D], name="sq")
    stats = P.sb([128, 32], name="stats")
    ln1 = P.sb([128, D], name="ln1")
    lnT = P.sb([128, KD, 128], name="lnT")
    zrow = P.sb([128, D], name="zrow")
    mbc = P.sb([128, 8], name="mbc")
    offb = P.sb([128, 4], name="offb")
    iob = P.sb([128, 1], I32, name="iob")
    idxf = P.sb([128, 1], name="idxf")
    idxi = P.sb([128, 1], I32, name="idxi")
    eselbc = P.sb([128, 8], name="eselbc")
    eselr = P.sb([1, 8], name="eselr")
    runr = P.sb([1, 8], name="runr")
    psA = P.ps([128, 512], name="psA")
    psT = P.ps([128, 512], name="psT")
    psW = P.ps([128, 8], name="psW")
    psR = P.ps([1, 8], name="psR")

    cc512_in = dview(ccin, 0, [[512, 16384], [1, 512]])
    cc512_out = dview(ccout, 0, [[512, 16384], [1, 512]])

    P.dma(idsb[:, :], ident)
    P.dma(trisb[:, :], tris.ap())
    P.dma(eselr[:, :], esel.ap())
    P.dma(mbc[:, :], dview(mrow, 0, [[0, 128], [1, 8]]))
    P.dma(offb[:, :], dview(offs, 0, [[0, 128], [1, 4]]))
    P.op("vector", lambda e: e.memset(onesr[:, :], 1.0))
    P.op("vector", lambda e: e.memset(ones128c[:, :], 1.0))
    P.op("vector", lambda e: e.memset(runr[:, :], 0.0))
    P.op("vector", lambda e: e.memset(zrow[:, :], 0.0))
    P.op("tensor", lambda e: e.matmul(psW[:, :], onesr[0:1, 0:128], eselr[:, :], start=True, stop=True))
    P.op("vector", lambda e: e.tensor_copy(eselbc[:, :], psW[:, :]))
    P.op("gpsimd", lambda e: e.iota(out=iob[:, :], pattern=[[0, 1]], base=0, channel_multiplier=1))
    for i, gb in enumerate((g1, b1, g2, b2)):
        P.dma(grow[:, i, :], gb.ap())
    for i in range(4):
        for c in range(2):
            P.op("tensor", lambda e, _i=i, _c=c: e.matmul(psA[:, :], onesr[0:1, 0:128], grow[:, _i, 512 * _c:512 * (_c + 1)], start=True, stop=True))
            P.op("vector", lambda e, _i=i, _c=c: e.tensor_copy(gbc[:, _i, 512 * _c:512 * (_c + 1)], psA[:, :]))

    def mk_idx(t, offcol):
        P.op("vector", lambda e, _t=t, _o=offcol: e.tensor_scalar(out=idxf[:, :], in0=iob[:, :], scalar1=float(128 * _t), scalar2=offb[:, _o:_o + 1], op0=ALU.add, op1=ALU.add))
        P.op("vector", lambda e: e.tensor_copy(idxi[:, :], idxf[:, :]))

    P.settag("zA")
    P.sub_open("pA_")
    xnl = P.sb([128, NTl, D], name="xnl")
    wsb = P.sb([128, KD, 512], name="wsb")
    qkv = P.sb([128, 1536], name="qkv")
    kvp = P.sb([128, 512], name="kvp")
    for t in range(64):
        P.dma(ccin.ap()[128 * t:128 * (t + 1), :], zrow[:, :], engine="gpsimd")
    for t in range(NTl):
        P.dma(ln1[:, :], x[128 * t:128 * (t + 1), :])
        emit_layernorm(P, ln1[:, :], xnl[:, t, :], sq, stats)
        emit_layernorm(P, xnl[:, t, :], ln1[:, :], sq, stats)
        P.op("vector", lambda e: e.tensor_tensor(out=ln1[:, :], in0=ln1[:, :], in1=gbc[:, 0, :], op=ALU.mult))
        P.op("vector", lambda e: e.tensor_tensor(out=ln1[:, :], in0=ln1[:, :], in1=gbc[:, 1, :], op=ALU.add))
        for g in range(2):
            emit_transpose_batch(P, idsb, [ln1[:, 128 * (4 * g + i):128 * (4 * g + i + 1)] for i in range(4)], psT,
                                 dst_ap_fn=lambda i, _g=g: lnT[:, 4 * _g + i, :])
        for oc in range(3):
            P.dma(wsb[:, :, :], wqkvT.ap()[:, 512 * oc:512 * (oc + 1)].rearrange("(c p) m -> p c m", p=128))
            emit_mm_group(P, psA[:, :], [lnT[:, c, :] for c in range(KD)], [wsb[:, c, :] for c in range(KD)])
            P.op("vector", lambda e, _oc=oc: e.tensor_copy(qkv[:, 512 * _oc:512 * (_oc + 1)], psA[:, :]))
        P.dma(qkv_d.ap()[128 * t:128 * (t + 1), :], qkv[:, :], engine="gpsimd")
        P.op("vector", lambda e: e.tensor_copy(kvp.rearrange("p (g d) -> p g d", d=64)[:, 0:4, :], qkv.rearrange("p (g w) -> p g w", w=384)[:, :, 256:320]))
        P.op("vector", lambda e: e.tensor_copy(kvp.rearrange("p (g d) -> p g d", d=64)[:, 4:8, :], qkv.rearrange("p (g w) -> p g w", w=384)[:, :, 320:384]))
        mk_idx(t, 0)
        P.op("gpsimd", lambda e: e.indirect_dma_start(
            out=cc512_in, out_offset=bass.IndirectOffsetOnAxis(ap=idxi[0:128, 0:1], axis=0),
            in_=kvp[:, :], in_offset=None), weight=16)
    P.settag("zCC1")
    P.op("gpsimd", lambda e: e.collective_compute(
        "AllReduce", ALU.add, replica_groups=[list(range(8))],
        ins=[ccin.ap()[0:8192, :].opt()], outs=[ccout.ap()[0:8192, :].opt()]), weight=1, kind="dma")
    P.settag("zB")

    qblk = P.sb([128, 4, 256], name="qblk")
    kblk = P.sb([128, 8, 64], name="kblk")
    vblk = P.sb([128, 8, 64], name="vblk")
    scb = P.sb([128, 8, 64], name="scb")
    sc8 = P.sb([128, 8], name="sc8")
    aot = P.sb([128, D], name="aot")
    for t in range(NTl):
        P.dma(qblk[:, :, :], dview(qkv_d, 128 * t * 1536, [[1536, 128], [384, 4], [1, 256]]))
        for g4 in range(4):
            P.dma(kblk[:, :, :], dview(cc512_out, 128 * t * 512 + 64 * g4, [[512, 128], [1048576, 8], [1, 64]]))
            P.dma(vblk[:, :, :], dview(cc512_out, 128 * t * 512 + 256 + 64 * g4, [[512, 128], [1048576, 8], [1, 64]]))
            for hq in range(4):
                qv = qblk[:, g4, 64 * hq:64 * (hq + 1)]
                P.op("vector", lambda e, _q=qv: e.tensor_tensor(
                    out=scb[:, :, :], in0=_q.rearrange("p (o d) -> p o d", o=1).broadcast_to([128, 8, 64]),
                    in1=kblk[:, :, :], op=ALU.mult))
                P.op("vector", lambda e: e.tensor_reduce(out=sc8[:, :], in_=scb[:, :, :], axis=AX.X, op=ALU.add))
                P.op("vector", lambda e: e.tensor_scalar(out=sc8[:, :], in0=sc8[:, :], scalar1=0.125, scalar2=None, op0=ALU.mult))
                P.op("vector", lambda e: e.tensor_tensor(out=sc8[:, :], in0=sc8[:, :], in1=mbc[:, :], op=ALU.add))
                P.op("vector", lambda e: e.tensor_reduce(out=stats[:, 12:13], in_=sc8[:, :], axis=AX.X, op=ALU.max))
                P.op("vector", lambda e: e.tensor_scalar(out=stats[:, 13:14], in0=stats[:, 12:13], scalar1=-1.0, scalar2=None, op0=ALU.mult))
                P.op("scalar", lambda e: e.activation(out=sc8[:, :], in_=sc8[:, :], func=AF.Exp, bias=stats[:, 13:14], accum_out=stats[:, 14:15]))
                P.op("vector", lambda e: e.reciprocal(out=stats[:, 14:15], in_=stats[:, 14:15]))
                P.op("vector", lambda e: e.tensor_scalar(out=sc8[:, :], in0=sc8[:, :], scalar1=stats[:, 14:15], scalar2=None, op0=ALU.mult))
                P.op("vector", lambda e: e.tensor_tensor(
                    out=scb[:, :, :], in0=sc8.rearrange("p (s o) -> p s o", o=1).broadcast_to([128, 8, 64]),
                    in1=vblk[:, :, :], op=ALU.mult))
                P.op("vector", lambda e, _g=g4, _h=hq: e.tensor_reduce(
                    out=aot[:, 256 * _g + 64 * _h:256 * _g + 64 * _h + 64],
                    in_=scb[:, :, :].transpose([0, 2, 1]), axis=AX.X, op=ALU.add))
        P.dma(ao_d.ap()[128 * t:128 * (t + 1), :], aot[:, :], engine="gpsimd")

    P.settag("zC")
    for t in range(131):
        P.dma(ccin.ap()[128 * t:128 * (t + 1), :], zrow[:, :], engine="gpsimd")
    rw = P.sb([128, KD, 8], name="rw")
    lg = P.sb([128, 8], name="lg")
    P.dma(rw[:, :, :], routW.ap().rearrange("(c p) m -> p c m", p=128))
    for t in range(NTl):
        P.dma(aot[:, :], ao_d.ap()[128 * t:128 * (t + 1), :])
        for g in range(2):
            emit_transpose_batch(P, idsb, [aot[:, 128 * (4 * g + i):128 * (4 * g + i + 1)] for i in range(4)], psT,
                                 dst_ap_fn=lambda i, _g=g: lnT[:, 4 * _g + i, :])
        for oc in range(2):
            P.dma(wsb[:, :, :], wprojT.ap()[:, 512 * oc:512 * (oc + 1)].rearrange("(c p) m -> p c m", p=128))
            emit_mm_group(P, psA[:, :], [lnT[:, c, :] for c in range(KD)], [wsb[:, c, :] for c in range(KD)])
            P.op("vector", lambda e, _t=t, _oc=oc: e.tensor_tensor(out=xnl[:, _t, 512 * _oc:512 * (_oc + 1)], in0=xnl[:, _t, 512 * _oc:512 * (_oc + 1)], in1=psA[:, :], op=ALU.add))
        P.dma(h1_d.ap()[128 * t:128 * (t + 1), :], xnl[:, t, :], engine="gpsimd")
        emit_layernorm(P, xnl[:, t, :], ln1[:, :], sq, stats)
        P.op("vector", lambda e: e.tensor_tensor(out=ln1[:, :], in0=ln1[:, :], in1=gbc[:, 2, :], op=ALU.mult))
        P.op("vector", lambda e: e.tensor_tensor(out=ln1[:, :], in0=ln1[:, :], in1=gbc[:, 3, :], op=ALU.add))
        mk_idx(t, 1)
        P.op("gpsimd", lambda e: e.indirect_dma_start(
            out=ccin.ap()[:, :], out_offset=bass.IndirectOffsetOnAxis(ap=idxi[0:128, 0:1], axis=0),
            in_=ln1[:, :], in_offset=None), weight=16)
        for g in range(2):
            emit_transpose_batch(P, idsb, [ln1[:, 128 * (4 * g + i):128 * (4 * g + i + 1)] for i in range(4)], psT,
                                 dst_ap_fn=lambda i, _g=g: lnT[:, 4 * _g + i, :])
        emit_mm_group(P, psA[:, 0:8], [lnT[:, c, :] for c in range(KD)], [rw[:, c, :] for c in range(KD)])
        P.op("vector", lambda e: e.tensor_copy(lg[:, :], psA[:, 0:8]))
        P.op("vector", lambda e: e.tensor_reduce(out=stats[:, 16:17], in_=lg[:, :], axis=AX.X, op=ALU.max))
        P.op("vector", lambda e: e.tensor_scalar(out=sq[:, 0:8], in0=lg[:, :], scalar1=stats[:, 16:17], scalar2=None, op0=ALU.is_equal))
        P.op("vector", lambda e: e.scalar_tensor_tensor(out=sq[:, 8:16], in0=sq[:, 0:8], scalar=-1e30, in1=lg[:, :], op0=ALU.mult, op1=ALU.add))
        P.op("vector", lambda e: e.tensor_reduce(out=stats[:, 17:18], in_=sq[:, 8:16], axis=AX.X, op=ALU.max))
        P.op("vector", lambda e: e.tensor_scalar(out=sq[:, 16:24], in0=sq[:, 8:16], scalar1=stats[:, 17:18], scalar2=None, op0=ALU.is_equal))
        P.op("vector", lambda e: e.tensor_tensor(out=stats[:, 18:19], in0=stats[:, 16:17], in1=stats[:, 17:18], op=ALU.subtract))
        P.op("scalar", lambda e: e.activation(out=stats[:, 20:21], in_=stats[:, 18:19], func=AF.Sigmoid))
        P.op("vector", lambda e: e.tensor_scalar(out=stats[:, 19:20], in0=stats[:, 18:19], scalar1=-1.0, scalar2=None, op0=ALU.mult))
        P.op("scalar", lambda e: e.activation(out=stats[:, 21:22], in_=stats[:, 19:20], func=AF.Sigmoid))
        mk_idx(t, 3)
        P.dma(dview(meta_d, 128 * t * 24 + 0, [[24, 128], [1, 8]]), sq[:, 0:8], engine="gpsimd")
        P.dma(dview(meta_d, 128 * t * 24 + 12, [[24, 128], [1, 8]]), sq[:, 16:24], engine="gpsimd")
        P.dma(dview(meta_d, 128 * t * 24 + 8, [[24, 128], [1, 1]]), stats[:, 20:21], engine="gpsimd", allow_slow=True)
        P.dma(dview(meta_d, 128 * t * 24 + 20, [[24, 128], [1, 1]]), stats[:, 21:22], engine="gpsimd", allow_slow=True)
        P.dma(dview(meta_d, 128 * t * 24 + 9, [[24, 128], [1, 1]]), idxf[:, :], engine="gpsimd", allow_slow=True)
        P.dma(dview(meta_d, 128 * t * 24 + 21, [[24, 128], [1, 1]]), idxf[:, :], engine="gpsimd", allow_slow=True)
    P.dma(sq[0:48, :], dview(meta_d, 0, [[1024, 48], [1, 1024]]))
    P.op("vector", lambda e: e.tensor_scalar(out=idxf[:, :], in0=iob[:, :], scalar1=offb[:, 2:3], scalar2=None, op0=ALU.add))
    P.op("vector", lambda e: e.tensor_copy(idxi[:, :], idxf[:, :]))
    P.op("gpsimd", lambda e: e.indirect_dma_start(
        out=ccin.ap()[:, :], out_offset=bass.IndirectOffsetOnAxis(ap=idxi[0:48, 0:1], axis=0),
        in_=sq[0:48, :], in_offset=None), weight=16)
    P.sub_close()
    P.settag("zCC2")
    P.op("gpsimd", lambda e: e.collective_compute(
        "AllReduce", ALU.add, replica_groups=[list(range(8))],
        ins=[ccin.ap()[0:16768, :].opt()], outs=[ccout.ap()[0:16768, :].opt()]), weight=1, kind="dma")
    P.settag("zD")

    P.sub_open("pB_")
    metl = P.sb([128, 16, 12], name="metl")
    assl = metl[:, :, 0:8]
    ohl16 = P.sb([128, 16], name="ohl16")
    tmp8 = P.sb([128, 8], name="tmp8")
    wtile = P.sb([128, 16], name="wtile")
    kvt = P.sb([128, 16], name="kvt")
    sc2 = P.sb([128, 2], name="sc2")
    sci = P.sb([128, 1], I32, name="sci")
    sci16 = P.sb([128, 16], I32, name="sci16")
    pcol = P.sb([128, 1], name="pcol")
    w1sb = P.sb([128, KD, 2048], name="w1sb")
    w2sb = P.sb([128, 16, 1024], name="w2sb")
    gath = P.sb([128, D], name="gath")
    gT = P.sb([128, 8, 128], name="gT")
    hfm = P.sb([128, 16, 128], name="hfm")
    outr = P.sb([128, D], name="outr")
    for t in range(128):
        P.dma(ccin.ap()[128 * t:128 * (t + 1), :], zrow[:, :], engine="gpsimd")
    for b0 in range(0, CAP + 128, 128):
        n = min(128, CAP + 128 - b0)
        P.dma(slot_d.ap()[b0:b0 + n, :], zrow[0:n, 0:2], engine="gpsimd")
    P.op("gpsimd", lambda e: e.iota(out=sci[:, :], pattern=[[0, 1]], base=CAP, channel_multiplier=1))
    P.op("vector", lambda e: e.tensor_copy(pcol[:, :], sci[:, :]))
    P.op("vector", lambda e: e.memset(runr[:, :], 0.0))
    MBASE = 16384 * 1024
    for big in range(16):
        P.dma(metl[:, :, :], dview(ccout, MBASE + 2048 * big * 12, [[12, 128], [1536, 16], [1, 12]]))
        for j in range(16):
            def pf(e, _j=j):
                e.matmul(psW[:, :], trisb[:, :], assl[:, _j, :], start=True, stop=False)
                return e.matmul(psW[:, :], onesr[0:1, 0:128], runr[:, :], start=False, stop=True)
            P.op("tensor", pf)
            P.op("vector", lambda e: e.tensor_tensor(out=tmp8[:, :], in0=psW[:, :], in1=eselbc[:, :], op=ALU.mult))
            P.op("vector", lambda e, _j=j: e.tensor_reduce(out=wtile[:, _j:_j + 1], in_=tmp8[:, :], axis=AX.X, op=ALU.add))
            P.op("tensor", lambda e, _j=j: e.matmul(psR[:, :], ones128c[:, :], assl[:, _j, :], start=True, stop=True))
            P.op("vector", lambda e: e.tensor_tensor(out=runr[:, :], in0=runr[:, :], in1=psR[:, :], op=ALU.add))
            P.op("vector", lambda e, _j=j: e.tensor_tensor(out=tmp8[:, :], in0=assl[:, _j, :], in1=eselbc[:, :], op=ALU.mult))
            P.op("vector", lambda e, _j=j: e.tensor_reduce(out=ohl16[:, _j:_j + 1], in_=tmp8[:, :], axis=AX.X, op=ALU.add))
        P.op("vector", lambda e: e.tensor_scalar(out=kvt[:, :], in0=wtile[:, :], scalar1=float(CAP), scalar2=None, op0=ALU.is_lt))
        P.op("vector", lambda e: e.tensor_tensor(out=kvt[:, :], in0=kvt[:, :], in1=ohl16[:, :], op=ALU.mult))
        P.op("vector", lambda e: e.scalar_tensor_tensor(out=wtile[:, :], in0=wtile[:, :], scalar=pcol[:, :], in1=kvt[:, :], op0=ALU.subtract, op1=ALU.mult))
        P.op("vector", lambda e: e.tensor_scalar(out=wtile[:, :], in0=wtile[:, :], scalar1=pcol[:, :], scalar2=None, op0=ALU.add))
        P.op("vector", lambda e: e.tensor_copy(sci16[:, :], wtile[:, :]))
        for j in range(16):
            P.op("gpsimd", lambda e, _j=j: e.indirect_dma_start(
                out=slot_d.ap()[:, :], out_offset=bass.IndirectOffsetOnAxis(ap=sci16[0:128, _j:_j + 1], axis=0),
                in_=metl[:, _j, 8:10], in_offset=None), weight=16)
    P.settag("zE")
    P.dma(w1sb[:, :, :], w1.ap().rearrange("(c p) m -> p c m", p=128))
    P.dma(w2sb[:, :, :], w2.ap().rearrange("(c p) m -> p c m", p=128))
    GC = 0.7978845608028654
    gsc4 = P.sb([128, 3, 512], name="gsc4")
    for ch in range(CAP // 128):
        P.dma(sc2[:, :], slot_d.ap()[128 * ch:128 * (ch + 1), :])
        P.op("vector", lambda e: e.tensor_copy(sci[:, :], sc2[:, 1:2]))
        P.op("gpsimd", lambda e: e.indirect_dma_start(
            out=gath[:, :], out_offset=None, in_=ccout.ap()[:, :],
            in_offset=bass.IndirectOffsetOnAxis(ap=sci[0:128, 0:1], axis=0)), weight=16)
        for g in range(2):
            emit_transpose_batch(P, idsb, [gath[:, 128 * (4 * g + i):128 * (4 * g + i + 1)] for i in range(4)], psT,
                                 evac_batch=gT[:, 4 * g:4 * g + 4, :].rearrange("p a b -> p (a b)"))
        for f4 in range(4):
            def mm4(e, _f4=f4):
                inst = None
                for s4 in range(4):
                    fc = 4 * _f4 + s4
                    for c in range(KD):
                        inst = e.matmul(psA[:, 128 * s4:128 * (s4 + 1)], w1sb[:, c, 128 * fc:128 * (fc + 1)], gT[:, c, :],
                                        start=(c == 0), stop=(c == KD - 1))
                return inst
            P.op("tensor", mm4)
            P.op("scalar", lambda e: e.activation(out=gsc4[:, 0, :], in_=psA[:, :], func=AF.Square))
            P.op("vector", lambda e: e.tensor_scalar(out=gsc4[:, 0, :], in0=gsc4[:, 0, :], scalar1=0.044715, scalar2=1.0, op0=ALU.mult, op1=ALU.add))
            P.op("vector", lambda e: e.tensor_tensor(out=gsc4[:, 0, :], in0=gsc4[:, 0, :], in1=psA[:, :], op=ALU.mult))
            P.op("scalar", lambda e: e.activation(out=gsc4[:, 1, :], in_=gsc4[:, 0, :], func=AF.Tanh, scale=GC))
            P.op("vector", lambda e: e.tensor_scalar(out=gsc4[:, 2, :], in0=psA[:, :], scalar1=0.5, scalar2=None, op0=ALU.mult))
            P.op("vector", lambda e, _f4=f4: e.scalar_tensor_tensor(out=hfm[:, 4 * _f4:4 * (_f4 + 1), :].rearrange("p a b -> p (a b)"), in0=gsc4[:, 1, :], scalar=1.0, in1=gsc4[:, 2, :], op0=ALU.add, op1=ALU.mult))
        for oh in range(2):
            emit_mm_group(P, psA[:, :], [hfm[:, c, :] for c in range(16)], [w2sb[:, c, 512 * oh:512 * (oh + 1)] for c in range(16)])
            P.op("vector", lambda e, _oh=oh: e.tensor_scalar(out=outr[:, 512 * _oh:512 * (_oh + 1)], in0=psA[:, :], scalar1=sc2[:, 0:1], scalar2=None, op0=ALU.mult))
        P.op("gpsimd", lambda e: e.indirect_dma_start(
            out=ccin.ap()[:, :], out_offset=bass.IndirectOffsetOnAxis(ap=sci[0:128, 0:1], axis=0),
            in_=outr[:, :], in_offset=None, compute_op=ALU.add), weight=16)
    P.sub_close()
    P.settag("zCC3")
    P.op("gpsimd", lambda e: e.collective_compute(
        "AllReduce", ALU.add, replica_groups=[list(range(8))],
        ins=[ccin.ap()[0:16384, :].opt()], outs=[ccout.ap()[0:16384, :].opt()]), weight=1, kind="dma")
    P.settag("zF")

    P.sub_open("pC_")
    a1 = P.sb([128, D], name="a1")
    b1s = P.sb([128, D], name="b1s")
    for t in range(NTl):
        mk_idx(t, 3)
        P.op("gpsimd", lambda e: e.indirect_dma_start(
            out=a1[:, :], out_offset=None, in_=ccout.ap()[:, :],
            in_offset=bass.IndirectOffsetOnAxis(ap=idxi[0:128, 0:1], axis=0)), weight=16)
        P.dma(b1s[:, :], h1_d.ap()[128 * t:128 * (t + 1), :])
        P.op("vector", lambda e: e.tensor_tensor(out=a1[:, :], in0=a1[:, :], in1=b1s[:, :], op=ALU.add))
        P.dma(b1s[:, :], x[128 * t:128 * (t + 1), :])
        P.op("vector", lambda e: e.scalar_tensor_tensor(out=a1[:, :], in0=a1[:, :], scalar=0.5, in1=b1s[:, :], op0=ALU.mult, op1=ALU.add))
        P.dma(xo[128 * t:128 * (t + 1), :], a1[:, :], engine="gpsimd")
    P.sub_close()


def run_all_fused(x, I, only=None):
    P = Prog()
    host = {}

    def din(name, arr):
        a0 = arr[0] if isinstance(arr, list) else arr
        h = P.din(name, list(a0.shape), _npdt(a0))
        host[name] = arr
        return h

    identh = din("c_ident", _IDENT)
    m8c, trisc = _l0ffn_consts()
    offs_list = []
    esel_list = []
    mrow_list = []
    for c in range(N_CORES):
        es = np.zeros((1, 8), np.float32); es[0, c] = 1.0
        esel_list.append(es)
        mr = np.where(np.arange(8) <= c, 0.0, -1e30).astype(np.float32).reshape(1, 8)
        mrow_list.append(mr)
        offs_list.append(np.array([[2048. * c, 2048. * c, 16384. + 48. * c, 2048. * c]], np.float32))
    l0f = {
        "wqkvT": din("l0f_wqkvT", np.ascontiguousarray(I["L0_ffn_qkv_weight"].T)),
        "wprojT": din("l0f_wprojT", np.ascontiguousarray(I["L0_ffn_proj_weight"].T)),
        "routW": din("l0f_routW", np.ascontiguousarray(I["L0_ffn_router_weight"])),
        "g1": din("l0f_g1", I["L0_ffn_ln1_weight"].reshape(1, D)),
        "b1": din("l0f_b1", I["L0_ffn_ln1_bias"].reshape(1, D)),
        "g2": din("l0f_g2", I["L0_ffn_ln2_weight"].reshape(1, D)),
        "b2": din("l0f_b2", I["L0_ffn_ln2_bias"].reshape(1, D)),
        "tris": din("l0f_tris", trisc),
        "esel": din("l0f_esel", esel_list),
        "mrow": din("l0f_mrow", mrow_list),
        "offs": din("l0f_offs", offs_list),
        "w1": din("l0f_w1", [np.ascontiguousarray(I["L0_ffn_moe_w1"][c]) for c in range(N_CORES)]),
        "w2": din("l0f_w2", [np.ascontiguousarray(I["L0_ffn_moe_w2"][c]) for c in range(N_CORES)]),
        "ident": identh,
    }
    Mdk, Msk = _l0_masks()
    l0a = {
        "Wq": din("l0a_Wq", np.ascontiguousarray(I["L0_attn_Wq"])),
        "Wk": din("l0a_Wk", np.ascontiguousarray(I["L0_attn_Wk"])),
        "Wv": din("l0a_Wv", np.ascontiguousarray(I["L0_attn_Wv"])),
        "bq": din("l0a_bq", I["L0_attn_bq"].reshape(1, D)),
        "bk": din("l0a_bk", I["L0_attn_bk"].reshape(1, D)),
        "bv": din("l0a_bv", I["L0_attn_bv"].reshape(1, D)),
        "mdiag": din("l0a_mdiag", Mdk),
        "msib": din("l0a_msib", Msk),
        "ident": identh,
    }
    for k in ("bq", "bk", "bv", "mdiag", "msib"):
        l0a[k] = l0a[k].ap()
    l1f = {
        "k0": din("l1f_k0", np.ascontiguousarray(I["L1_ffn_k0"]).view(np.uint8)),
        "k1": din("l1f_k1", np.ascontiguousarray(I["L1_ffn_k1"]).view(np.uint8)),
        "k2": din("l1f_k2", np.ascontiguousarray(I["L1_ffn_k2"]).view(np.uint8)),
        "s0": din("l1f_s0", np.float32(I["L1_ffn_s0"]).reshape(1, 1)),
        "s1": din("l1f_s1", np.float32(I["L1_ffn_s1"]).reshape(1, 1)),
        "s2": din("l1f_s2", np.float32(I["L1_ffn_s2"]).reshape(1, 1)),
        "ident": identh,
    }
    cw = np.concatenate([I["L1_attn_conv_w"].reshape(2048, 4), I["L1_attn_conv_b"].reshape(2048, 1)], axis=1).astype(np.float32)
    l1a = {
        "WinT": din("l1a_WinT", np.ascontiguousarray(I["L1_attn_W_in"].T)),
        "convw": din("l1a_convw", cw),
        "WxT": din("l1a_WxT", np.ascontiguousarray(I["L1_attn_W_x"].T)),
        "WdtT": din("l1a_WdtT", np.ascontiguousarray(I["L1_attn_W_dt"].T)),
        "bdt": din("l1a_bdt", np.ascontiguousarray(I["L1_attn_b_dt"].reshape(2048, 1))),
        "Alog": din("l1a_Alog", np.ascontiguousarray(I["L1_attn_A_log"])),
        "Dp": din("l1a_Dp", np.ascontiguousarray(I["L1_attn_Dp"].reshape(2048, 1))),
        "WoutT": din("l1a_WoutT", np.ascontiguousarray(I["L1_attn_W_out"].T)),
        "normw": din("l1a_normw", np.ascontiguousarray(I["L1_attn_norm_w"].reshape(1, D))),
        "ident": identh,
    }
    l2f = {
        "wg": din("l2f_wg", np.ascontiguousarray(I["L2_ffn_w_gate"])),
        "wu": din("l2f_wu", np.ascontiguousarray(I["L2_ffn_w_up"])),
        "wd": din("l2f_wd", np.ascontiguousarray(I["L2_ffn_w_down"])),
        "ident": identh,
    }
    G1, G2, E3 = _l2_consts()
    l2a = {
        "wqkvT": din("l2a_wqkvT", np.ascontiguousarray(I["L2_attn_w_qkv"].T)),
        "wpT": din("l2a_wpT", np.ascontiguousarray(I["L2_attn_w_proj"].T)),
        "bproj": din("l2a_bproj", np.ascontiguousarray(I["L2_attn_b_proj"].reshape(1, D))),
        "rkeT": din("l2a_rkeT", np.ascontiguousarray(np.swapaxes(I["L2_attn_rpe_k_table"], 1, 2))),
        "rqeT": din("l2a_rqeT", np.ascontiguousarray(np.swapaxes(I["L2_attn_rpe_q_table"], 1, 2))),
        "rk02": din("l2a_rk02", np.ascontiguousarray(I["L2_attn_rpe_k_table"][:, :, [0, 62]])),
        "rq02": din("l2a_rq02", np.ascontiguousarray(I["L2_attn_rpe_q_table"][:, :, [62, 0]])),
        "rv0in": din("l2a_rv0in", np.ascontiguousarray(I["L2_attn_rpe_v_table"][:, [0, 62], :])),
        "rvmid": din("l2a_rvmid", np.ascontiguousarray(I["L2_attn_rpe_v_table"])),
        "G1": din("l2a_G1", G1), "G2": din("l2a_G2", G2), "EXP3": din("l2a_E3", E3),
        "ident": identh,
    }
    l3f = {
        "wgT": din("l3f_wgT", np.ascontiguousarray(I["L3_ffn_w_gate"].T)),
        "wuT": din("l3f_wuT", np.ascontiguousarray(I["L3_ffn_w_up"].T)),
        "wdT": din("l3f_wdT", np.ascontiguousarray(I["L3_ffn_w_down"].T)),
        "ident": identh,
    }
    l3a = {
        "Wq": din("l3a_Wq", np.ascontiguousarray(I["L3_attn_Wq"])),
        "Wk": din("l3a_Wk", np.ascontiguousarray(I["L3_attn_Wk"])),
        "Wv": din("l3a_Wv", np.ascontiguousarray(I["L3_attn_Wv"])),
        "Wo": din("l3a_Wo", np.ascontiguousarray(I["L3_attn_Wo"])),
        "W1": din("l3a_W1", np.ascontiguousarray(I["L3_attn_W1"])),
        "W2": din("l3a_W2", np.ascontiguousarray(I["L3_attn_W2"])),
        "ln1w": din("l3a_ln1w", np.ascontiguousarray(I["L3_attn_ln1_w"].reshape(1, D))),
        "ln2w": din("l3a_ln2w", np.ascontiguousarray(I["L3_attn_ln2_w"].reshape(1, D))),
        "relb": din("l3a_relb", np.ascontiguousarray(I["L3_attn_rel_bias"])),
        "scale_h": din("l3a_scale", np.ascontiguousarray(I["L3_attn_scale"].reshape(1, 16))),
        "ET": din("l3a_ET", _build_ET()),
        "ranti": din("l3a_ranti", _RANTI),
        "ident": identh,
    }
    l3a["relb"] = l3a["relb"].ap()
    l3a["ET"] = l3a["ET"].ap()
    l3a["ranti"] = l3a["ranti"].ap()

    ccin = P.dtmp([16768, 1024], name="ccin")
    ccout = P.dtmp([16768, 1024], name="ccout")
    l0f["ccin"] = ccin
    l0f["ccout"] = ccout

    x_in = P.din("xin", [TOK, D])
    xo = P.dout("xof", [TOK, D])
    seq = [
        (emit_l0ffn_fused, l0f), (emit_l0attn, l0a), (emit_l0ffn_fused, l0f),
        (emit_l1ffn, l1f), (emit_l1attn, l1a), (emit_l1ffn, l1f),
        (emit_l2ffn, l2f), (emit_l2attn, l2a), (emit_l2ffn, l2f),
        (emit_l3ffn, l3f), (emit_l3attn, l3a), (emit_l3ffn, l3f),
    ]
    if only is not None:
        seq = seq[only[0]:only[1]]
    qwg_d = P.dtmp([D, 2816], name="l2qwg")
    qwu_d = P.dtmp([D, 2816], name="l2qwu")
    qwd_d = P.dtmp([2816, D], name="l2qwd")
    chain = [x_in] + [P.dtmp([TOK, D], name=f"chain{i}") for i in range(len(seq) - 1)] + [xo]
    n_l2f = 0
    for i, (fn, ins) in enumerate(seq):
        P.scope_open(f"s{i}_")
        IN = dict(ins)
        IN["ident"] = identh.ap()
        IN["x"] = chain[i].ap()
        IN["xo"] = chain[i + 1].ap()
        if fn is emit_l2ffn:
            IN["qwgT"], IN["qwuT"], IN["qwdT"] = qwg_d, qwu_d, qwd_d
            IN["quant_done"] = n_l2f > 0
            n_l2f += 1
        fn(P, IN)
        P.scope_close()
    nc = P.finalize()
    shards = _shard_tok(x)
    in_maps = []
    for c in range(N_CORES):
        m = {}
        for k, v in host.items():
            m[k] = v[c] if isinstance(v, list) else v
        m["xin"] = shards[c]
        in_maps.append(m)
    res = run_bass_kernel_spmd(nc, in_maps, core_ids=list(range(N_CORES)))
    return np.stack([r["xof"].reshape(S, D) for r in res.results])


def kernel(**inputs):
    I = inputs
    x = np.ascontiguousarray(I["L0_attn_hidden_states"], dtype=np.float32)
    x = run_all_fused(x, I)
    return np.ascontiguousarray(x.astype(np.float32))
